# version 08
# baseline (speedup 1.0000x reference)
from contextlib import ExitStack

import numpy as np
import concourse.bass as bass
import concourse.mybir as mybir
from concourse.bass_utils import run_bass_kernel_spmd

F32 = mybir.dt.float32
BF16 = mybir.dt.bfloat16
U8 = mybir.dt.uint8
I32 = mybir.dt.int32
AF = mybir.ActivationFunctionType
ALU = mybir.AluOpType
AX = mybir.AxisListType

NCORES = 8
SEQ = 8192
D = 2048
NH = 16
HD = 128
TPC = 1024
HALO = 64
TW = TPC + 2 * HALO
NB = TPC // 128
KC = D // 128
II = 8192
ICN = II // 128
INV_SQRT_D = float(1.0 / np.sqrt(128.0))
BIG = 1.0e30


class Sem:
    def __init__(self, nc, name):
        self.h = nc.alloc_semaphore(name=name)
        self.n = 0

    def inc(self, instr, k=1):
        instr.then_inc(self.h, k)
        self.n += k
        return self.n

    def wait(self, eng, val=None):
        eng.wait_ge(self.h, self.n if val is None else val)


def build_nc():
    nc = bass.Bass("TRN2", target_bir_lowering=False)

    x_main = nc.dram_tensor("x_main", [TPC, D], F32, kind="ExternalInput").ap()
    x_hl = nc.dram_tensor("x_hl", [HALO, D], F32, kind="ExternalInput").ap()
    x_hr = nc.dram_tensor("x_hr", [HALO, D], F32, kind="ExternalInput").ap()
    wq = nc.dram_tensor("wq", [KC, 2, 128, 1024], F32, kind="ExternalInput").ap()
    wk = nc.dram_tensor("wk", [KC, 2, 128, 1024], F32, kind="ExternalInput").ap()
    wv = nc.dram_tensor("wv", [KC, 2, 128, 1024], F32, kind="ExternalInput").ap()
    bq = nc.dram_tensor("bq", [128, KC], F32, kind="ExternalInput").ap()
    bk = nc.dram_tensor("bk", [128, KC], F32, kind="ExternalInput").ap()
    bv = nc.dram_tensor("bv", [128, KC], F32, kind="ExternalInput").ap()
    g_u8 = nc.dram_tensor("g_u8", [ICN, 128, D], U8, kind="ExternalInput").ap()
    u_u8 = nc.dram_tensor("u_u8", [ICN, 128, D], U8, kind="ExternalInput").ap()
    d_u8 = nc.dram_tensor("d_u8", [ICN, 4, 128, 512], U8, kind="ExternalInput").ap()
    gsc = nc.dram_tensor("gsc", [128, ICN], F32, kind="ExternalInput").ap()
    usc = nc.dram_tensor("usc", [128, ICN], F32, kind="ExternalInput").ap()
    dscb = nc.dram_tensor("dscb", [128, D], F32, kind="ExternalInput").ap()
    mask_l = nc.dram_tensor("mask_l", [128, 1], F32, kind="ExternalInput").ap()
    mask_r = nc.dram_tensor("mask_r", [128, 1], F32, kind="ExternalInput").ap()
    base16 = nc.dram_tensor("base16", [16, 1], F32, kind="ExternalInput").ap()

    y = nc.dram_tensor("y", [TPC, D], F32, kind="ExternalOutput").ap()

    xres_dram = nc.dram_tensor("xres_dram", [TPC, D], F32).ap()
    scr_c = nc.dram_tensor("scr_c", [128, 64], F32)
    scr_i = nc.dram_tensor("scr_i", [1, NH * 64], F32)
    scr_n = nc.dram_tensor("scr_n", [16, 8 * TPC], F32)
    cc_nrm_in = nc.dram_tensor("cc_nrm_in", [16, TPC], F32)
    cc_nrm_out = nc.dram_tensor("cc_nrm_out", [128, TPC], F32)
    cc_qg_in = nc.dram_tensor("cc_qg_in", [128, NH * 64], BF16)
    cc_qg_out = nc.dram_tensor("cc_qg_out", [128, NH * 64], BF16)
    cc_m_in = nc.dram_tensor("cc_m_in", [64, NH], F32)
    cc_m_out = nc.dram_tensor("cc_m_out", [64, NH], F32)
    cc_ol_in = nc.dram_tensor("cc_ol_in", [64, NH * HD + NH], F32)
    cc_ol_out = nc.dram_tensor("cc_ol_out", [64, NH * HD + NH], F32)
    RG = [list(range(NCORES))]

    nc.gpsimd.dma_reset(range(155, 256))
    nc.gpsimd.sem_clear(range(155, 256))
    nc.all_engine_barrier()

    sv = Sem(nc, "sv")
    ss = Sem(nc, "ss")
    st = Sem(nc, "st")
    sd = Sem(nc, "sd")
    sq = Sem(nc, "sq")
    sg = Sem(nc, "sg")
    scc = Sem(nc, "scc")
    s_w = [Sem(nc, "sw0"), Sem(nc, "sw1")]
    s_d = [Sem(nc, "sd0"), Sem(nc, "sd1")]
    so_sl = [Sem(nc, "so0"), Sem(nc, "so1")]

    V, S, T, G, SY = nc.vector, nc.scalar, nc.tensor, nc.gpsimd, nc.sync

    esK = ExitStack()

    def sbK(name, shape, dt):
        return esK.enter_context(nc.sbuf_tensor(name, shape, dt))

    ident_bf = sbK("ident_bf", [128, 128], BF16)
    ident_f32 = sbK("ident_f32", [128, 128], F32)
    iota_pf = sbK("iota_pf", [128, 128], F32)
    c_invsqd = sbK("c_invsqd", [128, 1], F32)
    c_eps = sbK("c_eps", [128, 1], F32)
    c_nmean = sbK("c_nmean", [128, 1], F32)
    ones_f32 = sbK("ones_f32", [128, 1], F32)
    ones_bf = sbK("ones_bf", [128, 1], BF16)
    onesr_f32 = sbK("onesr_f32", [1, 128], F32)
    bq_s = sbK("bq_s", [128, KC], F32)
    bk_s = sbK("bk_s", [128, KC], F32)
    bv_s = sbK("bv_s", [128, KC], F32)
    maskl_s = sbK("maskl_s", [128, 1], F32)
    maskr_s = sbK("maskr_s", [128, 1], F32)
    base16_s = sbK("base16_s", [16, 1], F32)
    ln_s = sbK("ln_s", [128, 10], F32)
    ln_q = sbK("ln_q", [128, 10], F32)
    ln_v = sbK("ln_v", [128, 10], F32)
    ln_sd = sbK("ln_sd", [128, 10], F32)
    ln_r = sbK("ln_r", [128, 10], F32)
    ln_m = sbK("ln_m", [128, 10], F32)

    iop = G.iota(iota_pf[:, :], pattern=[[-1, 128]], base=0, channel_multiplier=1,
                 allow_small_or_imprecise_dtypes=True)
    sg.inc(iop, 1)
    s_d[0].inc(SY.dma_start(out=bq_s[:], in_=bq[:, :]), 16)
    s_d[0].inc(SY.dma_start(out=bk_s[:], in_=bk[:, :]), 16)
    s_d[0].inc(SY.dma_start(out=bv_s[:], in_=bv[:, :]), 16)
    s_d[0].inc(SY.dma_start(out=maskl_s[:], in_=mask_l[:, :]), 16)
    s_d[0].inc(SY.dma_start(out=maskr_s[:], in_=mask_r[:, :]), 16)
    s_d[0].inc(SY.dma_start(out=base16_s[:], in_=base16[:, :]), 16)
    V.memset(c_invsqd[:], INV_SQRT_D)
    V.memset(c_eps[:], 1.0e-5)
    V.memset(c_nmean[:], -1.0 / float(D))
    V.memset(ones_f32[:], 1.0)
    V.memset(ones_bf[:], 1.0)
    sv.inc(V.memset(onesr_f32[:], 1.0), 1)
    sg.wait(V)
    V.tensor_single_scalar(out=ident_bf[:], in_=iota_pf[:], scalar=0.0, op=ALU.is_equal)
    sv.inc(V.tensor_single_scalar(out=ident_f32[:], in_=iota_pf[:], scalar=0.0, op=ALU.is_equal), 1)
    s_d[0].wait(V)

    esR1 = ExitStack()
    ctx = esR1.enter_context(nc.sbuf_tensor("ctx", [128, NB * D], BF16, side="right"))
    esR2 = ExitStack()
    q_fm = esR2.enter_context(nc.sbuf_tensor("q_fm", [128, NH * TPC], BF16, side="right"))
    k_fm = esR2.enter_context(nc.sbuf_tensor("k_fm", [128, NH * TW], BF16, side="right"))
    v_tm = esR2.enter_context(nc.sbuf_tensor("v_tm", [128, 9 * D], BF16, side="right"))
    q3 = q_fm.rearrange("p (h t) -> p h t", h=NH)
    k3 = k_fm.rearrange("p (h t) -> p h t", h=NH)
    v3 = v_tm.rearrange("p (j d) -> p j d", j=9)

    esLN = ExitStack()
    lnx = esLN.enter_context(nc.sbuf_tensor("lnx", [128, KC * TW], BF16, side="right"))
    lnx3 = lnx.rearrange("p (c t) -> p c t", c=KC)

    es = ExitStack()
    xstage = es.enter_context(nc.sbuf_tensor("xstage", [128, D], F32, side="right"))
    lntm = es.enter_context(nc.sbuf_tensor("lntm", [128, D], BF16, side="right"))
    sqd = es.enter_context(nc.sbuf_tensor("sqd", [128, D], F32, side="right"))
    pt_ln = es.enter_context(nc.psum_tensor("pt_ln", [128, 2048], BF16))

    def ln_tile(i, nrow, src_ap, out3, dst_col0, xs, sq_t, lntm_t, pt):
        sv.wait(SY)
        dcnt = sd.inc(SY.dma_start(out=xs, in_=src_ap), 16)
        sd.wait(V, dcnt)
        V.reduce_sum(out=ln_s[0:nrow, i:i + 1], in_=xs, axis=AX.X)
        op = V.scalar_tensor_tensor(out=sq_t, in0=xs, scalar=1.0, in1=xs,
                                    op0=ALU.mult, op1=ALU.mult,
                                    accum_out=ln_q[0:nrow, i:i + 1])
        sv.inc(op, 1)
        sv.wait(V)
        op = V.tensor_scalar(out=ln_m[0:nrow, i:i + 1], in0=ln_s[0:nrow, i:i + 1],
                             scalar1=c_nmean[0:nrow, :], scalar2=None, op0=ALU.mult)
        sv.inc(op, 1)
        sv.wait(V)
        op = V.tensor_mul(out=ln_sd[0:nrow, i:i + 1], in0=ln_m[0:nrow, i:i + 1],
                          in1=ln_m[0:nrow, i:i + 1])
        sv.inc(op, 1)
        sv.wait(V)
        op = V.scalar_tensor_tensor(out=ln_v[0:nrow, i:i + 1], in0=ln_q[0:nrow, i:i + 1],
                                    scalar=1.0 / float(D), in1=ln_sd[0:nrow, i:i + 1],
                                    op0=ALU.mult, op1=ALU.subtract)
        sv.inc(op, 1)
        sv.wait(S)
        op = S.activation(out=ln_sd[0:nrow, i:i + 1], in_=ln_v[0:nrow, i:i + 1],
                          func=AF.Sqrt, bias=c_eps[0:nrow, :], scale=1.0)
        ss.inc(op, 1)
        ss.wait(V)
        op = V.reciprocal(out=ln_r[0:nrow, i:i + 1], in_=ln_sd[0:nrow, i:i + 1])
        sv.inc(op, 1)
        sv.wait(V)
        op = V.tensor_scalar(out=lntm_t, in0=xs, scalar1=ln_m[0:nrow, i:i + 1],
                             scalar2=ln_r[0:nrow, i:i + 1], op0=ALU.add, op1=ALU.mult)
        sv.inc(op, 1)
        sv.wait(T)
        for kc in range(KC):
            mm = T.transpose(pt[:, kc * 128: kc * 128 + nrow],
                             lntm_t[0:nrow, kc * 128:(kc + 1) * 128] if False
                             else lntm_t.tensor[0:nrow, kc * 128:(kc + 1) * 128]
                             if hasattr(lntm_t, "tensor") else lntm_t,
                             ident_bf[:])
        st.inc(mm, 1)
        st.wait(V)
        out_ap = out3[:, :, dst_col0:dst_col0 + nrow]
        in_ap = pt.rearrange("p (c t) -> p c t", c=KC)[:, :, 0:nrow]
        sv.inc(V.tensor_copy(out=out_ap, in_=in_ap), 1)

    def ln_tile2(i, nrow, src_ap, out3, dst_col0):
        xs = xstage[0:nrow, 0:D]
        sv.wait(SY)
        dcnt = sd.inc(SY.dma_start(out=xs, in_=src_ap), 16)
        sd.wait(V, dcnt)
        V.reduce_sum(out=ln_s[0:nrow, i:i + 1], in_=xs, axis=AX.X)
        op = V.scalar_tensor_tensor(out=sqd[0:nrow, :], in0=xs, scalar=1.0, in1=xs,
                                    op0=ALU.mult, op1=ALU.mult,
                                    accum_out=ln_q[0:nrow, i:i + 1])
        sv.inc(op, 1)
        sv.wait(V)
        op = V.tensor_scalar(out=ln_m[0:nrow, i:i + 1], in0=ln_s[0:nrow, i:i + 1],
                             scalar1=c_nmean[0:nrow, :], scalar2=None, op0=ALU.mult)
        sv.inc(op, 1)
        sv.wait(V)
        op = V.tensor_mul(out=ln_sd[0:nrow, i:i + 1], in0=ln_m[0:nrow, i:i + 1],
                          in1=ln_m[0:nrow, i:i + 1])
        sv.inc(op, 1)
        sv.wait(V)
        op = V.scalar_tensor_tensor(out=ln_v[0:nrow, i:i + 1], in0=ln_q[0:nrow, i:i + 1],
                                    scalar=1.0 / float(D), in1=ln_sd[0:nrow, i:i + 1],
                                    op0=ALU.mult, op1=ALU.subtract)
        sv.inc(op, 1)
        sv.wait(S)
        op = S.activation(out=ln_sd[0:nrow, i:i + 1], in_=ln_v[0:nrow, i:i + 1],
                          func=AF.Sqrt, bias=c_eps[0:nrow, :], scale=1.0)
        ss.inc(op, 1)
        ss.wait(V)
        op = V.reciprocal(out=ln_r[0:nrow, i:i + 1], in_=ln_sd[0:nrow, i:i + 1])
        sv.inc(op, 1)
        sv.wait(V)
        op = V.tensor_scalar(out=lntm[0:nrow, 0:D], in0=xs, scalar1=ln_m[0:nrow, i:i + 1],
                             scalar2=ln_r[0:nrow, i:i + 1], op0=ALU.add, op1=ALU.mult)
        sv.inc(op, 1)
        sv.wait(T)
        for kc in range(KC):
            mm = T.transpose(pt_ln[:, kc * 128: kc * 128 + nrow],
                             lntm[0:nrow, kc * 128:(kc + 1) * 128],
                             ident_bf[0:nrow, 0:nrow])
        st.inc(mm, 1)
        st.wait(V)
        out_ap = out3[:, :, dst_col0:dst_col0 + nrow]
        in_ap = pt_ln.rearrange("p (c t) -> p c t", c=KC)[:, :, 0:nrow]
        sv.inc(V.tensor_copy(out=out_ap, in_=in_ap), 1)

    ln_tile2(8, HALO, x_hl[:, :], lnx3, 0)
    ln_tile2(9, HALO, x_hr[:, :], lnx3, HALO + TPC)
    for i in range(NB):
        ln_tile2(i, 128, x_main[i * 128:(i + 1) * 128, :], lnx3, HALO + i * 128)
    es.close()

    es = ExitStack()
    wstage = es.enter_context(nc.sbuf_tensor("wstage", [128, 2 * 1024], F32, side="right"))
    wbf = es.enter_context(nc.sbuf_tensor("wbf", [128, 2 * 1024], BF16, side="right"))
    vfm_s = es.enter_context(nc.sbuf_tensor("vfm_s", [128, TW], BF16, side="right"))
    sqb = es.enter_context(nc.sbuf_tensor("sqb", [128, 512], F32, side="right"))
    norms16 = es.enter_context(nc.sbuf_tensor("norms16", [16, TPC], F32, side="right"))
    pq0 = es.enter_context(nc.psum_tensor("pq0", [128, 512], F32))
    pq1 = es.enter_context(nc.psum_tensor("pq1", [128, 512], F32))
    pk2 = es.enter_context(nc.psum_tensor("pk2", [128, 128], F32))
    pt_v = es.enter_context(nc.psum_tensor("pt_v", [128, 9 * 128], BF16))

    wbf3 = wbf.rearrange("p (s j f) -> p s j f", s=2, j=8)
    slot_cast = {}
    slot_mm = {}
    gslot = [0]

    def stage_half(src, mc, wh, cast_eng, cast_sem):
        slot = gslot[0] % 2
        gslot[0] += 1
        if slot in slot_cast:
            sem_prev, ccnt_prev = slot_cast[slot]
            sem_prev.wait(SY, ccnt_prev)
        s_w[slot].inc(SY.dma_start(out=wstage[:, slot * 1024:(slot + 1) * 1024],
                                   in_=src[mc, wh, :, :]), 16)
        s_w[slot].wait(cast_eng)
        if slot in slot_mm:
            st.wait(cast_eng, slot_mm[slot])
        if cast_eng is S:
            op = cast_eng.copy(out=wbf[:, slot * 1024:(slot + 1) * 1024],
                               in_=wstage[:, slot * 1024:(slot + 1) * 1024])
        else:
            op = cast_eng.tensor_copy(out=wbf[:, slot * 1024:(slot + 1) * 1024],
                                      in_=wstage[:, slot * 1024:(slot + 1) * 1024])
        ccnt = cast_sem.inc(op, 1)
        slot_cast[slot] = (cast_sem, ccnt)
        return slot, ccnt

    sv.wait(SY)
    for mc in range(KC):
        sv.wait(T)
        for wh in range(2):
            slot, ccnt = stage_half(wq, mc, wh, V, sv)
            sv.wait(T, ccnt)
            for half in range(2):
                ps = pq0 if half == 0 else pq1
                for j in range(8):
                    kc = wh * 8 + j
                    mm = T.matmul(ps[:, 0:512], wbf3[:, slot, j, :],
                                  lnx3[:, kc, HALO + half * 512: HALO + half * 512 + 512],
                                  start=(kc == 0), stop=(kc == KC - 1))
            st.inc(mm, 1)
            slot_mm[slot] = st.n
        st.wait(V)
        for half in range(2):
            ps = pq0 if half == 0 else pq1
            op = V.tensor_scalar(out=q3[:, mc, half * 512: half * 512 + 512],
                                 in0=ps[:, 0:512], scalar1=bq_s[:, mc:mc + 1],
                                 scalar2=None, op0=ALU.add)
            sv.inc(op, 1)

    for mc in range(KC):
        sv.wait(T)
        for wh in range(2):
            slot, ccnt = stage_half(wk, mc, wh, S, ss)
            ss.wait(T, ccnt)
            for (ps, off, nlen) in ((pq0, 0, 512), (pq1, 512, 512), (pk2, 1024, 128)):
                for j in range(8):
                    kc = wh * 8 + j
                    mm = T.matmul(ps[:, 0:nlen], wbf3[:, slot, j, :],
                                  lnx3[:, kc, off:off + nlen],
                                  start=(kc == 0), stop=(kc == KC - 1))
            st.inc(mm, 1)
            slot_mm[slot] = st.n
        st.wait(V)
        for (ps, off, nlen) in ((pq0, 0, 512), (pq1, 512, 512), (pk2, 1024, 128)):
            op = V.tensor_scalar(out=k3[:, mc, off:off + nlen],
                                 in0=ps[:, 0:nlen], scalar1=bk_s[:, mc:mc + 1],
                                 scalar2=None, op0=ALU.add)
            sv.inc(op, 1)

    for mc in range(KC):
        sv.wait(T)
        for wh in range(2):
            slot, ccnt = stage_half(wv, mc, wh, S, ss)
            ss.wait(T, ccnt)
            for (ps, off, nlen) in ((pq0, 0, 512), (pq1, 512, 512), (pk2, 1024, 128)):
                for j in range(8):
                    kc = wh * 8 + j
                    mm = T.matmul(ps[:, 0:nlen], wbf3[:, slot, j, :],
                                  lnx3[:, kc, off:off + nlen],
                                  start=(kc == 0), stop=(kc == KC - 1))
            st.inc(mm, 1)
            slot_mm[slot] = st.n
        st.wait(V)
        for (ps, off, nlen) in ((pq0, 0, 512), (pq1, 512, 512), (pk2, 1024, 128)):
            op = V.tensor_scalar(out=vfm_s[:, off:off + nlen],
                                 in0=ps[:, 0:nlen], scalar1=bv_s[:, mc:mc + 1],
                                 scalar2=None, op0=ALU.add)
            sv.inc(op, 1)
        sv.wait(T)
        for j in range(9):
            mm = T.transpose(pt_v[:, j * 128:(j + 1) * 128],
                             vfm_s[:, j * 128:(j + 1) * 128], ident_bf[:])
        st.inc(mm, 1)
        st.wait(V)
        sv.inc(V.tensor_copy(out=v3[:, :, mc * 128:(mc + 1) * 128],
                             in_=pt_v.rearrange("p (j d) -> p j d", j=9)), 1)

    es2n = ExitStack()
    ohall_i = es2n.enter_context(nc.sbuf_tensor("ohall_i", [128, 256], F32, side="right"))
    ohall = es2n.enter_context(nc.sbuf_tensor("ohall", [128, 256], F32, side="right"))
    pnrm = es2n.enter_context(nc.psum_tensor("pnrm", [16, 512], F32))
    iop = G.iota(ohall_i[:, :], pattern=[[-1, 16], [1, 16]], base=0,
                 channel_multiplier=0, allow_small_or_imprecise_dtypes=True)
    sg.inc(iop, 1)
    sg.wait(V)
    op = V.tensor_single_scalar(out=ohall[:, :], in_=ohall_i[:, :], scalar=0.0,
                                op=ALU.is_equal)
    sv.inc(op, 1)
    for half in range(2):
        for h in range(NH):
            sv.wait(V)
            st.wait(V)
            op = V.tensor_mul(out=sqb[:, 0:512],
                              in0=q3[:, h, half * 512: half * 512 + 512],
                              in1=q3[:, h, half * 512: half * 512 + 512])
            sv.inc(op, 1)
            sv.wait(T)
            mm = T.matmul(pnrm[0:16, 0:512], ohall[:, h * 16:(h + 1) * 16],
                          sqb[:, 0:512], start=(h == 0), stop=(h == NH - 1))
            st.inc(mm, 1)
        st.wait(V)
        op = V.tensor_copy(out=norms16[:, half * 512: half * 512 + 512],
                           in_=pnrm[0:16, 0:512])
        sv.inc(op, 1)
        sv.wait(T)
    es2n.close()
    sv.wait(SY)
    sd.inc(SY.dma_start(out=cc_nrm_in.ap()[:, :], in_=norms16[:, :]), 16)
    sd.wait(G)
    cc = G.collective_compute("AllGather", ALU.bypass, replica_groups=RG,
                              ins=[cc_nrm_in.ap().opt()], outs=[cc_nrm_out.ap().opt()])
    scc.inc(cc, 1)
    es.close()
    esLN.close()

    es = ExitStack()
    e_loc = es.enter_context(nc.sbuf_tensor("e_loc", [128, 2 * 1024], BF16, side="right"))
    pr_loc = es.enter_context(nc.sbuf_tensor("pr_loc", [128, 2 * 1024], BF16, side="right"))
    pT_loc = es.enter_context(nc.sbuf_tensor("pT_loc", [128, 2 * 1024], BF16, side="right"))
    m4 = es.enter_context(nc.sbuf_tensor("m4", [128, 8], F32, side="right"))
    nm4 = es.enter_context(nc.sbuf_tensor("nm4", [128, 8], F32, side="right"))
    l4 = es.enter_context(nc.sbuf_tensor("l4", [128, 8], F32, side="right"))
    r4 = es.enter_context(nc.sbuf_tensor("r4", [128, 8], F32, side="right"))
    psc0 = es.enter_context(nc.psum_tensor("psc0", [128, 1024], F32))
    psc1 = es.enter_context(nc.psum_tensor("psc1", [128, 1024], F32))
    ppt0 = es.enter_context(nc.psum_tensor("ppt0", [128, 1024], BF16))
    ppt1 = es.enter_context(nc.psum_tensor("ppt1", [128, 1024], BF16))
    ppv0 = es.enter_context(nc.psum_tensor("ppv0", [128, 512], F32))
    ppv1 = es.enter_context(nc.psum_tensor("ppv1", [128, 512], F32))

    evac_cnt = {}
    pv_cnt = {}
    exp_cnt = {}
    prb_cnt = {}
    red_cnt = {}
    for b in range(NB):
        for g in range(4):
            par = (b * 4 + g) % 2
            psc = psc0 if par == 0 else psc1
            ppt = ppt0 if par == 0 else ppt1
            ppv = ppv0 if par == 0 else ppv1
            eb = e_loc[:, par * 1024:(par + 1) * 1024]
            prb = pr_loc[:, par * 1024:(par + 1) * 1024]
            ptb = pT_loc[:, par * 1024:(par + 1) * 1024]
            m4b = m4[:, par * 4:(par + 1) * 4]
            nm4b = nm4[:, par * 4:(par + 1) * 4]
            l4b = l4[:, par * 4:(par + 1) * 4]
            r4b = r4[:, par * 4:(par + 1) * 4]
            heads = [4 * g + i for i in range(4)]
            if par in exp_cnt:
                ss.wait(T, exp_cnt[par])
            if par in red_cnt:
                sv.wait(T, red_cnt[par])
            for i, h in enumerate(heads):
                mm = T.matmul(psc[:, i * 256:(i + 1) * 256],
                              q3[:, h, b * 128:(b + 1) * 128],
                              k3[:, h, b * 128: b * 128 + 256],
                              start=True, stop=True)
            qk_cnt = st.inc(mm, 1)
            st.wait(V, qk_cnt)
            op = V.reduce_max(out=m4b, in_=psc.rearrange("p (i k) -> p i k", i=4), axis=AX.X)
            sv.inc(op, 1)
            sv.wait(V)
            op = V.tensor_scalar(out=nm4b, in0=m4b, scalar1=-INV_SQRT_D,
                                 scalar2=None, op0=ALU.mult)
            nm_cnt = sv.inc(op, 1)
            red_cnt[par] = nm_cnt
            sv.wait(S, nm_cnt)
            edge = (b == 0) or (b == NB - 1)
            if par in prb_cnt:
                sv.wait(S, prb_cnt[par])
            for i, h in enumerate(heads):
                act = S.activation(out=eb[:, i * 256:(i + 1) * 256],
                                   in_=psc[:, i * 256:(i + 1) * 256],
                                   func=AF.Exp, bias=nm4b[:, i:i + 1],
                                   scale=c_invsqd[:, 0:1],
                                   accum_out=(None if edge else l4b[:, i:i + 1]))
            exp_cnt[par] = ss.inc(act, 1)
            ss.wait(V, exp_cnt[par])
            if b == 0:
                eb4 = eb.rearrange("p (i k) -> p i k", i=4)
                op = V.tensor_scalar(out=eb4[:, :, 0:64], in0=eb4[:, :, 0:64],
                                     scalar1=maskl_s[:, 0:1], scalar2=None, op0=ALU.mult)
                sv.inc(op, 1)
                sv.wait(V)
            if b == NB - 1:
                eb4 = eb.rearrange("p (i k) -> p i k", i=4)
                op = V.tensor_scalar(out=eb4[:, :, 192:256], in0=eb4[:, :, 192:256],
                                     scalar1=maskr_s[:, 0:1], scalar2=None, op0=ALU.mult)
                sv.inc(op, 1)
                sv.wait(V)
            if edge:
                op = V.reduce_sum(out=l4b, in_=eb.rearrange("p (i k) -> p i k", i=4),
                                  axis=AX.X)
                sv.inc(op, 1)
                sv.wait(V)
            op = V.reciprocal(out=r4b, in_=l4b)
            sv.inc(op, 1)
            sv.wait(V)
            if par in pv_cnt:
                st.wait(V, pv_cnt[par])
            op = V.tensor_tensor(out=prb.rearrange("p (i k) -> p i k", i=4),
                                 in0=eb.rearrange("p (i k) -> p i k", i=4),
                                 in1=r4b.unsqueeze(2).broadcast_to([128, 4, 256]),
                                 op=ALU.mult)
            prb_cnt[par] = sv.inc(op, 1)
            sv.wait(T, prb_cnt[par])
            if par in evac_cnt:
                sv.wait(T, evac_cnt[par])
            for i in range(4):
                for c in range(2):
                    mm = T.transpose(ppt[:, (i * 2 + c) * 128:(i * 2 + c + 1) * 128],
                                     prb[:, i * 256 + c * 128: i * 256 + c * 128 + 128],
                                     ident_bf[:])
            tr_cnt = st.inc(mm, 1)
            st.wait(V, tr_cnt)
            op = V.tensor_copy(out=ptb, in_=ppt[:, :])
            ptb_cnt = sv.inc(op, 1)
            sv.wait(T, ptb_cnt)
            for i, h in enumerate(heads):
                for c in range(2):
                    mm = T.matmul(ppv[:, i * 128:(i + 1) * 128],
                                  ptb[:, (i * 2 + c) * 128:(i * 2 + c + 1) * 128],
                                  v3[:, b + c, h * 128:(h + 1) * 128],
                                  start=(c == 0), stop=(c == 1))
            pv_cnt[par] = st.inc(mm, 1)
            st.wait(V, pv_cnt[par])
            op = V.tensor_copy(out=ctx[:, b * D + g * 512: b * D + (g + 1) * 512],
                               in_=ppv[:, 0:512])
            evac_cnt[par] = sv.inc(op, 1)
    es.close()

    esP = ExitStack()

    def sbP(name, shape, dt):
        return esP.enter_context(nc.sbuf_tensor(name, shape, dt))

    gidxn = sbP("gidxn", [128, TPC], F32)
    t_pb = sbP("t_pb", [128, 512], F32)
    iota_t = sbP("iota_t", [64, TPC], F32)
    bc16i = sbP("bc16i", [16, 128], F32)
    bc16 = sbP("bc16", [16, 128], F32)
    iota16 = sbP("iota16", [16, 1], F32)
    pm16 = esP.enter_context(nc.psum_tensor("pm16", [128, 1], F32))
    normsG = sbP("normsG", [128, TPC], F32)
    normsW = sbP("normsW", [128, TPC], F32)
    maskT = sbP("maskT", [128, TPC], U8)
    candA = sbP("candA", [128, 64], F32)
    cand2 = sbP("cand2", [16, 512], F32)
    sort2 = sbP("sort2", [16, 64], F32)
    thr = sbP("thr", [128, 1], F32)
    jrow = sbP("jrow", [128, 5], F32)
    i32a = sbP("i32a", [128, 1], I32)
    idx_g = sbP("idx_g", [16, 64], F32)
    idx_l = sbP("idx_l", [16, 64], F32)
    idxT = sbP("idxT", [64, 16], F32)
    idxflat = sbP("idxflat", [1, NH * 64], F32)
    idx_bc = sbP("idx_bc", [128, NH * 64], F32)
    qg_sb = sbP("qg_sb", [128, NH * 64], BF16)
    q_tmh = sbP("q_tmh", [128, TPC], BF16)
    oh_g = sbP("oh_g", [128, 512], BF16)
    m_all = sbP("m_all", [64, NH], F32)
    nmG = sbP("nmG", [64, NH], F32)
    rloc = sbP("rloc", [64, NH], F32)
    e_g = sbP("e_g", [64, TW], BF16)
    eT_g = sbP("eT_g", [128, 9 * 64], BF16)
    ol = sbP("ol", [64, NH * HD + NH], F32)
    gctx = sbP("gctx", [64, NH * HD], BF16)
    oh_s = sbP("oh_s", [64, 128], BF16)
    cnt_sb = sbP("cnt_sb", [128, NH], F32)
    zq = sbP("zq", [128, NH], F32)
    rq = sbP("rq", [128, NH], F32)
    t1_sb = sbP("t1_sb", [128, D], BF16)

    G.iota(gidxn[:, :], pattern=[[-1, TPC]], base=0, channel_multiplier=0,
           allow_small_or_imprecise_dtypes=True)
    G.iota(t_pb[:, :], pattern=[[128, 8], [0, 64]], base=0, channel_multiplier=1,
           allow_small_or_imprecise_dtypes=True)
    G.iota(iota_t[:, :], pattern=[[1, TPC]], base=0, channel_multiplier=0,
           allow_small_or_imprecise_dtypes=True)
    G.iota(bc16i[:, :], pattern=[[-1, 16], [0, 8]], base=0, channel_multiplier=1,
           allow_small_or_imprecise_dtypes=True)
    G.iota(iota16[:, :], pattern=[[0, 1]], base=0, channel_multiplier=1,
           allow_small_or_imprecise_dtypes=True)
    iop = G.iota(jrow[:, 0:1], pattern=[[0, 1]], base=0, channel_multiplier=1,
                 allow_small_or_imprecise_dtypes=True)
    sg.inc(iop, 1)
    sg.wait(V)
    sg.wait(T)
    sv.inc(V.tensor_single_scalar(out=bc16[:, :], in_=bc16i[:, :], scalar=0.0,
                                  op=ALU.is_equal), 1)
    sv.wait(T)
    mm = T.matmul(pm16[:, 0:1], bc16[:, :], iota16[:, :], start=True, stop=True)
    st.inc(mm, 1)
    st.wait(V)
    sv.inc(V.tensor_copy(out=jrow[:, 2:3], in_=pm16[:, 0:1]), 1)
    sv.wait(V)
    op = V.tensor_scalar(out=jrow[:, 3:4], in0=jrow[:, 2:3], scalar1=8192.0,
                         scalar2=None, op0=ALU.mult)
    sv.inc(op, 1)
    sv.wait(V)
    op = V.scalar_tensor_tensor(out=jrow[:, 4:5], in0=jrow[:, 0:1], scalar=-1024.0,
                                in1=jrow[:, 3:4], op0=ALU.mult, op1=ALU.add)
    sv.inc(op, 1)
    sv.wait(V)
    op = V.tensor_scalar(out=gidxn[:, :], in0=gidxn[:, :], scalar1=jrow[:, 4:5],
                         scalar2=None, op0=ALU.add)
    sv.inc(op, 1)

    scc.wait(G)
    sq.inc(G.dma_start(out=scr_n.ap().rearrange("h (j t) -> j h t", j=8),
                       in_=cc_nrm_out.ap().rearrange("(j h) t -> j h t", h=16)), 16)
    sq.wait(G)
    sq.inc(G.dma_start(out=normsG[:, :],
                       in_=scr_n.ap().rearrange("h (j t) -> (h j) t", j=8)), 16)
    sq.wait(V)
    sv.inc(V.tensor_copy(out=normsW[:, :], in_=normsG[:, :]), 1)
    sv.wait(V)

    def extract64(data, out_tile):
        for r in range(8):
            op = V.max(out=out_tile[:, r * 8:(r + 1) * 8], in_=data)
            sv.inc(op, 1)
            sv.wait(V)
            if r < 7:
                op = V.match_replace(out=data, in_to_replace=out_tile[:, r * 8:(r + 1) * 8],
                                     in_values=data, imm_value=-BIG)
                sv.inc(op, 1)
                sv.wait(V)

    def cand_regroup():
        sv.wait(G)
        sq.inc(G.dma_start(out=scr_c.ap()[:, :], in_=candA[:, :]), 16)
        sq.wait(G)
        sq.inc(G.dma_start(out=cand2[:, :],
                           in_=scr_c.ap().rearrange("(h j) c -> h (j c)", h=16)), 16)
        sq.wait(V)

    extract64(normsW[:, :], candA)
    cand_regroup()
    extract64(cand2[:, :], sort2)
    sv.wait(T)
    mm = T.matmul(pm16[:, 0:1], bc16[:, :], sort2[:, 61:62], start=True, stop=True)
    st.inc(mm, 1)
    st.wait(V)
    sv.inc(V.tensor_copy(out=thr[:, :], in_=pm16[:, 0:1]), 1)
    sv.wait(V)
    op = V.tensor_scalar(out=maskT[:, :], in0=normsG[:, :], scalar1=thr[:, 0:1],
                         scalar2=None, op0=ALU.is_ge)
    sv.inc(op, 1)
    sv.inc(V.memset(normsW[:, :], -BIG), 1)
    sv.wait(V)
    sv.inc(V.copy_predicated(out=normsW[:, :], mask=maskT[:, :], data=gidxn[:, :]), 1)
    sv.wait(V)
    extract64(normsW[:, :], candA)
    cand_regroup()
    extract64(cand2[:, :], sort2)
    sv.wait(V)
    sv.inc(V.memset(idx_g[:, 0:1], 0.0), 1)
    sv.inc(V.memset(idx_g[:, 63:64], 8191.0), 1)
    op = V.tensor_scalar(out=idx_g[:, 1:63], in0=sort2[:, 0:62], scalar1=-1.0,
                         scalar2=None, op0=ALU.mult)
    sv.inc(op, 1)
    sv.wait(V)
    op = V.tensor_scalar(out=idx_l[:, :], in0=idx_g[:, :], scalar1=base16_s[:, 0:1],
                         scalar2=None, op0=ALU.subtract)
    sv.inc(op, 1)
    sv.wait(T)
    sv.wait(G)

    es2 = ExitStack()
    ptf = es2.enter_context(nc.psum_tensor("ptf", [128, 512], F32))
    mm = T.transpose(ptf[0:64, 0:16], idx_l[:, :], ident_f32[0:16, 0:16])
    st.inc(mm, 1)
    st.wait(V)
    sv.inc(V.tensor_copy(out=idxT[:, :], in_=ptf[0:64, 0:16]), 1)
    sq.inc(G.dma_start(out=scr_i.ap().rearrange("o (h j) -> (o h) j", h=16),
                       in_=idx_l[:, :]), 16)
    sq.wait(G)
    sq.inc(G.dma_start(out=idxflat[0:1, :], in_=scr_i.ap()), 16)
    sq.wait(T)
    sv.wait(T)
    for half in range(2):
        mm = T.matmul(ptf[:, 0:512], onesr_f32[0:1, :],
                      idxflat[0:1, half * 512: half * 512 + 512], start=True, stop=True)
        st.inc(mm, 1)
        st.wait(V)
        op = V.tensor_copy(out=idx_bc[:, half * 512: half * 512 + 512], in_=ptf[:, 0:512])
        sv.inc(op, 1)
        sv.wait(T)
    es2.close()

    es2 = ExitStack()
    pqt = es2.enter_context(nc.psum_tensor("pqt", [128, 1024], BF16))
    pqg0 = es2.enter_context(nc.psum_tensor("pqg0", [128, 512], F32))
    pqg1 = es2.enter_context(nc.psum_tensor("pqg1", [128, 512], F32))
    for h in range(NH):
        sv.wait(T)
        for j in range(8):
            mm = T.transpose(pqt[:, j * 128:(j + 1) * 128],
                             q3[:, h, j * 128:(j + 1) * 128], ident_bf[:])
        st.inc(mm, 1)
        st.wait(V)
        sv.inc(V.tensor_copy(out=q_tmh[:, :], in_=pqt[:, :]), 1)
        in0 = idx_bc.ap()[:, h * 64:(h + 1) * 64].unsqueeze(1).broadcast_to([128, 8, 64])
        in1 = t_pb.rearrange("p (b j) -> p b j", b=8)
        op = V.tensor_tensor(out=oh_g.rearrange("p (b j) -> p b j", b=8),
                             in0=in0, in1=in1, op=ALU.is_equal)
        ohcnt = sv.inc(op, 1)
        sv.wait(T, ohcnt)
        pq = pqg0 if h < 8 else pqg1
        off = (h % 8) * 64
        for j in range(8):
            mm = T.matmul(pq[:, off:off + 64], q_tmh[:, j * 128:(j + 1) * 128],
                          oh_g[:, j * 64:(j + 1) * 64],
                          start=(j == 0), stop=(j == 7))
        st.inc(mm, 1)
    st.wait(V)
    sv.inc(V.tensor_copy(out=qg_sb[:, 0:512], in_=pqg0[:, :]), 1)
    sv.inc(V.tensor_copy(out=qg_sb[:, 512:1024], in_=pqg1[:, :]), 1)
    es2.close()
    sv.wait(SY)
    sd.inc(SY.dma_start(out=cc_qg_in.ap()[:, :], in_=qg_sb[:, :]), 16)
    sd.wait(G)
    cc = G.collective_compute("AllReduce", ALU.add, replica_groups=RG,
                              ins=[cc_qg_in.ap().opt()], outs=[cc_qg_out.ap().opt()])
    scc.inc(cc, 1)
    scc.wait(G)
    sq.inc(G.dma_start(out=qg_sb[:, :], in_=cc_qg_out.ap()[:, :]), 16)

    es2 = ExitStack()
    psg = es2.enter_context(nc.psum_tensor("psg", [64, TW], F32))
    pet = es2.enter_context(nc.psum_tensor("pet", [128, 9 * 64], BF16))
    pop = es2.enter_context(nc.psum_tensor("pop", [64, 128], F32))
    qgv = qg_sb.rearrange("p (h j) -> p h j", h=NH)
    sq.wait(T)

    def g_scores(h):
        for (off, nlen) in ((0, 512), (512, 512), (1024, 128)):
            mm = T.matmul(psg[:, off:off + nlen], qgv[:, h, :], k3[:, h, off:off + nlen],
                          start=True, stop=True)
        return mm

    for h in range(NH):
        if h > 0:
            sv.wait(T)
        mm = g_scores(h)
        st.inc(mm, 1)
        st.wait(V)
        sv.inc(V.reduce_max(out=m_all[:, h:h + 1], in_=psg[:, :], axis=AX.X), 1)
    sv.wait(SY)
    sd.inc(SY.dma_start(out=cc_m_in.ap()[:, :], in_=m_all[:, :]), 16)
    sd.wait(G)
    cc = G.collective_compute("AllReduce", ALU.max, replica_groups=RG,
                              ins=[cc_m_in.ap().opt()], outs=[cc_m_out.ap().opt()])
    scc.inc(cc, 1)
    scc.wait(G)
    sq.inc(G.dma_start(out=m_all[:, :], in_=cc_m_out.ap()[:, :]), 16)
    sq.wait(V)
    op = V.tensor_scalar(out=nmG[:, :], in0=m_all[:, :], scalar1=-INV_SQRT_D,
                         scalar2=None, op0=ALU.mult)
    sv.inc(op, 1)
    sv.wait(S)
    sv.wait(T)
    tr_g = None
    for h in range(NH):
        if h > 0:
            ss.wait(T)
        mm = g_scores(h)
        st.inc(mm, 1)
        st.wait(S)
        if tr_g is not None:
            st.wait(S, tr_g)
        act = S.activation(out=e_g[:, :], in_=psg[:, :], func=AF.Exp,
                           bias=nmG[0:64, h:h + 1], scale=c_invsqd[0:64, 0:1])
        ecnt = ss.inc(act, 1)
        ss.wait(V, ecnt)
        sv.inc(V.memset(e_g[:, 0:HALO], 0.0), 1)
        sv.inc(V.memset(e_g[:, HALO + TPC:TW], 0.0), 1)
        sv.wait(V)
        op = V.reduce_sum(out=ol[:, NH * HD + h: NH * HD + h + 1],
                          in_=e_g[:, HALO:HALO + TPC], axis=AX.X)
        sv.inc(op, 1)
        sv.wait(T)
        for j in range(9):
            mm = T.transpose(pet[:, j * 64:(j + 1) * 64],
                             e_g[:, j * 128:(j + 1) * 128], ident_bf[0:64, 0:64])
        tr_g = st.inc(mm, 1)
        st.wait(V, tr_g)
        sv.inc(V.tensor_copy(out=eT_g[:, :], in_=pet[:, :]), 1)
        sv.wait(T)
        for j in range(9):
            mm = T.matmul(pop[:, 0:128], eT_g[:, j * 64:(j + 1) * 64],
                          v3[:, j, h * 128:(h + 1) * 128],
                          start=(j == 0), stop=(j == 8))
        st.inc(mm, 1)
        st.wait(V)
        sv.inc(V.tensor_copy(out=ol[:, h * HD:(h + 1) * HD], in_=pop[:, 0:128]), 1)
    es2.close()
    sv.wait(SY)
    sd.inc(SY.dma_start(out=cc_ol_in.ap()[:, :], in_=ol[:, :]), 16)
    sd.wait(G)
    cc = G.collective_compute("AllReduce", ALU.add, replica_groups=RG,
                              ins=[cc_ol_in.ap().opt()], outs=[cc_ol_out.ap().opt()])
    scc.inc(cc, 1)
    scc.wait(G)
    sq.inc(G.dma_start(out=ol[:, :], in_=cc_ol_out.ap()[:, :]), 16)
    sq.wait(V)
    for h in range(NH):
        sv.inc(V.reciprocal(out=rloc[:, h:h + 1], in_=ol[:, NH * HD + h: NH * HD + h + 1]), 1)
    sv.wait(V)
    op = V.tensor_tensor(out=gctx.rearrange("p (h d) -> p h d", h=NH),
                         in0=ol.ap()[:, 0:NH * HD].rearrange("p (h d) -> p h d", h=NH),
                         in1=rloc.ap().unsqueeze(2).broadcast_to([64, NH, HD]),
                         op=ALU.mult)
    sv.inc(op, 1)

    es2 = ExitStack()
    pscat = es2.enter_context(nc.psum_tensor("pscat", [128, 2048], F32))
    pcnt = es2.enter_context(nc.psum_tensor("pcnt", [128, 16], F32))
    for b in range(NB):
        sv.wait(T)
        for h in range(NH):
            op = V.tensor_scalar(out=oh_s[:, 0:128],
                                 in0=iota_t[:, b * 128:(b + 1) * 128],
                                 scalar1=idxT[:, h:h + 1], scalar2=None, op0=ALU.is_equal)
            ohcnt = sv.inc(op, 1)
            sv.wait(T, ohcnt)
            mm = T.matmul(pscat[:, h * 128:(h + 1) * 128], oh_s[:, 0:128],
                          gctx[:, h * HD:(h + 1) * HD],
                          start=True, stop=True)
            mm2 = T.matmul(pcnt[:, h:h + 1], oh_s[:, 0:128], ones_bf[0:64, 0:1],
                           start=True, stop=True)
            st.inc(mm2, 1)
            st.wait(V)
        op = V.tensor_copy(out=cnt_sb[:, :], in_=pcnt[:, :])
        sv.inc(op, 1)
        sv.wait(V)
        sv.inc(V.tensor_single_scalar(out=zq[:, :], in_=cnt_sb[:, :], scalar=0.0,
                                      op=ALU.is_equal), 1)
        sv.inc(V.tensor_scalar_max(out=rq[:, :], in0=cnt_sb[:, :], scalar1=1.0), 1)
        sv.wait(V)
        sv.inc(V.reciprocal(out=rq[:, :], in_=rq[:, :]), 1)
        sv.wait(V)
        op = V.tensor_tensor(out=t1_sb.rearrange("p (h d) -> p h d", h=NH),
                             in0=pscat.rearrange("p (h d) -> p h d", h=NH),
                             in1=rq.ap().unsqueeze(2).broadcast_to([128, NH, HD]),
                             op=ALU.mult)
        sv.inc(op, 1)
        ctxb = ctx.ap()[:, b * D:(b + 1) * D]
        op = V.tensor_tensor(out=ctxb.rearrange("p (h d) -> p h d", h=NH),
                             in0=ctxb.rearrange("p (h d) -> p h d", h=NH),
                             in1=zq.ap().unsqueeze(2).broadcast_to([128, NH, HD]),
                             op=ALU.mult)
        sv.inc(op, 1)
        sv.wait(V)
        op = V.tensor_tensor(out=ctxb, in0=ctxb, in1=t1_sb[:, :], op=ALU.add)
        sv.inc(op, 1)
    es2.close()
    esP.close()
    esR2.close()

    es = ExitStack()
    xst3 = es.enter_context(nc.sbuf_tensor("xst3", [128, 2 * D], F32, side="right"))
    for i in range(NB):
        slot = i % 2
        xs = xst3[:, slot * D:(slot + 1) * D]
        sv.wait(SY)
        so_sl[slot].wait(SY)
        dcnt = sd.inc(SY.dma_start(out=xs, in_=x_main[i * 128:(i + 1) * 128, :]), 16)
        sd.wait(V, dcnt)
        sv.wait(V)
        op = V.tensor_tensor(out=xs, in0=xs, in1=ctx[:, i * D:(i + 1) * D], op=ALU.add)
        acnt = sv.inc(op, 1)
        sv.wait(SY, acnt)
        so_sl[slot].inc(SY.dma_start(out=xres_dram[i * 128:(i + 1) * 128, :], in_=xs), 16)
    es.close()
    esR1.close()

    esL2 = ExitStack()
    lnx2 = esL2.enter_context(nc.sbuf_tensor("lnx2", [128, KC * TPC], BF16))
    lnx23 = lnx2.rearrange("p (c t) -> p c t", c=KC)
    es = ExitStack()
    xst4 = es.enter_context(nc.sbuf_tensor("xst4", [128, D], F32, side="right"))
    lntm2 = es.enter_context(nc.sbuf_tensor("lntm2", [128, D], BF16, side="right"))
    sqd2 = es.enter_context(nc.sbuf_tensor("sqd2", [128, D], F32, side="right"))
    pt_ln2 = es.enter_context(nc.psum_tensor("pt_ln2", [128, 2048], BF16))
    so_sl[0].wait(SY)
    so_sl[1].wait(SY)
    for i in range(NB):
        sv.wait(SY)
        dcnt = sd.inc(SY.dma_start(out=xst4[:, :], in_=xres_dram[i * 128:(i + 1) * 128, :]), 16)
        sd.wait(V, dcnt)
        xs = xst4[:, :]
        V.reduce_sum(out=ln_s[:, i:i + 1], in_=xs, axis=AX.X)
        op = V.scalar_tensor_tensor(out=sqd2[:, :], in0=xs, scalar=1.0, in1=xs,
                                    op0=ALU.mult, op1=ALU.mult, accum_out=ln_q[:, i:i + 1])
        sv.inc(op, 1)
        sv.wait(V)
        op = V.tensor_scalar(out=ln_m[:, i:i + 1], in0=ln_s[:, i:i + 1],
                             scalar1=c_nmean[:, 0:1], scalar2=None, op0=ALU.mult)
        sv.inc(op, 1)
        sv.wait(V)
        op = V.tensor_mul(out=ln_sd[:, i:i + 1], in0=ln_m[:, i:i + 1], in1=ln_m[:, i:i + 1])
        sv.inc(op, 1)
        sv.wait(V)
        op = V.scalar_tensor_tensor(out=ln_v[:, i:i + 1], in0=ln_q[:, i:i + 1],
                                    scalar=1.0 / float(D), in1=ln_sd[:, i:i + 1],
                                    op0=ALU.mult, op1=ALU.subtract)
        sv.inc(op, 1)
        sv.wait(S)
        op = S.activation(out=ln_sd[:, i:i + 1], in_=ln_v[:, i:i + 1], func=AF.Sqrt,
                          bias=c_eps[:, 0:1], scale=1.0)
        ss.inc(op, 1)
        ss.wait(V)
        op = V.reciprocal(out=ln_r[:, i:i + 1], in_=ln_sd[:, i:i + 1])
        sv.inc(op, 1)
        sv.wait(V)
        op = V.tensor_scalar(out=lntm2[:, :], in0=xs, scalar1=ln_m[:, i:i + 1],
                             scalar2=ln_r[:, i:i + 1], op0=ALU.add, op1=ALU.mult)
        sv.inc(op, 1)
        sv.wait(T)
        for kc in range(KC):
            mm = T.transpose(pt_ln2[:, kc * 128:(kc + 1) * 128],
                             lntm2[:, kc * 128:(kc + 1) * 128], ident_bf[:])
        st.inc(mm, 1)
        st.wait(V)
        op = V.tensor_copy(out=lnx23[:, :, i * 128:(i + 1) * 128],
                           in_=pt_ln2.rearrange("p (c t) -> p c t", c=KC))
        sv.inc(op, 1)
    es.close()

    es = ExitStack()
    hbuf = es.enter_context(nc.sbuf_tensor("hbuf", [128, ICN * 512], BF16, side="right"))
    gu_u8s = es.enter_context(nc.sbuf_tensor("gu_u8s", [128, 4 * D], U8, side="right"))
    gu_bf = es.enter_context(nc.sbuf_tensor("gu_bf", [128, 4 * D], BF16, side="right"))
    wd_u8s = es.enter_context(nc.sbuf_tensor("wd_u8s", [128, 2 * 512], U8, side="right"))
    wd_bf = es.enter_context(nc.sbuf_tensor("wd_bf", [128, 2 * 512], BF16, side="right"))
    silu_s = es.enter_context(nc.sbuf_tensor("silu_s", [128, 512], F32, side="right"))
    gsc_s = es.enter_context(nc.sbuf_tensor("gsc_s", [128, ICN], F32, side="right"))
    usc_s = es.enter_context(nc.sbuf_tensor("usc_s", [128, ICN], F32, side="right"))
    dscb_s = es.enter_context(nc.sbuf_tensor("dscb_s", [128, D], F32, side="right"))
    c_nzp = es.enter_context(nc.sbuf_tensor("c_nzp", [128, 1], F32, side="right"))
    xr_st = es.enter_context(nc.sbuf_tensor("xr_st", [128, 2 * D], F32, side="right"))
    out_st = es.enter_context(nc.sbuf_tensor("out_st", [128, 2 * 512], F32, side="right"))
    pg = es.enter_context(nc.psum_tensor("pg", [128, 512], F32))
    pu = es.enter_context(nc.psum_tensor("pu", [128, 512], F32))
    pds = [es.enter_context(nc.psum_tensor(f"pd{i}", [128, 512], F32)) for i in range(4)]

    sv.wait(SY)
    st.wait(SY)
    sd.inc(SY.dma_start(out=gsc_s[:], in_=gsc[:, :]), 16)
    sd.inc(SY.dma_start(out=usc_s[:], in_=usc[:, :]), 16)
    dcnt = sd.inc(SY.dma_start(out=dscb_s[:], in_=dscb[:, :]), 16)
    sv.wait(V)
    sv.inc(V.memset(c_nzp[:], -128.0), 1)
    sd.wait(S, dcnt)
    sd.wait(V, dcnt)
    sv.wait(S)

    guw_cast = {}
    guw_mm = {}
    wd_cast = {}
    wd_mm = {}
    for ph in range(2):
        tok0 = ph * 512
        st.wait(V)
        for ic in range(ICN):
            slot = ic % 2
            if slot in guw_cast:
                ss.wait(SY, guw_cast[slot])
            s_w[slot].inc(SY.dma_start(out=gu_u8s[:, (2 * slot) * D:(2 * slot + 1) * D],
                                       in_=g_u8[ic, :, :]), 16)
            d2 = SY.dma_start(out=gu_u8s[:, (2 * slot + 1) * D:(2 * slot + 2) * D],
                              in_=u_u8[ic, :, :])
            s_w[slot].inc(d2, 16)
            s_w[slot].wait(S)
            if slot in guw_mm:
                st.wait(S, guw_mm[slot])
            S.activation(out=gu_bf[:, (2 * slot) * D:(2 * slot + 1) * D],
                         in_=gu_u8s[:, (2 * slot) * D:(2 * slot + 1) * D],
                         func=AF.Copy, bias=-128.0, scale=1.0)
            a2 = S.activation(out=gu_bf[:, (2 * slot + 1) * D:(2 * slot + 2) * D],
                              in_=gu_u8s[:, (2 * slot + 1) * D:(2 * slot + 2) * D],
                              func=AF.Copy, bias=-128.0, scale=1.0)
            guw_cast[slot] = ss.inc(a2, 2)
            ss.wait(T, guw_cast[slot])
            ss.wait(T)
            sv.wait(T)
            gb = gu_bf.rearrange("p (s c f) -> p s c f", s=4, c=KC)
            for kc in range(KC):
                mm = T.matmul(pg[:, :], gb[:, 2 * slot, kc, :], lnx23[:, kc, tok0:tok0 + 512],
                              start=(kc == 0), stop=(kc == KC - 1))
            for kc in range(KC):
                mm = T.matmul(pu[:, :], gb[:, 2 * slot + 1, kc, :], lnx23[:, kc, tok0:tok0 + 512],
                              start=(kc == 0), stop=(kc == KC - 1))
            guw_mm[slot] = st.inc(mm, 1)
            st.wait(S, guw_mm[slot])
            act = S.activation(out=silu_s[:, :], in_=pg[:, :], func=AF.Silu,
                               bias=0.0, scale=gsc_s[:, ic:ic + 1])
            scnt = ss.inc(act, 1)
            ss.wait(V, scnt)
            op = V.scalar_tensor_tensor(out=hbuf[:, ic * 512:(ic + 1) * 512],
                                        in0=pu[:, :], scalar=usc_s[:, ic:ic + 1],
                                        in1=silu_s[:, :], op0=ALU.mult, op1=ALU.mult)
            sv.inc(op, 1)
        for dc in range(4):
            for ic in range(ICN):
                slot = ic % 2
                if slot in wd_cast:
                    ss.wait(SY, wd_cast[slot])
                d1 = SY.dma_start(out=wd_u8s[:, slot * 512:(slot + 1) * 512],
                                  in_=d_u8[ic, dc, :, :])
                s_d[slot].inc(d1, 16)
                s_d[slot].wait(S)
                if slot in wd_mm:
                    st.wait(S, wd_mm[slot])
                a1 = S.activation(out=wd_bf[:, slot * 512:(slot + 1) * 512],
                                  in_=wd_u8s[:, slot * 512:(slot + 1) * 512],
                                  func=AF.Copy, bias=-128.0, scale=1.0)
                wd_cast[slot] = ss.inc(a1, 1)
                ss.wait(T, wd_cast[slot])
                if ic == 0:
                    sv.wait(T)
                for tc in range(4):
                    mm = T.matmul(pds[tc][:, :],
                                  hbuf[:, ic * 512 + tc * 128: ic * 512 + tc * 128 + 128],
                                  wd_bf[:, slot * 512:(slot + 1) * 512],
                                  start=(ic == 0), stop=(ic == ICN - 1))
                wd_mm[slot] = st.inc(mm, 1)
            st.wait(V)
            for tc in range(4):
                gtc = ph * 4 + tc
                slot = tc % 2
                sv.wait(SY)
                dcnt = sd.inc(SY.dma_start(out=xr_st[:, slot * D:(slot + 1) * D],
                                           in_=xres_dram[gtc * 128:(gtc + 1) * 128, :]), 16)
                sd.wait(V, dcnt)
                sv.wait(V)
                so_sl[slot].wait(V)
                op = V.tensor_tensor(out=out_st[:, slot * 512:(slot + 1) * 512],
                                     in0=pds[tc][:, :], in1=dscb_s[:, dc * 512:(dc + 1) * 512],
                                     op=ALU.mult)
                sv.inc(op, 1)
                sv.wait(V)
                op = V.tensor_tensor(out=out_st[:, slot * 512:(slot + 1) * 512],
                                     in0=out_st[:, slot * 512:(slot + 1) * 512],
                                     in1=xr_st[:, slot * D + dc * 512: slot * D + dc * 512 + 512],
                                     op=ALU.add)
                ocnt = sv.inc(op, 1)
                sv.wait(SY, ocnt)
                d = SY.dma_start(out=y[gtc * 128:(gtc + 1) * 128, dc * 512:(dc + 1) * 512],
                                 in_=out_st[:, slot * 512:(slot + 1) * 512])
                so_sl[slot].inc(d, 16)
    es.close()
    esL2.close()
    esK.close()

    sd.wait(SY)
    so_sl[0].wait(SY)
    so_sl[1].wait(SY)
    nc.all_engine_barrier()
    return nc


def _prep_inputs(inputs):
    x = np.asarray(inputs["L0_attn_hidden_states"]).reshape(SEQ, D)
    Wq = np.asarray(inputs["L0_attn_Wq"])
    Wk = np.asarray(inputs["L0_attn_Wk"])
    Wv = np.asarray(inputs["L0_attn_Wv"])
    bqv = np.asarray(inputs["L0_attn_bq"])
    bkv = np.asarray(inputs["L0_attn_bk"])
    bvv = np.asarray(inputs["L0_attn_bv"])
    gw = np.asarray(inputs["L0_ffn_gate_w"])
    uw = np.asarray(inputs["L0_ffn_up_w"])
    dw = np.asarray(inputs["L0_ffn_down_w"])
    gs = np.asarray(inputs["L0_ffn_gate_scale"])
    us = np.asarray(inputs["L0_ffn_up_scale"])
    ds = np.asarray(inputs["L0_ffn_down_scale"])

    def tile_w(W):
        t = W.reshape(KC, 128, KC, 128).transpose(2, 0, 1, 3)
        t = t.reshape(KC, 2, 8, 128, 128).transpose(0, 1, 3, 2, 4)
        return np.ascontiguousarray(t.reshape(KC, 2, 128, 1024))

    def u8_of(Wi32):
        return Wi32.view(np.uint8).reshape(*Wi32.shape, 4)[..., 0]

    g8 = u8_of(gw)
    u8a = u8_of(uw)
    d8 = u8_of(dw)
    g_t = g8.T.reshape(KC, 128, ICN, 128).transpose(2, 0, 1, 3)
    g_t = np.ascontiguousarray(g_t.transpose(0, 2, 1, 3).reshape(ICN, 128, D))
    u_t = u8a.T.reshape(KC, 128, ICN, 128).transpose(2, 0, 1, 3)
    u_t = np.ascontiguousarray(u_t.transpose(0, 2, 1, 3).reshape(ICN, 128, D))
    d_t = np.ascontiguousarray(d8.T.reshape(ICN, 128, 4, 512).transpose(0, 2, 1, 3))

    bq_s = np.ascontiguousarray(bqv.reshape(KC, 128).T)
    bk_s = np.ascontiguousarray(bkv.reshape(KC, 128).T)
    bv_s = np.ascontiguousarray(bvv.reshape(KC, 128).T)
    gsc_a = np.ascontiguousarray(gs.reshape(ICN, 128).T)
    usc_a = np.ascontiguousarray(us.reshape(ICN, 128).T)
    dscb_a = np.ascontiguousarray(np.broadcast_to(ds.reshape(1, D), (128, D)))
    wq_t, wk_t, wv_t = tile_w(Wq), tile_w(Wk), tile_w(Wv)

    in_maps = []
    for c in range(NCORES):
        t0 = c * TPC
        hl = x[t0 - HALO: t0] if t0 - HALO >= 0 else x[0:HALO]
        hr = x[t0 + TPC: t0 + TPC + HALO] if t0 + TPC + HALO <= SEQ else x[SEQ - HALO:SEQ]
        in_maps.append({
            "x_main": np.ascontiguousarray(x[t0:t0 + TPC]),
            "x_hl": np.ascontiguousarray(hl),
            "x_hr": np.ascontiguousarray(hr),
            "wq": wq_t, "wk": wk_t, "wv": wv_t,
            "bq": bq_s, "bk": bk_s, "bv": bv_s,
            "g_u8": g_t, "u_u8": u_t, "d_u8": d_t,
            "gsc": gsc_a, "usc": usc_a, "dscb": dscb_a,
            "mask_l": np.full((128, 1), 0.0 if c == 0 else 1.0, np.float32),
            "mask_r": np.full((128, 1), 0.0 if c == NCORES - 1 else 1.0, np.float32),
            "base16": np.full((16, 1), float(t0), np.float32),
        })
    return in_maps


def kernel(**inputs):
    in_maps = _prep_inputs(inputs)
    nc = build_nc()
    res = run_bass_kernel_spmd(nc, in_maps, core_ids=list(range(NCORES)))
    out = np.concatenate([res.results[c]["y"] for c in range(NCORES)], axis=0)
    return out.reshape(1, SEQ, D).astype(np.float32)


# version 09
# speedup vs baseline: 1.0886x; 1.0886x over previous
from contextlib import ExitStack

import numpy as np
import concourse.bass as bass
import concourse.mybir as mybir
from concourse.bass_utils import run_bass_kernel_spmd

F32 = mybir.dt.float32
BF16 = mybir.dt.bfloat16
U8 = mybir.dt.uint8
I32 = mybir.dt.int32
AF = mybir.ActivationFunctionType
ALU = mybir.AluOpType
AX = mybir.AxisListType

NCORES = 8
SEQ = 8192
D = 2048
NH = 16
HD = 128
TPC = 1024
HALO = 64
TW = TPC + 2 * HALO
NB = TPC // 128
KC = D // 128
II = 8192
ICN = II // 128
INV_SQRT_D = float(1.0 / np.sqrt(128.0))
BIG = 1.0e30


class Sem:
    def __init__(self, nc, name):
        self.h = nc.alloc_semaphore(name=name)
        self.n = 0

    def inc(self, instr, k=1):
        instr.then_inc(self.h, k)
        self.n += k
        return self.n

    def wait(self, eng, val=None):
        eng.wait_ge(self.h, self.n if val is None else val)


def build_nc():
    nc = bass.Bass("TRN2", target_bir_lowering=False)

    x_main = nc.dram_tensor("x_main", [TPC, D], F32, kind="ExternalInput").ap()
    x_hl = nc.dram_tensor("x_hl", [HALO, D], F32, kind="ExternalInput").ap()
    x_hr = nc.dram_tensor("x_hr", [HALO, D], F32, kind="ExternalInput").ap()
    wq = nc.dram_tensor("wq", [KC, 2, 128, 1024], F32, kind="ExternalInput").ap()
    wk = nc.dram_tensor("wk", [KC, 2, 128, 1024], F32, kind="ExternalInput").ap()
    wv = nc.dram_tensor("wv", [KC, 2, 128, 1024], F32, kind="ExternalInput").ap()
    bq = nc.dram_tensor("bq", [128, KC], F32, kind="ExternalInput").ap()
    bk = nc.dram_tensor("bk", [128, KC], F32, kind="ExternalInput").ap()
    bv = nc.dram_tensor("bv", [128, KC], F32, kind="ExternalInput").ap()
    g_u8 = nc.dram_tensor("g_u8", [ICN, 128, D], U8, kind="ExternalInput").ap()
    u_u8 = nc.dram_tensor("u_u8", [ICN, 128, D], U8, kind="ExternalInput").ap()
    d_u8 = nc.dram_tensor("d_u8", [ICN, 4, 128, 512], U8, kind="ExternalInput").ap()
    gsc = nc.dram_tensor("gsc", [128, ICN], F32, kind="ExternalInput").ap()
    usc = nc.dram_tensor("usc", [128, ICN], F32, kind="ExternalInput").ap()
    dscb = nc.dram_tensor("dscb", [128, D], F32, kind="ExternalInput").ap()
    mask_l = nc.dram_tensor("mask_l", [128, 1], F32, kind="ExternalInput").ap()
    mask_r = nc.dram_tensor("mask_r", [128, 1], F32, kind="ExternalInput").ap()
    base16 = nc.dram_tensor("base16", [16, 1], F32, kind="ExternalInput").ap()

    y = nc.dram_tensor("y", [TPC, D], F32, kind="ExternalOutput").ap()

    xres_dram = nc.dram_tensor("xres_dram", [TPC, D], F32).ap()
    scr_c = nc.dram_tensor("scr_c", [128, 64], F32)
    scr_i = nc.dram_tensor("scr_i", [1, NH * 64], F32)
    scr_n = nc.dram_tensor("scr_n", [16, 8 * TPC], F32)
    cc_nrm_in = nc.dram_tensor("cc_nrm_in", [16, TPC], F32)
    cc_nrm_out = nc.dram_tensor("cc_nrm_out", [128, TPC], F32)
    cc_qg_in = nc.dram_tensor("cc_qg_in", [128, NH * 64], BF16)
    cc_qg_out = nc.dram_tensor("cc_qg_out", [128, NH * 64], BF16)
    cc_m_in = nc.dram_tensor("cc_m_in", [64, NH], F32)
    cc_m_out = nc.dram_tensor("cc_m_out", [64, NH], F32)
    cc_ol_in = nc.dram_tensor("cc_ol_in", [64, NH * HD + NH], F32)
    cc_ol_out = nc.dram_tensor("cc_ol_out", [64, NH * HD + NH], F32)
    RG = [list(range(NCORES))]

    nc.gpsimd.dma_reset(range(155, 256))
    nc.gpsimd.sem_clear(range(155, 256))
    nc.all_engine_barrier()

    sv = Sem(nc, "sv")
    ss = Sem(nc, "ss")
    st = Sem(nc, "st")
    sd = Sem(nc, "sd")
    sq = Sem(nc, "sq")
    sg = Sem(nc, "sg")
    scc = Sem(nc, "scc")
    s_w = [Sem(nc, "sw0"), Sem(nc, "sw1")]
    s_d = [Sem(nc, "sd0"), Sem(nc, "sd1")]
    so_sl = [Sem(nc, "so0"), Sem(nc, "so1")]

    V, S, T, G, SY = nc.vector, nc.scalar, nc.tensor, nc.gpsimd, nc.sync

    esK = ExitStack()

    def sbK(name, shape, dt):
        return esK.enter_context(nc.sbuf_tensor(name, shape, dt))

    ident_bf = sbK("ident_bf", [128, 128], BF16)
    ident_f32 = sbK("ident_f32", [128, 128], F32)
    iota_pf = sbK("iota_pf", [128, 128], F32)
    c_invsqd = sbK("c_invsqd", [128, 1], F32)
    c_eps = sbK("c_eps", [128, 1], F32)
    c_nmean = sbK("c_nmean", [128, 1], F32)
    ones_f32 = sbK("ones_f32", [128, 1], F32)
    ones_bf = sbK("ones_bf", [128, 1], BF16)
    onesr_f32 = sbK("onesr_f32", [1, 128], F32)
    bq_s = sbK("bq_s", [128, KC], F32)
    bk_s = sbK("bk_s", [128, KC], F32)
    bv_s = sbK("bv_s", [128, KC], F32)
    maskl_s = sbK("maskl_s", [128, 1], F32)
    maskr_s = sbK("maskr_s", [128, 1], F32)
    base16_s = sbK("base16_s", [16, 1], F32)
    ln_s = sbK("ln_s", [128, 10], F32)
    ln_q = sbK("ln_q", [128, 10], F32)
    ln_v = sbK("ln_v", [128, 10], F32)
    ln_sd = sbK("ln_sd", [128, 10], F32)
    ln_r = sbK("ln_r", [128, 10], F32)
    ln_m = sbK("ln_m", [128, 10], F32)

    iop = G.iota(iota_pf[:, :], pattern=[[-1, 128]], base=0, channel_multiplier=1,
                 allow_small_or_imprecise_dtypes=True)
    sg.inc(iop, 1)
    s_d[0].inc(SY.dma_start(out=bq_s[:], in_=bq[:, :]), 16)
    s_d[0].inc(SY.dma_start(out=bk_s[:], in_=bk[:, :]), 16)
    s_d[0].inc(SY.dma_start(out=bv_s[:], in_=bv[:, :]), 16)
    s_d[0].inc(SY.dma_start(out=maskl_s[:], in_=mask_l[:, :]), 16)
    s_d[0].inc(SY.dma_start(out=maskr_s[:], in_=mask_r[:, :]), 16)
    s_d[0].inc(SY.dma_start(out=base16_s[:], in_=base16[:, :]), 16)
    V.memset(c_invsqd[:], INV_SQRT_D)
    V.memset(c_eps[:], 1.0e-5)
    V.memset(c_nmean[:], -1.0 / float(D))
    V.memset(ones_f32[:], 1.0)
    V.memset(ones_bf[:], 1.0)
    sv.inc(V.memset(onesr_f32[:], 1.0), 1)
    sg.wait(V)
    V.tensor_single_scalar(out=ident_bf[:], in_=iota_pf[:], scalar=0.0, op=ALU.is_equal)
    sv.inc(V.tensor_single_scalar(out=ident_f32[:], in_=iota_pf[:], scalar=0.0, op=ALU.is_equal), 1)
    s_d[0].wait(V)

    esR1 = ExitStack()
    ctx = esR1.enter_context(nc.sbuf_tensor("ctx", [128, NB * D], BF16, side="right"))
    esR2 = ExitStack()
    q_fm = esR2.enter_context(nc.sbuf_tensor("q_fm", [128, NH * TPC], BF16, side="right"))
    k_fm = esR2.enter_context(nc.sbuf_tensor("k_fm", [128, NH * TW], BF16, side="right"))
    v_tm = esR2.enter_context(nc.sbuf_tensor("v_tm", [128, 9 * D], BF16, side="right"))
    q3 = q_fm.rearrange("p (h t) -> p h t", h=NH)
    k3 = k_fm.rearrange("p (h t) -> p h t", h=NH)
    v3 = v_tm.rearrange("p (j d) -> p j d", j=9)

    esLN = ExitStack()
    lnx = esLN.enter_context(nc.sbuf_tensor("lnx", [128, KC * TW], BF16, side="right"))
    lnx3 = lnx.rearrange("p (c t) -> p c t", c=KC)

    es = ExitStack()
    xstage = es.enter_context(nc.sbuf_tensor("xstage", [128, D], F32, side="right"))
    lntm = es.enter_context(nc.sbuf_tensor("lntm", [128, D], BF16, side="right"))
    sqd = es.enter_context(nc.sbuf_tensor("sqd", [128, D], F32, side="right"))
    pt_ln = es.enter_context(nc.psum_tensor("pt_ln", [128, 2048], BF16))

    def ln_tile(i, nrow, src_ap, out3, dst_col0, xs, sq_t, lntm_t, pt):
        sv.wait(SY)
        dcnt = sd.inc(SY.dma_start(out=xs, in_=src_ap), 16)
        sd.wait(V, dcnt)
        V.reduce_sum(out=ln_s[0:nrow, i:i + 1], in_=xs, axis=AX.X)
        op = V.scalar_tensor_tensor(out=sq_t, in0=xs, scalar=1.0, in1=xs,
                                    op0=ALU.mult, op1=ALU.mult,
                                    accum_out=ln_q[0:nrow, i:i + 1])
        sv.inc(op, 1)
        sv.wait(V)
        op = V.tensor_scalar(out=ln_m[0:nrow, i:i + 1], in0=ln_s[0:nrow, i:i + 1],
                             scalar1=c_nmean[0:nrow, :], scalar2=None, op0=ALU.mult)
        sv.inc(op, 1)
        sv.wait(V)
        op = V.tensor_mul(out=ln_sd[0:nrow, i:i + 1], in0=ln_m[0:nrow, i:i + 1],
                          in1=ln_m[0:nrow, i:i + 1])
        sv.inc(op, 1)
        sv.wait(V)
        op = V.scalar_tensor_tensor(out=ln_v[0:nrow, i:i + 1], in0=ln_q[0:nrow, i:i + 1],
                                    scalar=1.0 / float(D), in1=ln_sd[0:nrow, i:i + 1],
                                    op0=ALU.mult, op1=ALU.subtract)
        sv.inc(op, 1)
        sv.wait(S)
        op = S.activation(out=ln_sd[0:nrow, i:i + 1], in_=ln_v[0:nrow, i:i + 1],
                          func=AF.Sqrt, bias=c_eps[0:nrow, :], scale=1.0)
        ss.inc(op, 1)
        ss.wait(V)
        op = V.reciprocal(out=ln_r[0:nrow, i:i + 1], in_=ln_sd[0:nrow, i:i + 1])
        sv.inc(op, 1)
        sv.wait(V)
        op = V.tensor_scalar(out=lntm_t, in0=xs, scalar1=ln_m[0:nrow, i:i + 1],
                             scalar2=ln_r[0:nrow, i:i + 1], op0=ALU.add, op1=ALU.mult)
        sv.inc(op, 1)
        sv.wait(T)
        for kc in range(KC):
            mm = T.transpose(pt[:, kc * 128: kc * 128 + nrow],
                             lntm_t[0:nrow, kc * 128:(kc + 1) * 128] if False
                             else lntm_t.tensor[0:nrow, kc * 128:(kc + 1) * 128]
                             if hasattr(lntm_t, "tensor") else lntm_t,
                             ident_bf[:])
        st.inc(mm, 1)
        st.wait(V)
        out_ap = out3[:, :, dst_col0:dst_col0 + nrow]
        in_ap = pt.rearrange("p (c t) -> p c t", c=KC)[:, :, 0:nrow]
        sv.inc(V.tensor_copy(out=out_ap, in_=in_ap), 1)

    def ln_tile2(i, nrow, src_ap, out3, dst_col0):
        xs = xstage[0:nrow, 0:D]
        sv.wait(SY)
        dcnt = sd.inc(SY.dma_start(out=xs, in_=src_ap), 16)
        sd.wait(V, dcnt)
        V.reduce_sum(out=ln_s[0:nrow, i:i + 1], in_=xs, axis=AX.X)
        op = V.scalar_tensor_tensor(out=sqd[0:nrow, :], in0=xs, scalar=1.0, in1=xs,
                                    op0=ALU.mult, op1=ALU.mult,
                                    accum_out=ln_q[0:nrow, i:i + 1])
        sv.inc(op, 1)
        sv.wait(V)
        op = V.tensor_scalar(out=ln_m[0:nrow, i:i + 1], in0=ln_s[0:nrow, i:i + 1],
                             scalar1=c_nmean[0:nrow, :], scalar2=None, op0=ALU.mult)
        sv.inc(op, 1)
        sv.wait(V)
        op = V.tensor_mul(out=ln_sd[0:nrow, i:i + 1], in0=ln_m[0:nrow, i:i + 1],
                          in1=ln_m[0:nrow, i:i + 1])
        sv.inc(op, 1)
        sv.wait(V)
        op = V.scalar_tensor_tensor(out=ln_v[0:nrow, i:i + 1], in0=ln_q[0:nrow, i:i + 1],
                                    scalar=1.0 / float(D), in1=ln_sd[0:nrow, i:i + 1],
                                    op0=ALU.mult, op1=ALU.subtract)
        sv.inc(op, 1)
        sv.wait(S)
        op = S.activation(out=ln_sd[0:nrow, i:i + 1], in_=ln_v[0:nrow, i:i + 1],
                          func=AF.Sqrt, bias=c_eps[0:nrow, :], scale=1.0)
        ss.inc(op, 1)
        ss.wait(V)
        op = V.reciprocal(out=ln_r[0:nrow, i:i + 1], in_=ln_sd[0:nrow, i:i + 1])
        sv.inc(op, 1)
        sv.wait(V)
        op = V.tensor_scalar(out=lntm[0:nrow, 0:D], in0=xs, scalar1=ln_m[0:nrow, i:i + 1],
                             scalar2=ln_r[0:nrow, i:i + 1], op0=ALU.add, op1=ALU.mult)
        sv.inc(op, 1)
        sv.wait(T)
        for kc in range(KC):
            mm = T.transpose(pt_ln[:, kc * 128: kc * 128 + nrow],
                             lntm[0:nrow, kc * 128:(kc + 1) * 128],
                             ident_bf[0:nrow, 0:nrow])
        st.inc(mm, 1)
        st.wait(V)
        out_ap = out3[:, :, dst_col0:dst_col0 + nrow]
        in_ap = pt_ln.rearrange("p (c t) -> p c t", c=KC)[:, :, 0:nrow]
        sv.inc(V.tensor_copy(out=out_ap, in_=in_ap), 1)

    ln_tile2(8, HALO, x_hl[:, :], lnx3, 0)
    ln_tile2(9, HALO, x_hr[:, :], lnx3, HALO + TPC)
    for i in range(NB):
        ln_tile2(i, 128, x_main[i * 128:(i + 1) * 128, :], lnx3, HALO + i * 128)
    es.close()

    es = ExitStack()
    wstage = es.enter_context(nc.sbuf_tensor("wstage", [128, 2 * 1024], F32, side="right"))
    wbf = es.enter_context(nc.sbuf_tensor("wbf", [128, 2 * 1024], BF16, side="right"))
    vfm_s = es.enter_context(nc.sbuf_tensor("vfm_s", [128, TW], BF16, side="right"))
    sqb = es.enter_context(nc.sbuf_tensor("sqb", [128, 512], F32, side="right"))
    norms16 = es.enter_context(nc.sbuf_tensor("norms16", [16, TPC], F32, side="right"))
    pq0 = es.enter_context(nc.psum_tensor("pq0", [128, 512], F32))
    pq1 = es.enter_context(nc.psum_tensor("pq1", [128, 512], F32))
    pk2 = es.enter_context(nc.psum_tensor("pk2", [128, 128], F32))
    pt_v = es.enter_context(nc.psum_tensor("pt_v", [128, 9 * 128], BF16))

    wbf3 = wbf.rearrange("p (s j f) -> p s j f", s=2, j=8)
    slot_cast = {}
    slot_mm = {}
    gslot = [0]

    def stage_half(src, mc, wh, cast_eng, cast_sem):
        slot = gslot[0] % 2
        gslot[0] += 1
        if slot in slot_cast:
            sem_prev, ccnt_prev = slot_cast[slot]
            sem_prev.wait(SY, ccnt_prev)
        s_w[slot].inc(SY.dma_start(out=wstage[:, slot * 1024:(slot + 1) * 1024],
                                   in_=src[mc, wh, :, :]), 16)
        s_w[slot].wait(cast_eng)
        if slot in slot_mm:
            st.wait(cast_eng, slot_mm[slot])
        if cast_eng is S:
            op = cast_eng.copy(out=wbf[:, slot * 1024:(slot + 1) * 1024],
                               in_=wstage[:, slot * 1024:(slot + 1) * 1024])
        else:
            op = cast_eng.tensor_copy(out=wbf[:, slot * 1024:(slot + 1) * 1024],
                                      in_=wstage[:, slot * 1024:(slot + 1) * 1024])
        ccnt = cast_sem.inc(op, 1)
        slot_cast[slot] = (cast_sem, ccnt)
        return slot, ccnt

    sv.wait(SY)
    for mc in range(KC):
        sv.wait(T)
        for wh in range(2):
            slot, ccnt = stage_half(wq, mc, wh, V, sv)
            sv.wait(T, ccnt)
            for half in range(2):
                ps = pq0 if half == 0 else pq1
                for j in range(8):
                    kc = wh * 8 + j
                    mm = T.matmul(ps[:, 0:512], wbf3[:, slot, j, :],
                                  lnx3[:, kc, HALO + half * 512: HALO + half * 512 + 512],
                                  start=(kc == 0), stop=(kc == KC - 1))
            st.inc(mm, 1)
            slot_mm[slot] = st.n
        st.wait(V)
        for half in range(2):
            ps = pq0 if half == 0 else pq1
            op = V.tensor_scalar(out=q3[:, mc, half * 512: half * 512 + 512],
                                 in0=ps[:, 0:512], scalar1=bq_s[:, mc:mc + 1],
                                 scalar2=None, op0=ALU.add)
            sv.inc(op, 1)

    for mc in range(KC):
        sv.wait(T)
        for wh in range(2):
            slot, ccnt = stage_half(wk, mc, wh, S, ss)
            ss.wait(T, ccnt)
            for (ps, off, nlen) in ((pq0, 0, 512), (pq1, 512, 512), (pk2, 1024, 128)):
                for j in range(8):
                    kc = wh * 8 + j
                    mm = T.matmul(ps[:, 0:nlen], wbf3[:, slot, j, :],
                                  lnx3[:, kc, off:off + nlen],
                                  start=(kc == 0), stop=(kc == KC - 1))
            st.inc(mm, 1)
            slot_mm[slot] = st.n
        st.wait(V)
        for (ps, off, nlen) in ((pq0, 0, 512), (pq1, 512, 512), (pk2, 1024, 128)):
            op = V.tensor_scalar(out=k3[:, mc, off:off + nlen],
                                 in0=ps[:, 0:nlen], scalar1=bk_s[:, mc:mc + 1],
                                 scalar2=None, op0=ALU.add)
            sv.inc(op, 1)

    for mc in range(KC):
        sv.wait(T)
        for wh in range(2):
            slot, ccnt = stage_half(wv, mc, wh, S, ss)
            ss.wait(T, ccnt)
            for (ps, off, nlen) in ((pq0, 0, 512), (pq1, 512, 512), (pk2, 1024, 128)):
                for j in range(8):
                    kc = wh * 8 + j
                    mm = T.matmul(ps[:, 0:nlen], wbf3[:, slot, j, :],
                                  lnx3[:, kc, off:off + nlen],
                                  start=(kc == 0), stop=(kc == KC - 1))
            st.inc(mm, 1)
            slot_mm[slot] = st.n
        st.wait(V)
        for (ps, off, nlen) in ((pq0, 0, 512), (pq1, 512, 512), (pk2, 1024, 128)):
            op = V.tensor_scalar(out=vfm_s[:, off:off + nlen],
                                 in0=ps[:, 0:nlen], scalar1=bv_s[:, mc:mc + 1],
                                 scalar2=None, op0=ALU.add)
            sv.inc(op, 1)
        sv.wait(T)
        for j in range(9):
            mm = T.transpose(pt_v[:, j * 128:(j + 1) * 128],
                             vfm_s[:, j * 128:(j + 1) * 128], ident_bf[:])
        st.inc(mm, 1)
        st.wait(V)
        sv.inc(V.tensor_copy(out=v3[:, :, mc * 128:(mc + 1) * 128],
                             in_=pt_v.rearrange("p (j d) -> p j d", j=9)), 1)

    es2n = ExitStack()
    ohall_i = es2n.enter_context(nc.sbuf_tensor("ohall_i", [128, 256], F32, side="right"))
    ohall = es2n.enter_context(nc.sbuf_tensor("ohall", [128, 256], F32, side="right"))
    pnrm = es2n.enter_context(nc.psum_tensor("pnrm", [16, 512], F32))
    iop = G.iota(ohall_i[:, :], pattern=[[-1, 16], [1, 16]], base=0,
                 channel_multiplier=0, allow_small_or_imprecise_dtypes=True)
    sg.inc(iop, 1)
    sg.wait(V)
    op = V.tensor_single_scalar(out=ohall[:, :], in_=ohall_i[:, :], scalar=0.0,
                                op=ALU.is_equal)
    sv.inc(op, 1)
    for half in range(2):
        for h in range(NH):
            sv.wait(V)
            st.wait(V)
            op = V.tensor_mul(out=sqb[:, 0:512],
                              in0=q3[:, h, half * 512: half * 512 + 512],
                              in1=q3[:, h, half * 512: half * 512 + 512])
            sv.inc(op, 1)
            sv.wait(T)
            mm = T.matmul(pnrm[0:16, 0:512], ohall[:, h * 16:(h + 1) * 16],
                          sqb[:, 0:512], start=(h == 0), stop=(h == NH - 1))
            st.inc(mm, 1)
        st.wait(V)
        op = V.tensor_copy(out=norms16[:, half * 512: half * 512 + 512],
                           in_=pnrm[0:16, 0:512])
        sv.inc(op, 1)
        sv.wait(T)
    es2n.close()
    sv.wait(SY)
    sd.inc(SY.dma_start(out=cc_nrm_in.ap()[:, :], in_=norms16[:, :]), 16)
    sd.wait(G)
    cc = G.collective_compute("AllGather", ALU.bypass, replica_groups=RG,
                              ins=[cc_nrm_in.ap().opt()], outs=[cc_nrm_out.ap().opt()])
    scc.inc(cc, 1)
    es.close()
    esLN.close()

    es = ExitStack()
    e_loc = es.enter_context(nc.sbuf_tensor("e_loc", [128, 2 * 1024], BF16, side="right"))
    pr_loc = es.enter_context(nc.sbuf_tensor("pr_loc", [128, 2 * 1024], BF16, side="right"))
    pT_loc = es.enter_context(nc.sbuf_tensor("pT_loc", [128, 2 * 1024], BF16, side="right"))
    m4 = es.enter_context(nc.sbuf_tensor("m4", [128, 8], F32, side="right"))
    nm4 = es.enter_context(nc.sbuf_tensor("nm4", [128, 8], F32, side="right"))
    l4 = es.enter_context(nc.sbuf_tensor("l4", [128, 8], F32, side="right"))
    r4 = es.enter_context(nc.sbuf_tensor("r4", [128, 8], F32, side="right"))
    psc0 = es.enter_context(nc.psum_tensor("psc0", [128, 1024], F32))
    psc1 = es.enter_context(nc.psum_tensor("psc1", [128, 1024], F32))
    ppt0 = es.enter_context(nc.psum_tensor("ppt0", [128, 1024], BF16))
    ppt1 = es.enter_context(nc.psum_tensor("ppt1", [128, 1024], BF16))
    ppv0 = es.enter_context(nc.psum_tensor("ppv0", [128, 512], F32))
    ppv1 = es.enter_context(nc.psum_tensor("ppv1", [128, 512], F32))

    evac_cnt = {}
    pv_cnt = {}
    exp_cnt = {}
    prb_cnt = {}
    red_cnt = {}
    for b in range(NB):
        for g in range(4):
            par = (b * 4 + g) % 2
            psc = psc0 if par == 0 else psc1
            ppt = ppt0 if par == 0 else ppt1
            ppv = ppv0 if par == 0 else ppv1
            eb = e_loc[:, par * 1024:(par + 1) * 1024]
            prb = pr_loc[:, par * 1024:(par + 1) * 1024]
            ptb = pT_loc[:, par * 1024:(par + 1) * 1024]
            m4b = m4[:, par * 4:(par + 1) * 4]
            nm4b = nm4[:, par * 4:(par + 1) * 4]
            l4b = l4[:, par * 4:(par + 1) * 4]
            r4b = r4[:, par * 4:(par + 1) * 4]
            heads = [4 * g + i for i in range(4)]
            if par in exp_cnt:
                ss.wait(T, exp_cnt[par])
            if par in red_cnt:
                sv.wait(T, red_cnt[par])
            for i, h in enumerate(heads):
                mm = T.matmul(psc[:, i * 256:(i + 1) * 256],
                              q3[:, h, b * 128:(b + 1) * 128],
                              k3[:, h, b * 128: b * 128 + 256],
                              start=True, stop=True)
            qk_cnt = st.inc(mm, 1)
            st.wait(V, qk_cnt)
            op = V.reduce_max(out=m4b, in_=psc.rearrange("p (i k) -> p i k", i=4), axis=AX.X)
            sv.inc(op, 1)
            sv.wait(V)
            op = V.tensor_scalar(out=nm4b, in0=m4b, scalar1=-INV_SQRT_D,
                                 scalar2=None, op0=ALU.mult)
            nm_cnt = sv.inc(op, 1)
            red_cnt[par] = nm_cnt
            sv.wait(S, nm_cnt)
            edge = (b == 0) or (b == NB - 1)
            if par in prb_cnt:
                sv.wait(S, prb_cnt[par])
            for i, h in enumerate(heads):
                act = S.activation(out=eb[:, i * 256:(i + 1) * 256],
                                   in_=psc[:, i * 256:(i + 1) * 256],
                                   func=AF.Exp, bias=nm4b[:, i:i + 1],
                                   scale=c_invsqd[:, 0:1],
                                   accum_out=(None if edge else l4b[:, i:i + 1]))
            exp_cnt[par] = ss.inc(act, 1)
            ss.wait(V, exp_cnt[par])
            if b == 0:
                eb4 = eb.rearrange("p (i k) -> p i k", i=4)
                op = V.tensor_scalar(out=eb4[:, :, 0:64], in0=eb4[:, :, 0:64],
                                     scalar1=maskl_s[:, 0:1], scalar2=None, op0=ALU.mult)
                sv.inc(op, 1)
                sv.wait(V)
            if b == NB - 1:
                eb4 = eb.rearrange("p (i k) -> p i k", i=4)
                op = V.tensor_scalar(out=eb4[:, :, 192:256], in0=eb4[:, :, 192:256],
                                     scalar1=maskr_s[:, 0:1], scalar2=None, op0=ALU.mult)
                sv.inc(op, 1)
                sv.wait(V)
            if edge:
                op = V.reduce_sum(out=l4b, in_=eb.rearrange("p (i k) -> p i k", i=4),
                                  axis=AX.X)
                sv.inc(op, 1)
                sv.wait(V)
            op = V.reciprocal(out=r4b, in_=l4b)
            sv.inc(op, 1)
            sv.wait(V)
            if par in pv_cnt:
                st.wait(V, pv_cnt[par])
            op = V.tensor_tensor(out=prb.rearrange("p (i k) -> p i k", i=4),
                                 in0=eb.rearrange("p (i k) -> p i k", i=4),
                                 in1=r4b.unsqueeze(2).broadcast_to([128, 4, 256]),
                                 op=ALU.mult)
            prb_cnt[par] = sv.inc(op, 1)
            sv.wait(T, prb_cnt[par])
            if par in evac_cnt:
                sv.wait(T, evac_cnt[par])
            for i in range(4):
                for c in range(2):
                    mm = T.transpose(ppt[:, (i * 2 + c) * 128:(i * 2 + c + 1) * 128],
                                     prb[:, i * 256 + c * 128: i * 256 + c * 128 + 128],
                                     ident_bf[:])
            tr_cnt = st.inc(mm, 1)
            st.wait(V, tr_cnt)
            op = V.tensor_copy(out=ptb, in_=ppt[:, :])
            ptb_cnt = sv.inc(op, 1)
            sv.wait(T, ptb_cnt)
            for i, h in enumerate(heads):
                for c in range(2):
                    mm = T.matmul(ppv[:, i * 128:(i + 1) * 128],
                                  ptb[:, (i * 2 + c) * 128:(i * 2 + c + 1) * 128],
                                  v3[:, b + c, h * 128:(h + 1) * 128],
                                  start=(c == 0), stop=(c == 1))
            pv_cnt[par] = st.inc(mm, 1)
            st.wait(V, pv_cnt[par])
            op = V.tensor_copy(out=ctx[:, b * D + g * 512: b * D + (g + 1) * 512],
                               in_=ppv[:, 0:512])
            evac_cnt[par] = sv.inc(op, 1)
    es.close()

    esP = ExitStack()

    def sbP(name, shape, dt):
        return esP.enter_context(nc.sbuf_tensor(name, shape, dt))

    gidxn = sbP("gidxn", [128, TPC], F32)
    t_pb = sbP("t_pb", [128, 512], F32)
    iota_t = sbP("iota_t", [64, TPC], F32)
    bc16i = sbP("bc16i", [16, 128], F32)
    bc16 = sbP("bc16", [16, 128], F32)
    iota16 = sbP("iota16", [16, 1], F32)
    pm16 = esP.enter_context(nc.psum_tensor("pm16", [128, 1], F32))
    normsG = sbP("normsG", [128, TPC], F32)
    normsW = sbP("normsW", [128, TPC], F32)
    maskT = sbP("maskT", [128, TPC], U8)
    candA = sbP("candA", [128, 64], F32)
    cand2 = sbP("cand2", [16, 512], F32)
    sort2 = sbP("sort2", [16, 64], F32)
    thr = sbP("thr", [128, 1], F32)
    jrow = sbP("jrow", [128, 5], F32)
    i32a = sbP("i32a", [128, 1], I32)
    idx_g = sbP("idx_g", [16, 64], F32)
    idx_l = sbP("idx_l", [16, 64], F32)
    idxT = sbP("idxT", [64, 16], F32)
    idxflat = sbP("idxflat", [1, NH * 64], F32)
    idx_bc = sbP("idx_bc", [128, NH * 64], F32)
    qg_sb = sbP("qg_sb", [128, NH * 64], BF16)
    q_tmh = sbP("q_tmh", [128, TPC], BF16)
    oh_g = sbP("oh_g", [128, 512], BF16)
    m_all = sbP("m_all", [64, NH], F32)
    nmG = sbP("nmG", [64, NH], F32)
    rloc = sbP("rloc", [64, NH], F32)
    e_g = sbP("e_g", [64, TW], BF16)
    eT_g = sbP("eT_g", [128, 9 * 64], BF16)
    ol = sbP("ol", [64, NH * HD + NH], F32)
    gctx = sbP("gctx", [64, NH * HD], BF16)
    oh_s = sbP("oh_s", [64, 128], BF16)
    cnt_sb = sbP("cnt_sb", [128, NH], F32)
    zq = sbP("zq", [128, NH], F32)
    rq = sbP("rq", [128, NH], F32)
    t1_sb = sbP("t1_sb", [128, D], BF16)

    G.iota(gidxn[:, :], pattern=[[-1, TPC]], base=0, channel_multiplier=0,
           allow_small_or_imprecise_dtypes=True)
    G.iota(t_pb[:, :], pattern=[[128, 8], [0, 64]], base=0, channel_multiplier=1,
           allow_small_or_imprecise_dtypes=True)
    G.iota(iota_t[:, :], pattern=[[1, TPC]], base=0, channel_multiplier=0,
           allow_small_or_imprecise_dtypes=True)
    G.iota(bc16i[:, :], pattern=[[-1, 16], [0, 8]], base=0, channel_multiplier=1,
           allow_small_or_imprecise_dtypes=True)
    G.iota(iota16[:, :], pattern=[[0, 1]], base=0, channel_multiplier=1,
           allow_small_or_imprecise_dtypes=True)
    iop = G.iota(jrow[:, 0:1], pattern=[[0, 1]], base=0, channel_multiplier=1,
                 allow_small_or_imprecise_dtypes=True)
    sg.inc(iop, 1)
    sg.wait(V)
    sg.wait(T)
    sv.inc(V.tensor_single_scalar(out=bc16[:, :], in_=bc16i[:, :], scalar=0.0,
                                  op=ALU.is_equal), 1)
    sv.wait(T)
    mm = T.matmul(pm16[:, 0:1], bc16[:, :], iota16[:, :], start=True, stop=True)
    st.inc(mm, 1)
    st.wait(V)
    sv.inc(V.tensor_copy(out=jrow[:, 2:3], in_=pm16[:, 0:1]), 1)
    sv.wait(V)
    op = V.tensor_scalar(out=jrow[:, 3:4], in0=jrow[:, 2:3], scalar1=8192.0,
                         scalar2=None, op0=ALU.mult)
    sv.inc(op, 1)
    sv.wait(V)
    op = V.scalar_tensor_tensor(out=jrow[:, 4:5], in0=jrow[:, 0:1], scalar=-1024.0,
                                in1=jrow[:, 3:4], op0=ALU.mult, op1=ALU.add)
    sv.inc(op, 1)
    sv.wait(V)
    op = V.tensor_scalar(out=gidxn[:, :], in0=gidxn[:, :], scalar1=jrow[:, 4:5],
                         scalar2=None, op0=ALU.add)
    sv.inc(op, 1)

    scc.wait(G)
    sq.inc(G.dma_start(out=scr_n.ap().rearrange("h (j t) -> j h t", j=8),
                       in_=cc_nrm_out.ap().rearrange("(j h) t -> j h t", h=16)), 16)
    sq.wait(G)
    sq.inc(G.dma_start(out=normsG[:, :],
                       in_=scr_n.ap().rearrange("h (j t) -> (h j) t", j=8)), 16)
    sq.wait(V)
    sv.inc(V.tensor_copy(out=normsW[:, :], in_=normsG[:, :]), 1)
    sv.wait(V)

    def extract64(data, out_tile):
        for r in range(8):
            op = V.max(out=out_tile[:, r * 8:(r + 1) * 8], in_=data)
            sv.inc(op, 1)
            sv.wait(V)
            if r < 7:
                op = V.match_replace(out=data, in_to_replace=out_tile[:, r * 8:(r + 1) * 8],
                                     in_values=data, imm_value=-BIG)
                sv.inc(op, 1)
                sv.wait(V)

    def cand_regroup():
        sv.wait(G)
        sq.inc(G.dma_start(out=scr_c.ap()[:, :], in_=candA[:, :]), 16)
        sq.wait(G)
        sq.inc(G.dma_start(out=cand2[:, :],
                           in_=scr_c.ap().rearrange("(h j) c -> h (j c)", h=16)), 16)
        sq.wait(V)

    extract64(normsW[:, :], candA)
    cand_regroup()
    extract64(cand2[:, :], sort2)
    sv.wait(T)
    mm = T.matmul(pm16[:, 0:1], bc16[:, :], sort2[:, 61:62], start=True, stop=True)
    st.inc(mm, 1)
    st.wait(V)
    sv.inc(V.tensor_copy(out=thr[:, :], in_=pm16[:, 0:1]), 1)
    sv.wait(V)
    op = V.tensor_scalar(out=maskT[:, :], in0=normsG[:, :], scalar1=thr[:, 0:1],
                         scalar2=None, op0=ALU.is_ge)
    sv.inc(op, 1)
    sv.inc(V.memset(normsW[:, :], -BIG), 1)
    sv.wait(V)
    sv.inc(V.copy_predicated(out=normsW[:, :], mask=maskT[:, :], data=gidxn[:, :]), 1)
    sv.wait(V)
    extract64(normsW[:, :], candA)
    cand_regroup()
    extract64(cand2[:, :], sort2)
    sv.wait(V)
    sv.inc(V.memset(idx_g[:, 0:1], 0.0), 1)
    sv.inc(V.memset(idx_g[:, 63:64], 8191.0), 1)
    op = V.tensor_scalar(out=idx_g[:, 1:63], in0=sort2[:, 0:62], scalar1=-1.0,
                         scalar2=None, op0=ALU.mult)
    sv.inc(op, 1)
    sv.wait(V)
    op = V.tensor_scalar(out=idx_l[:, :], in0=idx_g[:, :], scalar1=base16_s[:, 0:1],
                         scalar2=None, op0=ALU.subtract)
    sv.inc(op, 1)
    sv.wait(T)
    sv.wait(G)

    es2 = ExitStack()
    ptf = es2.enter_context(nc.psum_tensor("ptf", [128, 512], F32))
    mm = T.transpose(ptf[0:64, 0:16], idx_l[:, :], ident_f32[0:16, 0:16])
    st.inc(mm, 1)
    st.wait(V)
    sv.inc(V.tensor_copy(out=idxT[:, :], in_=ptf[0:64, 0:16]), 1)
    sq.inc(G.dma_start(out=scr_i.ap().rearrange("o (h j) -> (o h) j", h=16),
                       in_=idx_l[:, :]), 16)
    sq.wait(G)
    sq.inc(G.dma_start(out=idxflat[0:1, :], in_=scr_i.ap()), 16)
    sq.wait(T)
    sv.wait(T)
    for half in range(2):
        mm = T.matmul(ptf[:, 0:512], onesr_f32[0:1, :],
                      idxflat[0:1, half * 512: half * 512 + 512], start=True, stop=True)
        st.inc(mm, 1)
        st.wait(V)
        op = V.tensor_copy(out=idx_bc[:, half * 512: half * 512 + 512], in_=ptf[:, 0:512])
        sv.inc(op, 1)
        sv.wait(T)
    es2.close()

    es2 = ExitStack()
    pqt = es2.enter_context(nc.psum_tensor("pqt", [128, 1024], BF16))
    pqg0 = es2.enter_context(nc.psum_tensor("pqg0", [128, 512], F32))
    pqg1 = es2.enter_context(nc.psum_tensor("pqg1", [128, 512], F32))
    for h in range(NH):
        sv.wait(T)
        for j in range(8):
            mm = T.transpose(pqt[:, j * 128:(j + 1) * 128],
                             q3[:, h, j * 128:(j + 1) * 128], ident_bf[:])
        st.inc(mm, 1)
        st.wait(V)
        sv.inc(V.tensor_copy(out=q_tmh[:, :], in_=pqt[:, :]), 1)
        in0 = idx_bc.ap()[:, h * 64:(h + 1) * 64].unsqueeze(1).broadcast_to([128, 8, 64])
        in1 = t_pb.rearrange("p (b j) -> p b j", b=8)
        op = V.tensor_tensor(out=oh_g.rearrange("p (b j) -> p b j", b=8),
                             in0=in0, in1=in1, op=ALU.is_equal)
        ohcnt = sv.inc(op, 1)
        sv.wait(T, ohcnt)
        pq = pqg0 if h < 8 else pqg1
        off = (h % 8) * 64
        for j in range(8):
            mm = T.matmul(pq[:, off:off + 64], q_tmh[:, j * 128:(j + 1) * 128],
                          oh_g[:, j * 64:(j + 1) * 64],
                          start=(j == 0), stop=(j == 7))
        st.inc(mm, 1)
    st.wait(V)
    sv.inc(V.tensor_copy(out=qg_sb[:, 0:512], in_=pqg0[:, :]), 1)
    sv.inc(V.tensor_copy(out=qg_sb[:, 512:1024], in_=pqg1[:, :]), 1)
    es2.close()
    sv.wait(SY)
    sd.inc(SY.dma_start(out=cc_qg_in.ap()[:, :], in_=qg_sb[:, :]), 16)
    sd.wait(G)
    cc = G.collective_compute("AllReduce", ALU.add, replica_groups=RG,
                              ins=[cc_qg_in.ap().opt()], outs=[cc_qg_out.ap().opt()])
    scc.inc(cc, 1)
    scc.wait(G)
    sq.inc(G.dma_start(out=qg_sb[:, :], in_=cc_qg_out.ap()[:, :]), 16)

    es2 = ExitStack()
    psg = es2.enter_context(nc.psum_tensor("psg", [64, TW], F32))
    pet = es2.enter_context(nc.psum_tensor("pet", [128, 9 * 64], BF16))
    pop = es2.enter_context(nc.psum_tensor("pop", [64, 128], F32))
    qgv = qg_sb.rearrange("p (h j) -> p h j", h=NH)
    sq.wait(T)

    def g_scores(h):
        for (off, nlen) in ((0, 512), (512, 512), (1024, 128)):
            mm = T.matmul(psg[:, off:off + nlen], qgv[:, h, :], k3[:, h, off:off + nlen],
                          start=True, stop=True)
        return mm

    sv.wait(S)
    sv.wait(T)
    tr_g = None
    for h in range(NH):
        if h > 0:
            ss.wait(T)
        mm = g_scores(h)
        st.inc(mm, 1)
        st.wait(S)
        if tr_g is not None:
            st.wait(S, tr_g)
        act = S.activation(out=e_g[:, :], in_=psg[:, :], func=AF.Exp,
                           bias=0.0, scale=c_invsqd[0:64, 0:1])
        ecnt = ss.inc(act, 1)
        ss.wait(V, ecnt)
        sv.inc(V.memset(e_g[:, 0:HALO], 0.0), 1)
        sv.inc(V.memset(e_g[:, HALO + TPC:TW], 0.0), 1)
        sv.wait(V)
        op = V.reduce_sum(out=ol[:, NH * HD + h: NH * HD + h + 1],
                          in_=e_g[:, HALO:HALO + TPC], axis=AX.X)
        sv.inc(op, 1)
        sv.wait(T)
        for j in range(9):
            mm = T.transpose(pet[:, j * 64:(j + 1) * 64],
                             e_g[:, j * 128:(j + 1) * 128], ident_bf[0:64, 0:64])
        tr_g = st.inc(mm, 1)
        st.wait(V, tr_g)
        sv.inc(V.tensor_copy(out=eT_g[:, :], in_=pet[:, :]), 1)
        sv.wait(T)
        for j in range(9):
            mm = T.matmul(pop[:, 0:128], eT_g[:, j * 64:(j + 1) * 64],
                          v3[:, j, h * 128:(h + 1) * 128],
                          start=(j == 0), stop=(j == 8))
        st.inc(mm, 1)
        st.wait(V)
        sv.inc(V.tensor_copy(out=ol[:, h * HD:(h + 1) * HD], in_=pop[:, 0:128]), 1)
    es2.close()
    sv.wait(SY)
    sd.inc(SY.dma_start(out=cc_ol_in.ap()[:, :], in_=ol[:, :]), 16)
    sd.wait(G)
    cc = G.collective_compute("AllReduce", ALU.add, replica_groups=RG,
                              ins=[cc_ol_in.ap().opt()], outs=[cc_ol_out.ap().opt()])
    scc.inc(cc, 1)
    scc.wait(G)
    sq.inc(G.dma_start(out=ol[:, :], in_=cc_ol_out.ap()[:, :]), 16)
    sq.wait(V)
    for h in range(NH):
        sv.inc(V.reciprocal(out=rloc[:, h:h + 1], in_=ol[:, NH * HD + h: NH * HD + h + 1]), 1)
    sv.wait(V)
    op = V.tensor_tensor(out=gctx.rearrange("p (h d) -> p h d", h=NH),
                         in0=ol.ap()[:, 0:NH * HD].rearrange("p (h d) -> p h d", h=NH),
                         in1=rloc.ap().unsqueeze(2).broadcast_to([64, NH, HD]),
                         op=ALU.mult)
    sv.inc(op, 1)

    es2 = ExitStack()
    pscat = es2.enter_context(nc.psum_tensor("pscat", [128, 2048], F32))
    pcnt = es2.enter_context(nc.psum_tensor("pcnt", [128, 16], F32))
    for b in range(NB):
        sv.wait(T)
        for h in range(NH):
            op = V.tensor_scalar(out=oh_s[:, 0:128],
                                 in0=iota_t[:, b * 128:(b + 1) * 128],
                                 scalar1=idxT[:, h:h + 1], scalar2=None, op0=ALU.is_equal)
            ohcnt = sv.inc(op, 1)
            sv.wait(T, ohcnt)
            mm = T.matmul(pscat[:, h * 128:(h + 1) * 128], oh_s[:, 0:128],
                          gctx[:, h * HD:(h + 1) * HD],
                          start=True, stop=True)
            mm2 = T.matmul(pcnt[:, h:h + 1], oh_s[:, 0:128], ones_bf[0:64, 0:1],
                           start=True, stop=True)
            st.inc(mm2, 1)
            st.wait(V)
        op = V.tensor_copy(out=cnt_sb[:, :], in_=pcnt[:, :])
        sv.inc(op, 1)
        sv.wait(V)
        sv.inc(V.tensor_single_scalar(out=zq[:, :], in_=cnt_sb[:, :], scalar=0.0,
                                      op=ALU.is_equal), 1)
        sv.inc(V.tensor_scalar_max(out=rq[:, :], in0=cnt_sb[:, :], scalar1=1.0), 1)
        sv.wait(V)
        sv.inc(V.reciprocal(out=rq[:, :], in_=rq[:, :]), 1)
        sv.wait(V)
        op = V.tensor_tensor(out=t1_sb.rearrange("p (h d) -> p h d", h=NH),
                             in0=pscat.rearrange("p (h d) -> p h d", h=NH),
                             in1=rq.ap().unsqueeze(2).broadcast_to([128, NH, HD]),
                             op=ALU.mult)
        sv.inc(op, 1)
        ctxb = ctx.ap()[:, b * D:(b + 1) * D]
        op = V.tensor_tensor(out=ctxb.rearrange("p (h d) -> p h d", h=NH),
                             in0=ctxb.rearrange("p (h d) -> p h d", h=NH),
                             in1=zq.ap().unsqueeze(2).broadcast_to([128, NH, HD]),
                             op=ALU.mult)
        sv.inc(op, 1)
        sv.wait(V)
        op = V.tensor_tensor(out=ctxb, in0=ctxb, in1=t1_sb[:, :], op=ALU.add)
        sv.inc(op, 1)
    es2.close()
    esP.close()
    esR2.close()

    es = ExitStack()
    xst3 = es.enter_context(nc.sbuf_tensor("xst3", [128, 2 * D], F32, side="right"))
    for i in range(NB):
        slot = i % 2
        xs = xst3[:, slot * D:(slot + 1) * D]
        sv.wait(SY)
        so_sl[slot].wait(SY)
        dcnt = sd.inc(SY.dma_start(out=xs, in_=x_main[i * 128:(i + 1) * 128, :]), 16)
        sd.wait(V, dcnt)
        sv.wait(V)
        op = V.tensor_tensor(out=xs, in0=xs, in1=ctx[:, i * D:(i + 1) * D], op=ALU.add)
        acnt = sv.inc(op, 1)
        sv.wait(SY, acnt)
        so_sl[slot].inc(SY.dma_start(out=xres_dram[i * 128:(i + 1) * 128, :], in_=xs), 16)
    es.close()
    esR1.close()

    esL2 = ExitStack()
    lnx2 = esL2.enter_context(nc.sbuf_tensor("lnx2", [128, KC * TPC], BF16))
    lnx23 = lnx2.rearrange("p (c t) -> p c t", c=KC)
    es = ExitStack()
    xst4 = es.enter_context(nc.sbuf_tensor("xst4", [128, D], F32, side="right"))
    lntm2 = es.enter_context(nc.sbuf_tensor("lntm2", [128, D], BF16, side="right"))
    sqd2 = es.enter_context(nc.sbuf_tensor("sqd2", [128, D], F32, side="right"))
    pt_ln2 = es.enter_context(nc.psum_tensor("pt_ln2", [128, 2048], BF16))
    so_sl[0].wait(SY)
    so_sl[1].wait(SY)
    for i in range(NB):
        sv.wait(SY)
        dcnt = sd.inc(SY.dma_start(out=xst4[:, :], in_=xres_dram[i * 128:(i + 1) * 128, :]), 16)
        sd.wait(V, dcnt)
        xs = xst4[:, :]
        V.reduce_sum(out=ln_s[:, i:i + 1], in_=xs, axis=AX.X)
        op = V.scalar_tensor_tensor(out=sqd2[:, :], in0=xs, scalar=1.0, in1=xs,
                                    op0=ALU.mult, op1=ALU.mult, accum_out=ln_q[:, i:i + 1])
        sv.inc(op, 1)
        sv.wait(V)
        op = V.tensor_scalar(out=ln_m[:, i:i + 1], in0=ln_s[:, i:i + 1],
                             scalar1=c_nmean[:, 0:1], scalar2=None, op0=ALU.mult)
        sv.inc(op, 1)
        sv.wait(V)
        op = V.tensor_mul(out=ln_sd[:, i:i + 1], in0=ln_m[:, i:i + 1], in1=ln_m[:, i:i + 1])
        sv.inc(op, 1)
        sv.wait(V)
        op = V.scalar_tensor_tensor(out=ln_v[:, i:i + 1], in0=ln_q[:, i:i + 1],
                                    scalar=1.0 / float(D), in1=ln_sd[:, i:i + 1],
                                    op0=ALU.mult, op1=ALU.subtract)
        sv.inc(op, 1)
        sv.wait(S)
        op = S.activation(out=ln_sd[:, i:i + 1], in_=ln_v[:, i:i + 1], func=AF.Sqrt,
                          bias=c_eps[:, 0:1], scale=1.0)
        ss.inc(op, 1)
        ss.wait(V)
        op = V.reciprocal(out=ln_r[:, i:i + 1], in_=ln_sd[:, i:i + 1])
        sv.inc(op, 1)
        sv.wait(V)
        op = V.tensor_scalar(out=lntm2[:, :], in0=xs, scalar1=ln_m[:, i:i + 1],
                             scalar2=ln_r[:, i:i + 1], op0=ALU.add, op1=ALU.mult)
        sv.inc(op, 1)
        sv.wait(T)
        for kc in range(KC):
            mm = T.transpose(pt_ln2[:, kc * 128:(kc + 1) * 128],
                             lntm2[:, kc * 128:(kc + 1) * 128], ident_bf[:])
        st.inc(mm, 1)
        st.wait(V)
        op = V.tensor_copy(out=lnx23[:, :, i * 128:(i + 1) * 128],
                           in_=pt_ln2.rearrange("p (c t) -> p c t", c=KC))
        sv.inc(op, 1)
    es.close()

    es = ExitStack()
    hbuf = es.enter_context(nc.sbuf_tensor("hbuf", [128, ICN * 512], BF16, side="right"))
    gu_u8s = es.enter_context(nc.sbuf_tensor("gu_u8s", [128, 4 * D], U8, side="right"))
    gu_bf = es.enter_context(nc.sbuf_tensor("gu_bf", [128, 4 * D], BF16, side="right"))
    wd_u8s = es.enter_context(nc.sbuf_tensor("wd_u8s", [128, 2 * 512], U8, side="right"))
    wd_bf = es.enter_context(nc.sbuf_tensor("wd_bf", [128, 2 * 512], BF16, side="right"))
    silu_s = es.enter_context(nc.sbuf_tensor("silu_s", [128, 2 * 512], F32, side="right"))
    gsc_s = es.enter_context(nc.sbuf_tensor("gsc_s", [128, ICN], F32, side="right"))
    usc_s = es.enter_context(nc.sbuf_tensor("usc_s", [128, ICN], F32, side="right"))
    dscb_s = es.enter_context(nc.sbuf_tensor("dscb_s", [128, D], F32, side="right"))
    c_nzp = es.enter_context(nc.sbuf_tensor("c_nzp", [128, 1], F32, side="right"))
    xr_st = es.enter_context(nc.sbuf_tensor("xr_st", [128, 2 * 512], F32, side="right"))
    out_st = es.enter_context(nc.sbuf_tensor("out_st", [128, 2 * 512], F32, side="right"))
    pgs = [es.enter_context(nc.psum_tensor(f"pg{i}", [128, 512], F32)) for i in range(2)]
    pus = [es.enter_context(nc.psum_tensor(f"pu{i}", [128, 512], F32)) for i in range(2)]
    pds = [es.enter_context(nc.psum_tensor(f"pd{i}", [128, 512], F32)) for i in range(4)]

    sv.wait(SY)
    st.wait(SY)
    sd.inc(SY.dma_start(out=gsc_s[:], in_=gsc[:, :]), 16)
    sd.inc(SY.dma_start(out=usc_s[:], in_=usc[:, :]), 16)
    dcnt = sd.inc(SY.dma_start(out=dscb_s[:], in_=dscb[:, :]), 16)
    sv.wait(V)
    sv.inc(V.memset(c_nzp[:], -128.0), 1)
    sd.wait(S, dcnt)
    sd.wait(V, dcnt)
    sv.wait(S)

    guw_gcast = {}
    guw_ucast = {}
    guw_mm = {}
    silu_cnt = {}
    hm_cnt = {}
    wd_cast = {}
    wd_mm = {}
    for ph in range(2):
        tok0 = ph * 512
        st.wait(V)
        st.wait(S)
        for ic in range(ICN):
            slot = ic % 2
            if slot in guw_gcast:
                ss.wait(SY, guw_gcast[slot])
            if slot in guw_ucast:
                sv.wait(SY, guw_ucast[slot])
            s_w[slot].inc(SY.dma_start(out=gu_u8s[:, (2 * slot) * D:(2 * slot + 1) * D],
                                       in_=g_u8[ic, :, :]), 16)
            d2 = SY.dma_start(out=gu_u8s[:, (2 * slot + 1) * D:(2 * slot + 2) * D],
                              in_=u_u8[ic, :, :])
            s_w[slot].inc(d2, 16)
            s_w[slot].wait(S)
            s_w[slot].wait(V)
            if slot in guw_mm:
                st.wait(S, guw_mm[slot])
                st.wait(V, guw_mm[slot])
            a1 = S.activation(out=gu_bf[:, (2 * slot) * D:(2 * slot + 1) * D],
                              in_=gu_u8s[:, (2 * slot) * D:(2 * slot + 1) * D],
                              func=AF.Copy, bias=-128.0, scale=1.0)
            guw_gcast[slot] = ss.inc(a1, 1)
            o1 = V.tensor_scalar(out=gu_bf[:, (2 * slot + 1) * D:(2 * slot + 2) * D],
                                 in0=gu_u8s[:, (2 * slot + 1) * D:(2 * slot + 2) * D],
                                 scalar1=-128.0, scalar2=None, op0=ALU.add)
            guw_ucast[slot] = sv.inc(o1, 1)
            ss.wait(T, guw_gcast[slot])
            sv.wait(T, guw_ucast[slot])
            if slot in silu_cnt:
                ss.wait(T, silu_cnt[slot])
            if slot in hm_cnt:
                sv.wait(T, hm_cnt[slot])
            gb = gu_bf.rearrange("p (s c f) -> p s c f", s=4, c=KC)
            for kc in range(KC):
                mm = T.matmul(pgs[slot][:, :], gb[:, 2 * slot, kc, :],
                              lnx23[:, kc, tok0:tok0 + 512],
                              start=(kc == 0), stop=(kc == KC - 1))
            for kc in range(KC):
                mm = T.matmul(pus[slot][:, :], gb[:, 2 * slot + 1, kc, :],
                              lnx23[:, kc, tok0:tok0 + 512],
                              start=(kc == 0), stop=(kc == KC - 1))
            guw_mm[slot] = st.inc(mm, 1)
            st.wait(S, guw_mm[slot])
            if slot in hm_cnt:
                sv.wait(S, hm_cnt[slot])
            act = S.activation(out=silu_s[:, slot * 512:(slot + 1) * 512],
                               in_=pgs[slot][:, :], func=AF.Silu,
                               bias=0.0, scale=gsc_s[:, ic:ic + 1])
            silu_cnt[slot] = ss.inc(act, 1)
            ss.wait(V, silu_cnt[slot])
            op = V.scalar_tensor_tensor(out=hbuf[:, ic * 512:(ic + 1) * 512],
                                        in0=pus[slot][:, :], scalar=usc_s[:, ic:ic + 1],
                                        in1=silu_s[:, slot * 512:(slot + 1) * 512],
                                        op0=ALU.mult, op1=ALU.mult)
            hm_cnt[slot] = sv.inc(op, 1)
        for dc in range(4):
            for ic in range(ICN):
                slot = ic % 2
                if slot in wd_cast:
                    ss.wait(SY, wd_cast[slot])
                d1 = SY.dma_start(out=wd_u8s[:, slot * 512:(slot + 1) * 512],
                                  in_=d_u8[ic, dc, :, :])
                s_d[slot].inc(d1, 16)
                s_d[slot].wait(S)
                if slot in wd_mm:
                    st.wait(S, wd_mm[slot])
                a1 = S.activation(out=wd_bf[:, slot * 512:(slot + 1) * 512],
                                  in_=wd_u8s[:, slot * 512:(slot + 1) * 512],
                                  func=AF.Copy, bias=-128.0, scale=1.0)
                wd_cast[slot] = ss.inc(a1, 1)
                ss.wait(T, wd_cast[slot])
                if ic == 0:
                    sv.wait(T)
                for tc in range(4):
                    mm = T.matmul(pds[tc][:, :],
                                  hbuf[:, ic * 512 + tc * 128: ic * 512 + tc * 128 + 128],
                                  wd_bf[:, slot * 512:(slot + 1) * 512],
                                  start=(ic == 0), stop=(ic == ICN - 1))
                wd_mm[slot] = st.inc(mm, 1)
            st.wait(V)
            for tc in range(4):
                gtc = ph * 4 + tc
                slot = tc % 2
                sv.wait(SY)
                dcnt = sd.inc(SY.dma_start(out=xr_st[:, slot * 512:(slot + 1) * 512],
                                           in_=xres_dram[gtc * 128:(gtc + 1) * 128,
                                                         dc * 512:(dc + 1) * 512]), 16)
                sd.wait(V, dcnt)
                sv.wait(V)
                so_sl[slot].wait(V)
                op = V.tensor_tensor(out=out_st[:, slot * 512:(slot + 1) * 512],
                                     in0=pds[tc][:, :], in1=dscb_s[:, dc * 512:(dc + 1) * 512],
                                     op=ALU.mult)
                sv.inc(op, 1)
                sv.wait(V)
                op = V.tensor_tensor(out=out_st[:, slot * 512:(slot + 1) * 512],
                                     in0=out_st[:, slot * 512:(slot + 1) * 512],
                                     in1=xr_st[:, slot * 512:(slot + 1) * 512],
                                     op=ALU.add)
                ocnt = sv.inc(op, 1)
                sv.wait(SY, ocnt)
                d = SY.dma_start(out=y[gtc * 128:(gtc + 1) * 128, dc * 512:(dc + 1) * 512],
                                 in_=out_st[:, slot * 512:(slot + 1) * 512])
                so_sl[slot].inc(d, 16)
    es.close()
    esL2.close()
    esK.close()

    sd.wait(SY)
    so_sl[0].wait(SY)
    so_sl[1].wait(SY)
    nc.all_engine_barrier()
    return nc


def _prep_inputs(inputs):
    x = np.asarray(inputs["L0_attn_hidden_states"]).reshape(SEQ, D)
    Wq = np.asarray(inputs["L0_attn_Wq"])
    Wk = np.asarray(inputs["L0_attn_Wk"])
    Wv = np.asarray(inputs["L0_attn_Wv"])
    bqv = np.asarray(inputs["L0_attn_bq"])
    bkv = np.asarray(inputs["L0_attn_bk"])
    bvv = np.asarray(inputs["L0_attn_bv"])
    gw = np.asarray(inputs["L0_ffn_gate_w"])
    uw = np.asarray(inputs["L0_ffn_up_w"])
    dw = np.asarray(inputs["L0_ffn_down_w"])
    gs = np.asarray(inputs["L0_ffn_gate_scale"])
    us = np.asarray(inputs["L0_ffn_up_scale"])
    ds = np.asarray(inputs["L0_ffn_down_scale"])

    def tile_w(W):
        t = W.reshape(KC, 128, KC, 128).transpose(2, 0, 1, 3)
        t = t.reshape(KC, 2, 8, 128, 128).transpose(0, 1, 3, 2, 4)
        return np.ascontiguousarray(t.reshape(KC, 2, 128, 1024))

    def u8_of(Wi32):
        return Wi32.view(np.uint8).reshape(*Wi32.shape, 4)[..., 0]

    g8 = u8_of(gw)
    u8a = u8_of(uw)
    d8 = u8_of(dw)
    g_t = g8.T.reshape(KC, 128, ICN, 128).transpose(2, 0, 1, 3)
    g_t = np.ascontiguousarray(g_t.transpose(0, 2, 1, 3).reshape(ICN, 128, D))
    u_t = u8a.T.reshape(KC, 128, ICN, 128).transpose(2, 0, 1, 3)
    u_t = np.ascontiguousarray(u_t.transpose(0, 2, 1, 3).reshape(ICN, 128, D))
    d_t = np.ascontiguousarray(d8.T.reshape(ICN, 128, 4, 512).transpose(0, 2, 1, 3))

    bq_s = np.ascontiguousarray(bqv.reshape(KC, 128).T)
    bk_s = np.ascontiguousarray(bkv.reshape(KC, 128).T)
    bv_s = np.ascontiguousarray(bvv.reshape(KC, 128).T)
    gsc_a = np.ascontiguousarray(gs.reshape(ICN, 128).T)
    usc_a = np.ascontiguousarray(us.reshape(ICN, 128).T)
    dscb_a = np.ascontiguousarray(np.broadcast_to(ds.reshape(1, D), (128, D)))
    wq_t, wk_t, wv_t = tile_w(Wq), tile_w(Wk), tile_w(Wv)

    in_maps = []
    for c in range(NCORES):
        t0 = c * TPC
        hl = x[t0 - HALO: t0] if t0 - HALO >= 0 else x[0:HALO]
        hr = x[t0 + TPC: t0 + TPC + HALO] if t0 + TPC + HALO <= SEQ else x[SEQ - HALO:SEQ]
        in_maps.append({
            "x_main": np.ascontiguousarray(x[t0:t0 + TPC]),
            "x_hl": np.ascontiguousarray(hl),
            "x_hr": np.ascontiguousarray(hr),
            "wq": wq_t, "wk": wk_t, "wv": wv_t,
            "bq": bq_s, "bk": bk_s, "bv": bv_s,
            "g_u8": g_t, "u_u8": u_t, "d_u8": d_t,
            "gsc": gsc_a, "usc": usc_a, "dscb": dscb_a,
            "mask_l": np.full((128, 1), 0.0 if c == 0 else 1.0, np.float32),
            "mask_r": np.full((128, 1), 0.0 if c == NCORES - 1 else 1.0, np.float32),
            "base16": np.full((16, 1), float(t0), np.float32),
        })
    return in_maps


def kernel(**inputs):
    in_maps = _prep_inputs(inputs)
    nc = build_nc()
    res = run_bass_kernel_spmd(nc, in_maps, core_ids=list(range(NCORES)))
    out = np.concatenate([res.results[c]["y"] for c in range(NCORES)], axis=0)
    return out.reshape(1, SEQ, D).astype(np.float32)


# version 10
# speedup vs baseline: 1.1065x; 1.0164x over previous
from contextlib import ExitStack

import numpy as np
import concourse.bass as bass
import concourse.mybir as mybir
from concourse.bass_utils import run_bass_kernel_spmd

F32 = mybir.dt.float32
BF16 = mybir.dt.bfloat16
U8 = mybir.dt.uint8
I32 = mybir.dt.int32
AF = mybir.ActivationFunctionType
ALU = mybir.AluOpType
AX = mybir.AxisListType

NCORES = 8
SEQ = 8192
D = 2048
NH = 16
HD = 128
TPC = 1024
HALO = 64
TW = TPC + 2 * HALO
NB = TPC // 128
KC = D // 128
II = 8192
ICN = II // 128
INV_SQRT_D = float(1.0 / np.sqrt(128.0))
BIG = 1.0e30


class Sem:
    def __init__(self, nc, name):
        self.h = nc.alloc_semaphore(name=name)
        self.n = 0

    def inc(self, instr, k=1):
        instr.then_inc(self.h, k)
        self.n += k
        return self.n

    def wait(self, eng, val=None):
        eng.wait_ge(self.h, self.n if val is None else val)


def build_nc():
    nc = bass.Bass("TRN2", target_bir_lowering=False)

    x_main = nc.dram_tensor("x_main", [TPC, D], F32, kind="ExternalInput").ap()
    x_hl = nc.dram_tensor("x_hl", [HALO, D], F32, kind="ExternalInput").ap()
    x_hr = nc.dram_tensor("x_hr", [HALO, D], F32, kind="ExternalInput").ap()
    wq = nc.dram_tensor("wq", [KC, 2, 128, 1024], F32, kind="ExternalInput").ap()
    wk = nc.dram_tensor("wk", [KC, 2, 128, 1024], F32, kind="ExternalInput").ap()
    wv = nc.dram_tensor("wv", [KC, 2, 128, 1024], F32, kind="ExternalInput").ap()
    bq = nc.dram_tensor("bq", [128, KC], F32, kind="ExternalInput").ap()
    bk = nc.dram_tensor("bk", [128, KC], F32, kind="ExternalInput").ap()
    bv = nc.dram_tensor("bv", [128, KC], F32, kind="ExternalInput").ap()
    g_u8 = nc.dram_tensor("g_u8", [ICN, 128, D], U8, kind="ExternalInput").ap()
    u_u8 = nc.dram_tensor("u_u8", [ICN, 128, D], U8, kind="ExternalInput").ap()
    d_u8 = nc.dram_tensor("d_u8", [ICN, 4, 128, 512], U8, kind="ExternalInput").ap()
    gsc = nc.dram_tensor("gsc", [128, ICN], F32, kind="ExternalInput").ap()
    usc = nc.dram_tensor("usc", [128, ICN], F32, kind="ExternalInput").ap()
    dscb = nc.dram_tensor("dscb", [128, D], F32, kind="ExternalInput").ap()
    mask_l = nc.dram_tensor("mask_l", [128, 1], F32, kind="ExternalInput").ap()
    mask_r = nc.dram_tensor("mask_r", [128, 1], F32, kind="ExternalInput").ap()
    base16 = nc.dram_tensor("base16", [16, 1], F32, kind="ExternalInput").ap()

    y = nc.dram_tensor("y", [TPC, D], F32, kind="ExternalOutput").ap()

    xres_dram = nc.dram_tensor("xres_dram", [TPC, D], F32).ap()
    scr_c = nc.dram_tensor("scr_c", [128, 64], F32)
    scr_i = nc.dram_tensor("scr_i", [1, NH * 64], F32)
    scr_n = nc.dram_tensor("scr_n", [16, 8 * TPC], F32)
    cc_nrm_in = nc.dram_tensor("cc_nrm_in", [16, TPC], F32)
    cc_nrm_out = nc.dram_tensor("cc_nrm_out", [128, TPC], F32)
    cc_qg_in = nc.dram_tensor("cc_qg_in", [128, NH * 64], BF16)
    cc_qg_out = nc.dram_tensor("cc_qg_out", [128, NH * 64], BF16)
    cc_m_in = nc.dram_tensor("cc_m_in", [64, NH], F32)
    cc_m_out = nc.dram_tensor("cc_m_out", [64, NH], F32)
    cc_ol_in = nc.dram_tensor("cc_ol_in", [64, NH * HD + NH], F32)
    cc_ol_out = nc.dram_tensor("cc_ol_out", [64, NH * HD + NH], F32)
    RG = [list(range(NCORES))]

    nc.gpsimd.dma_reset(range(155, 256))
    nc.gpsimd.sem_clear(range(155, 256))
    nc.all_engine_barrier()

    sv = Sem(nc, "sv")
    ss = Sem(nc, "ss")
    st = Sem(nc, "st")
    sd = Sem(nc, "sd")
    sq = Sem(nc, "sq")
    sg = Sem(nc, "sg")
    scc = Sem(nc, "scc")
    s_w = [Sem(nc, "sw0"), Sem(nc, "sw1")]
    s_d = [Sem(nc, "sd0"), Sem(nc, "sd1")]
    so_sl = [Sem(nc, "so0"), Sem(nc, "so1")]

    V, S, T, G, SY = nc.vector, nc.scalar, nc.tensor, nc.gpsimd, nc.sync

    esK = ExitStack()

    def sbK(name, shape, dt):
        return esK.enter_context(nc.sbuf_tensor(name, shape, dt))

    ident_bf = sbK("ident_bf", [128, 128], BF16)
    ident_f32 = sbK("ident_f32", [128, 128], F32)
    iota_pf = sbK("iota_pf", [128, 128], F32)
    c_invsqd = sbK("c_invsqd", [128, 1], F32)
    c_eps = sbK("c_eps", [128, 1], F32)
    c_nmean = sbK("c_nmean", [128, 1], F32)
    ones_f32 = sbK("ones_f32", [128, 1], F32)
    ones_bf = sbK("ones_bf", [128, 1], BF16)
    onesr_f32 = sbK("onesr_f32", [1, 128], F32)
    bq_s = sbK("bq_s", [128, KC], F32)
    bk_s = sbK("bk_s", [128, KC], F32)
    bv_s = sbK("bv_s", [128, KC], F32)
    maskl_s = sbK("maskl_s", [128, 1], F32)
    maskr_s = sbK("maskr_s", [128, 1], F32)
    base16_s = sbK("base16_s", [16, 1], F32)
    ln_s = sbK("ln_s", [128, 10], F32)
    ln_q = sbK("ln_q", [128, 10], F32)
    ln_v = sbK("ln_v", [128, 10], F32)
    ln_sd = sbK("ln_sd", [128, 10], F32)
    ln_r = sbK("ln_r", [128, 10], F32)
    ln_m = sbK("ln_m", [128, 10], F32)

    iop = G.iota(iota_pf[:, :], pattern=[[-1, 128]], base=0, channel_multiplier=1,
                 allow_small_or_imprecise_dtypes=True)
    sg.inc(iop, 1)
    s_d[0].inc(SY.dma_start(out=bq_s[:], in_=bq[:, :]), 16)
    s_d[0].inc(SY.dma_start(out=bk_s[:], in_=bk[:, :]), 16)
    s_d[0].inc(SY.dma_start(out=bv_s[:], in_=bv[:, :]), 16)
    s_d[0].inc(SY.dma_start(out=maskl_s[:], in_=mask_l[:, :]), 16)
    s_d[0].inc(SY.dma_start(out=maskr_s[:], in_=mask_r[:, :]), 16)
    s_d[0].inc(SY.dma_start(out=base16_s[:], in_=base16[:, :]), 16)
    V.memset(c_invsqd[:], INV_SQRT_D)
    V.memset(c_eps[:], 1.0e-5)
    V.memset(c_nmean[:], -1.0 / float(D))
    V.memset(ones_f32[:], 1.0)
    V.memset(ones_bf[:], 1.0)
    sv.inc(V.memset(onesr_f32[:], 1.0), 1)
    sg.wait(V)
    V.tensor_single_scalar(out=ident_bf[:], in_=iota_pf[:], scalar=0.0, op=ALU.is_equal)
    sv.inc(V.tensor_single_scalar(out=ident_f32[:], in_=iota_pf[:], scalar=0.0, op=ALU.is_equal), 1)
    s_d[0].wait(V)

    esR1 = ExitStack()
    ctx = esR1.enter_context(nc.sbuf_tensor("ctx", [128, NB * D], BF16, side="right"))
    esR2 = ExitStack()
    q_fm = esR2.enter_context(nc.sbuf_tensor("q_fm", [128, NH * TPC], BF16, side="right"))
    k_fm = esR2.enter_context(nc.sbuf_tensor("k_fm", [128, NH * TW], BF16, side="right"))
    v_tm = esR2.enter_context(nc.sbuf_tensor("v_tm", [128, 9 * D], BF16, side="right"))
    q3 = q_fm.rearrange("p (h t) -> p h t", h=NH)
    k3 = k_fm.rearrange("p (h t) -> p h t", h=NH)
    v3 = v_tm.rearrange("p (j d) -> p j d", j=9)

    esLN = ExitStack()
    lnx = esLN.enter_context(nc.sbuf_tensor("lnx", [128, KC * TW], BF16, side="right"))
    lnx3 = lnx.rearrange("p (c t) -> p c t", c=KC)

    es = ExitStack()
    xstage = es.enter_context(nc.sbuf_tensor("xstage", [128, D], F32, side="right"))
    lntm = es.enter_context(nc.sbuf_tensor("lntm", [128, D], BF16, side="right"))
    sqd = es.enter_context(nc.sbuf_tensor("sqd", [128, D], F32, side="right"))
    pt_ln = es.enter_context(nc.psum_tensor("pt_ln", [128, 2048], BF16))

    def ln_tile(i, nrow, src_ap, out3, dst_col0, xs, sq_t, lntm_t, pt):
        sv.wait(SY)
        dcnt = sd.inc(SY.dma_start(out=xs, in_=src_ap), 16)
        sd.wait(V, dcnt)
        V.reduce_sum(out=ln_s[0:nrow, i:i + 1], in_=xs, axis=AX.X)
        op = V.scalar_tensor_tensor(out=sq_t, in0=xs, scalar=1.0, in1=xs,
                                    op0=ALU.mult, op1=ALU.mult,
                                    accum_out=ln_q[0:nrow, i:i + 1])
        sv.inc(op, 1)
        sv.wait(V)
        op = V.tensor_scalar(out=ln_m[0:nrow, i:i + 1], in0=ln_s[0:nrow, i:i + 1],
                             scalar1=c_nmean[0:nrow, :], scalar2=None, op0=ALU.mult)
        sv.inc(op, 1)
        sv.wait(V)
        op = V.tensor_mul(out=ln_sd[0:nrow, i:i + 1], in0=ln_m[0:nrow, i:i + 1],
                          in1=ln_m[0:nrow, i:i + 1])
        sv.inc(op, 1)
        sv.wait(V)
        op = V.scalar_tensor_tensor(out=ln_v[0:nrow, i:i + 1], in0=ln_q[0:nrow, i:i + 1],
                                    scalar=1.0 / float(D), in1=ln_sd[0:nrow, i:i + 1],
                                    op0=ALU.mult, op1=ALU.subtract)
        sv.inc(op, 1)
        sv.wait(S)
        op = S.activation(out=ln_sd[0:nrow, i:i + 1], in_=ln_v[0:nrow, i:i + 1],
                          func=AF.Sqrt, bias=c_eps[0:nrow, :], scale=1.0)
        ss.inc(op, 1)
        ss.wait(V)
        op = V.reciprocal(out=ln_r[0:nrow, i:i + 1], in_=ln_sd[0:nrow, i:i + 1])
        sv.inc(op, 1)
        sv.wait(V)
        op = V.tensor_scalar(out=lntm_t, in0=xs, scalar1=ln_m[0:nrow, i:i + 1],
                             scalar2=ln_r[0:nrow, i:i + 1], op0=ALU.add, op1=ALU.mult)
        sv.inc(op, 1)
        sv.wait(T)
        for kc in range(KC):
            mm = T.transpose(pt[:, kc * 128: kc * 128 + nrow],
                             lntm_t[0:nrow, kc * 128:(kc + 1) * 128] if False
                             else lntm_t.tensor[0:nrow, kc * 128:(kc + 1) * 128]
                             if hasattr(lntm_t, "tensor") else lntm_t,
                             ident_bf[:])
        st.inc(mm, 1)
        st.wait(V)
        out_ap = out3[:, :, dst_col0:dst_col0 + nrow]
        in_ap = pt.rearrange("p (c t) -> p c t", c=KC)[:, :, 0:nrow]
        sv.inc(V.tensor_copy(out=out_ap, in_=in_ap), 1)

    def ln_tile2(i, nrow, src_ap, out3, dst_col0):
        xs = xstage[0:nrow, 0:D]
        sv.wait(SY)
        dcnt = sd.inc(SY.dma_start(out=xs, in_=src_ap), 16)
        sd.wait(V, dcnt)
        V.reduce_sum(out=ln_s[0:nrow, i:i + 1], in_=xs, axis=AX.X)
        op = V.scalar_tensor_tensor(out=sqd[0:nrow, :], in0=xs, scalar=1.0, in1=xs,
                                    op0=ALU.mult, op1=ALU.mult,
                                    accum_out=ln_q[0:nrow, i:i + 1])
        sv.inc(op, 1)
        sv.wait(V)
        op = V.tensor_scalar(out=ln_m[0:nrow, i:i + 1], in0=ln_s[0:nrow, i:i + 1],
                             scalar1=c_nmean[0:nrow, :], scalar2=None, op0=ALU.mult)
        sv.inc(op, 1)
        sv.wait(V)
        op = V.tensor_mul(out=ln_sd[0:nrow, i:i + 1], in0=ln_m[0:nrow, i:i + 1],
                          in1=ln_m[0:nrow, i:i + 1])
        sv.inc(op, 1)
        sv.wait(V)
        op = V.scalar_tensor_tensor(out=ln_v[0:nrow, i:i + 1], in0=ln_q[0:nrow, i:i + 1],
                                    scalar=1.0 / float(D), in1=ln_sd[0:nrow, i:i + 1],
                                    op0=ALU.mult, op1=ALU.subtract)
        sv.inc(op, 1)
        sv.wait(S)
        op = S.activation(out=ln_sd[0:nrow, i:i + 1], in_=ln_v[0:nrow, i:i + 1],
                          func=AF.Sqrt, bias=c_eps[0:nrow, :], scale=1.0)
        ss.inc(op, 1)
        ss.wait(V)
        op = V.reciprocal(out=ln_r[0:nrow, i:i + 1], in_=ln_sd[0:nrow, i:i + 1])
        sv.inc(op, 1)
        sv.wait(V)
        op = V.tensor_scalar(out=lntm[0:nrow, 0:D], in0=xs, scalar1=ln_m[0:nrow, i:i + 1],
                             scalar2=ln_r[0:nrow, i:i + 1], op0=ALU.add, op1=ALU.mult)
        sv.inc(op, 1)
        sv.wait(T)
        for kc in range(KC):
            mm = T.transpose(pt_ln[:, kc * 128: kc * 128 + nrow],
                             lntm[0:nrow, kc * 128:(kc + 1) * 128],
                             ident_bf[0:nrow, 0:nrow])
        st.inc(mm, 1)
        st.wait(V)
        out_ap = out3[:, :, dst_col0:dst_col0 + nrow]
        in_ap = pt_ln.rearrange("p (c t) -> p c t", c=KC)[:, :, 0:nrow]
        sv.inc(V.tensor_copy(out=out_ap, in_=in_ap), 1)

    ln_tile2(8, HALO, x_hl[:, :], lnx3, 0)
    ln_tile2(9, HALO, x_hr[:, :], lnx3, HALO + TPC)
    for i in range(NB):
        ln_tile2(i, 128, x_main[i * 128:(i + 1) * 128, :], lnx3, HALO + i * 128)
    es.close()

    es = ExitStack()
    wstage = es.enter_context(nc.sbuf_tensor("wstage", [128, 2 * 1024], F32, side="right"))
    wbf = es.enter_context(nc.sbuf_tensor("wbf", [128, 2 * 1024], BF16, side="right"))
    vfm_s = es.enter_context(nc.sbuf_tensor("vfm_s", [128, TW], BF16, side="right"))
    sqb = es.enter_context(nc.sbuf_tensor("sqb", [128, 512], F32, side="right"))
    norms16 = es.enter_context(nc.sbuf_tensor("norms16", [16, TPC], F32, side="right"))
    pq0 = es.enter_context(nc.psum_tensor("pq0", [128, 512], F32))
    pq1 = es.enter_context(nc.psum_tensor("pq1", [128, 512], F32))
    pk2 = es.enter_context(nc.psum_tensor("pk2", [128, 128], F32))
    pt_v = es.enter_context(nc.psum_tensor("pt_v", [128, 9 * 128], BF16))

    wbf3 = wbf.rearrange("p (s j f) -> p s j f", s=2, j=8)
    slot_cast = {}
    slot_mm = {}
    gslot = [0]

    def stage_half(src, mc, wh, cast_eng, cast_sem):
        slot = gslot[0] % 2
        gslot[0] += 1
        if slot in slot_cast:
            sem_prev, ccnt_prev = slot_cast[slot]
            sem_prev.wait(SY, ccnt_prev)
        s_w[slot].inc(SY.dma_start(out=wstage[:, slot * 1024:(slot + 1) * 1024],
                                   in_=src[mc, wh, :, :]), 16)
        s_w[slot].wait(cast_eng)
        if slot in slot_mm:
            st.wait(cast_eng, slot_mm[slot])
        if cast_eng is S:
            op = cast_eng.copy(out=wbf[:, slot * 1024:(slot + 1) * 1024],
                               in_=wstage[:, slot * 1024:(slot + 1) * 1024])
        else:
            op = cast_eng.tensor_copy(out=wbf[:, slot * 1024:(slot + 1) * 1024],
                                      in_=wstage[:, slot * 1024:(slot + 1) * 1024])
        ccnt = cast_sem.inc(op, 1)
        slot_cast[slot] = (cast_sem, ccnt)
        return slot, ccnt

    sv.wait(SY)
    for mc in range(KC):
        sv.wait(T)
        for wh in range(2):
            slot, ccnt = stage_half(wq, mc, wh, V, sv)
            sv.wait(T, ccnt)
            for half in range(2):
                ps = pq0 if half == 0 else pq1
                for j in range(8):
                    kc = wh * 8 + j
                    mm = T.matmul(ps[:, 0:512], wbf3[:, slot, j, :],
                                  lnx3[:, kc, HALO + half * 512: HALO + half * 512 + 512],
                                  start=(kc == 0), stop=(kc == KC - 1))
            st.inc(mm, 1)
            slot_mm[slot] = st.n
        st.wait(V)
        for half in range(2):
            ps = pq0 if half == 0 else pq1
            op = V.tensor_scalar(out=q3[:, mc, half * 512: half * 512 + 512],
                                 in0=ps[:, 0:512], scalar1=bq_s[:, mc:mc + 1],
                                 scalar2=None, op0=ALU.add)
            sv.inc(op, 1)

    for mc in range(KC):
        sv.wait(T)
        for wh in range(2):
            slot, ccnt = stage_half(wk, mc, wh, S, ss)
            ss.wait(T, ccnt)
            for (ps, off, nlen) in ((pq0, 0, 512), (pq1, 512, 512), (pk2, 1024, 128)):
                for j in range(8):
                    kc = wh * 8 + j
                    mm = T.matmul(ps[:, 0:nlen], wbf3[:, slot, j, :],
                                  lnx3[:, kc, off:off + nlen],
                                  start=(kc == 0), stop=(kc == KC - 1))
            st.inc(mm, 1)
            slot_mm[slot] = st.n
        st.wait(V)
        for (ps, off, nlen) in ((pq0, 0, 512), (pq1, 512, 512), (pk2, 1024, 128)):
            op = V.tensor_scalar(out=k3[:, mc, off:off + nlen],
                                 in0=ps[:, 0:nlen], scalar1=bk_s[:, mc:mc + 1],
                                 scalar2=None, op0=ALU.add)
            sv.inc(op, 1)

    for mc in range(KC):
        sv.wait(T)
        for wh in range(2):
            slot, ccnt = stage_half(wv, mc, wh, S, ss)
            ss.wait(T, ccnt)
            for (ps, off, nlen) in ((pq0, 0, 512), (pq1, 512, 512), (pk2, 1024, 128)):
                for j in range(8):
                    kc = wh * 8 + j
                    mm = T.matmul(ps[:, 0:nlen], wbf3[:, slot, j, :],
                                  lnx3[:, kc, off:off + nlen],
                                  start=(kc == 0), stop=(kc == KC - 1))
            st.inc(mm, 1)
            slot_mm[slot] = st.n
        st.wait(V)
        for (ps, off, nlen) in ((pq0, 0, 512), (pq1, 512, 512), (pk2, 1024, 128)):
            op = V.tensor_scalar(out=vfm_s[:, off:off + nlen],
                                 in0=ps[:, 0:nlen], scalar1=bv_s[:, mc:mc + 1],
                                 scalar2=None, op0=ALU.add)
            sv.inc(op, 1)
        sv.wait(T)
        for j in range(9):
            mm = T.transpose(pt_v[:, j * 128:(j + 1) * 128],
                             vfm_s[:, j * 128:(j + 1) * 128], ident_bf[:])
        st.inc(mm, 1)
        st.wait(V)
        sv.inc(V.tensor_copy(out=v3[:, :, mc * 128:(mc + 1) * 128],
                             in_=pt_v.rearrange("p (j d) -> p j d", j=9)), 1)

    es2n = ExitStack()
    ohall_i = es2n.enter_context(nc.sbuf_tensor("ohall_i", [128, 256], F32, side="right"))
    ohall = es2n.enter_context(nc.sbuf_tensor("ohall", [128, 256], F32, side="right"))
    pnrm = es2n.enter_context(nc.psum_tensor("pnrm", [16, 512], F32))
    iop = G.iota(ohall_i[:, :], pattern=[[-1, 16], [1, 16]], base=0,
                 channel_multiplier=0, allow_small_or_imprecise_dtypes=True)
    sg.inc(iop, 1)
    sg.wait(V)
    op = V.tensor_single_scalar(out=ohall[:, :], in_=ohall_i[:, :], scalar=0.0,
                                op=ALU.is_equal)
    sv.inc(op, 1)
    for half in range(2):
        for h in range(NH):
            sv.wait(V)
            st.wait(V)
            op = V.tensor_mul(out=sqb[:, 0:512],
                              in0=q3[:, h, half * 512: half * 512 + 512],
                              in1=q3[:, h, half * 512: half * 512 + 512])
            sv.inc(op, 1)
            sv.wait(T)
            mm = T.matmul(pnrm[0:16, 0:512], ohall[:, h * 16:(h + 1) * 16],
                          sqb[:, 0:512], start=(h == 0), stop=(h == NH - 1))
            st.inc(mm, 1)
        st.wait(V)
        op = V.tensor_copy(out=norms16[:, half * 512: half * 512 + 512],
                           in_=pnrm[0:16, 0:512])
        sv.inc(op, 1)
        sv.wait(T)
    es2n.close()
    sv.wait(SY)
    sd.inc(SY.dma_start(out=cc_nrm_in.ap()[:, :], in_=norms16[:, :]), 16)
    sd.wait(G)
    cc = G.collective_compute("AllGather", ALU.bypass, replica_groups=RG,
                              ins=[cc_nrm_in.ap().opt()], outs=[cc_nrm_out.ap().opt()])
    scc.inc(cc, 1)
    es.close()
    esLN.close()

    es = ExitStack()
    e_loc = es.enter_context(nc.sbuf_tensor("e_loc", [128, 2 * 1024], BF16, side="right"))
    pr_loc = es.enter_context(nc.sbuf_tensor("pr_loc", [128, 2 * 1024], BF16, side="right"))
    pT_loc = es.enter_context(nc.sbuf_tensor("pT_loc", [128, 2 * 1024], BF16, side="right"))
    m4 = es.enter_context(nc.sbuf_tensor("m4", [128, 8], F32, side="right"))
    nm4 = es.enter_context(nc.sbuf_tensor("nm4", [128, 8], F32, side="right"))
    l4 = es.enter_context(nc.sbuf_tensor("l4", [128, 8], F32, side="right"))
    r4 = es.enter_context(nc.sbuf_tensor("r4", [128, 8], F32, side="right"))
    psc0 = es.enter_context(nc.psum_tensor("psc0", [128, 1024], F32))
    psc1 = es.enter_context(nc.psum_tensor("psc1", [128, 1024], F32))
    ppt0 = es.enter_context(nc.psum_tensor("ppt0", [128, 1024], BF16))
    ppt1 = es.enter_context(nc.psum_tensor("ppt1", [128, 1024], BF16))
    ppv0 = es.enter_context(nc.psum_tensor("ppv0", [128, 512], F32))
    ppv1 = es.enter_context(nc.psum_tensor("ppv1", [128, 512], F32))

    evac_cnt = {}
    pv_cnt = {}
    exp_cnt = {}
    tr_cnt = {}
    ptb_cnt = {}
    recip_cnt = {}
    for b in range(NB):
        for g in range(4):
            par = (b * 4 + g) % 2
            psc = psc0 if par == 0 else psc1
            ppt = ppt0 if par == 0 else ppt1
            ppv = ppv0 if par == 0 else ppv1
            eb = e_loc[:, par * 1024:(par + 1) * 1024]
            ptb = pT_loc[:, par * 1024:(par + 1) * 1024]
            l4b = l4[:, par * 4:(par + 1) * 4]
            r4b = r4[:, par * 4:(par + 1) * 4]
            heads = [4 * g + i for i in range(4)]
            if par in exp_cnt:
                ss.wait(T, exp_cnt[par])
            for i, h in enumerate(heads):
                mm = T.matmul(psc[:, i * 256:(i + 1) * 256],
                              q3[:, h, b * 128:(b + 1) * 128],
                              k3[:, h, b * 128: b * 128 + 256],
                              start=True, stop=True)
            qk_cnt = st.inc(mm, 1)
            st.wait(S, qk_cnt)
            edge = (b == 0) or (b == NB - 1)
            if par in tr_cnt:
                st.wait(S, tr_cnt[par])
            if par in recip_cnt:
                sv.wait(S, recip_cnt[par])
            for i, h in enumerate(heads):
                act = S.activation(out=eb[:, i * 256:(i + 1) * 256],
                                   in_=psc[:, i * 256:(i + 1) * 256],
                                   func=AF.Exp, bias=0.0,
                                   scale=c_invsqd[:, 0:1],
                                   accum_out=(None if edge else l4b[:, i:i + 1]))
            exp_cnt[par] = ss.inc(act, 1)
            ss.wait(V, exp_cnt[par])
            if b == 0:
                eb4 = eb.rearrange("p (i k) -> p i k", i=4)
                op = V.tensor_scalar(out=eb4[:, :, 0:64], in0=eb4[:, :, 0:64],
                                     scalar1=maskl_s[:, 0:1], scalar2=None, op0=ALU.mult)
                sv.inc(op, 1)
                sv.wait(V)
            if b == NB - 1:
                eb4 = eb.rearrange("p (i k) -> p i k", i=4)
                op = V.tensor_scalar(out=eb4[:, :, 192:256], in0=eb4[:, :, 192:256],
                                     scalar1=maskr_s[:, 0:1], scalar2=None, op0=ALU.mult)
                sv.inc(op, 1)
                sv.wait(V)
            if edge:
                op = V.reduce_sum(out=l4b, in_=eb.rearrange("p (i k) -> p i k", i=4),
                                  axis=AX.X)
                sv.inc(op, 1)
                sv.wait(V)
            op = V.reciprocal(out=r4b, in_=l4b)
            recip_cnt[par] = sv.inc(op, 1)
            ss.wait(T, exp_cnt[par])
            if edge:
                sv.wait(T, recip_cnt[par])
            if par in ptb_cnt:
                ss.wait(T, ptb_cnt[par])
            for i in range(4):
                for c in range(2):
                    mm = T.transpose(ppt[:, (i * 2 + c) * 128:(i * 2 + c + 1) * 128],
                                     eb[:, i * 256 + c * 128: i * 256 + c * 128 + 128],
                                     ident_bf[:])
            tr_cnt[par] = st.inc(mm, 1)
            st.wait(S, tr_cnt[par])
            if par in pv_cnt:
                st.wait(S, pv_cnt[par])
            cop = S.copy(out=ptb, in_=ppt[:, :])
            ptb_cnt[par] = ss.inc(cop, 1)
            ss.wait(T, ptb_cnt[par])
            if par in evac_cnt:
                sv.wait(T, evac_cnt[par])
            for i, h in enumerate(heads):
                for c in range(2):
                    mm = T.matmul(ppv[:, i * 128:(i + 1) * 128],
                                  ptb[:, (i * 2 + c) * 128:(i * 2 + c + 1) * 128],
                                  v3[:, b + c, h * 128:(h + 1) * 128],
                                  start=(c == 0), stop=(c == 1))
            pv_cnt[par] = st.inc(mm, 1)
            st.wait(V, pv_cnt[par])
            ctxg = ctx.ap()[:, b * D + g * 512: b * D + (g + 1) * 512]
            op = V.tensor_tensor(out=ctxg.rearrange("p (i d) -> p i d", i=4),
                                 in0=ppv.rearrange("p (i d) -> p i d", i=4),
                                 in1=r4b.unsqueeze(2).broadcast_to([128, 4, 128]),
                                 op=ALU.mult)
            evac_cnt[par] = sv.inc(op, 1)
    es.close()

    esP = ExitStack()

    def sbP(name, shape, dt):
        return esP.enter_context(nc.sbuf_tensor(name, shape, dt))

    gidxn = sbP("gidxn", [128, TPC], F32)
    t_pb = sbP("t_pb", [128, 512], F32)
    iota_t = sbP("iota_t", [64, TPC], F32)
    bc16i = sbP("bc16i", [16, 128], F32)
    bc16 = sbP("bc16", [16, 128], F32)
    iota16 = sbP("iota16", [16, 1], F32)
    pm16 = esP.enter_context(nc.psum_tensor("pm16", [128, 1], F32))
    normsG = sbP("normsG", [128, TPC], F32)
    normsW = sbP("normsW", [128, TPC], F32)
    maskT = sbP("maskT", [128, TPC], U8)
    candA = sbP("candA", [128, 64], F32)
    cand2 = sbP("cand2", [16, 512], F32)
    sort2 = sbP("sort2", [16, 64], F32)
    thr = sbP("thr", [128, 1], F32)
    jrow = sbP("jrow", [128, 5], F32)
    i32a = sbP("i32a", [128, 1], I32)
    idx_g = sbP("idx_g", [16, 64], F32)
    idx_l = sbP("idx_l", [16, 64], F32)
    idxT = sbP("idxT", [64, 16], F32)
    idxflat = sbP("idxflat", [1, NH * 64], F32)
    idx_bc = sbP("idx_bc", [128, NH * 64], F32)
    qg_sb = sbP("qg_sb", [128, NH * 64], BF16)
    q_tmh = sbP("q_tmh", [128, TPC], BF16)
    oh_g = sbP("oh_g", [128, 512], BF16)
    m_all = sbP("m_all", [64, NH], F32)
    nmG = sbP("nmG", [64, NH], F32)
    rloc = sbP("rloc", [64, NH], F32)
    e_g = sbP("e_g", [64, TW], BF16)
    eT_g = sbP("eT_g", [128, 9 * 64], BF16)
    ol = sbP("ol", [64, NH * HD + NH], F32)
    gctx = sbP("gctx", [64, NH * HD], BF16)
    oh_s = sbP("oh_s", [64, 128], BF16)
    cnt_sb = sbP("cnt_sb", [128, NH], F32)
    zq = sbP("zq", [128, NH], F32)
    rq = sbP("rq", [128, NH], F32)
    t1_sb = sbP("t1_sb", [128, D], BF16)

    G.iota(gidxn[:, :], pattern=[[-1, TPC]], base=0, channel_multiplier=0,
           allow_small_or_imprecise_dtypes=True)
    G.iota(t_pb[:, :], pattern=[[128, 8], [0, 64]], base=0, channel_multiplier=1,
           allow_small_or_imprecise_dtypes=True)
    G.iota(iota_t[:, :], pattern=[[1, TPC]], base=0, channel_multiplier=0,
           allow_small_or_imprecise_dtypes=True)
    G.iota(bc16i[:, :], pattern=[[-1, 16], [0, 8]], base=0, channel_multiplier=1,
           allow_small_or_imprecise_dtypes=True)
    G.iota(iota16[:, :], pattern=[[0, 1]], base=0, channel_multiplier=1,
           allow_small_or_imprecise_dtypes=True)
    iop = G.iota(jrow[:, 0:1], pattern=[[0, 1]], base=0, channel_multiplier=1,
                 allow_small_or_imprecise_dtypes=True)
    sg.inc(iop, 1)
    sg.wait(V)
    sg.wait(T)
    sv.inc(V.tensor_single_scalar(out=bc16[:, :], in_=bc16i[:, :], scalar=0.0,
                                  op=ALU.is_equal), 1)
    sv.wait(T)
    mm = T.matmul(pm16[:, 0:1], bc16[:, :], iota16[:, :], start=True, stop=True)
    st.inc(mm, 1)
    st.wait(V)
    sv.inc(V.tensor_copy(out=jrow[:, 2:3], in_=pm16[:, 0:1]), 1)
    sv.wait(V)
    op = V.tensor_scalar(out=jrow[:, 3:4], in0=jrow[:, 2:3], scalar1=8192.0,
                         scalar2=None, op0=ALU.mult)
    sv.inc(op, 1)
    sv.wait(V)
    op = V.scalar_tensor_tensor(out=jrow[:, 4:5], in0=jrow[:, 0:1], scalar=-1024.0,
                                in1=jrow[:, 3:4], op0=ALU.mult, op1=ALU.add)
    sv.inc(op, 1)
    sv.wait(V)
    op = V.tensor_scalar(out=gidxn[:, :], in0=gidxn[:, :], scalar1=jrow[:, 4:5],
                         scalar2=None, op0=ALU.add)
    sv.inc(op, 1)

    scc.wait(G)
    sq.inc(G.dma_start(out=scr_n.ap().rearrange("h (j t) -> j h t", j=8),
                       in_=cc_nrm_out.ap().rearrange("(j h) t -> j h t", h=16)), 16)
    sq.wait(G)
    sq.inc(G.dma_start(out=normsG[:, :],
                       in_=scr_n.ap().rearrange("h (j t) -> (h j) t", j=8)), 16)
    sq.wait(V)
    sv.inc(V.tensor_copy(out=normsW[:, :], in_=normsG[:, :]), 1)
    sv.wait(V)

    def extract64(data, out_tile):
        for r in range(8):
            op = V.max(out=out_tile[:, r * 8:(r + 1) * 8], in_=data)
            sv.inc(op, 1)
            sv.wait(V)
            if r < 7:
                op = V.match_replace(out=data, in_to_replace=out_tile[:, r * 8:(r + 1) * 8],
                                     in_values=data, imm_value=-BIG)
                sv.inc(op, 1)
                sv.wait(V)

    def cand_regroup():
        sv.wait(G)
        sq.inc(G.dma_start(out=scr_c.ap()[:, :], in_=candA[:, :]), 16)
        sq.wait(G)
        sq.inc(G.dma_start(out=cand2[:, :],
                           in_=scr_c.ap().rearrange("(h j) c -> h (j c)", h=16)), 16)
        sq.wait(V)

    extract64(normsW[:, :], candA)
    cand_regroup()
    extract64(cand2[:, :], sort2)
    sv.wait(T)
    mm = T.matmul(pm16[:, 0:1], bc16[:, :], sort2[:, 61:62], start=True, stop=True)
    st.inc(mm, 1)
    st.wait(V)
    sv.inc(V.tensor_copy(out=thr[:, :], in_=pm16[:, 0:1]), 1)
    sv.wait(V)
    op = V.tensor_scalar(out=maskT[:, :], in0=normsG[:, :], scalar1=thr[:, 0:1],
                         scalar2=None, op0=ALU.is_ge)
    sv.inc(op, 1)
    sv.inc(V.memset(normsW[:, :], -BIG), 1)
    sv.wait(V)
    sv.inc(V.copy_predicated(out=normsW[:, :], mask=maskT[:, :], data=gidxn[:, :]), 1)
    sv.wait(V)
    extract64(normsW[:, :], candA)
    cand_regroup()
    extract64(cand2[:, :], sort2)
    sv.wait(V)
    sv.inc(V.memset(idx_g[:, 0:1], 0.0), 1)
    sv.inc(V.memset(idx_g[:, 63:64], 8191.0), 1)
    op = V.tensor_scalar(out=idx_g[:, 1:63], in0=sort2[:, 0:62], scalar1=-1.0,
                         scalar2=None, op0=ALU.mult)
    sv.inc(op, 1)
    sv.wait(V)
    op = V.tensor_scalar(out=idx_l[:, :], in0=idx_g[:, :], scalar1=base16_s[:, 0:1],
                         scalar2=None, op0=ALU.subtract)
    sv.inc(op, 1)
    sv.wait(T)
    sv.wait(G)

    es2 = ExitStack()
    ptf = es2.enter_context(nc.psum_tensor("ptf", [128, 512], F32))
    mm = T.transpose(ptf[0:64, 0:16], idx_l[:, :], ident_f32[0:16, 0:16])
    st.inc(mm, 1)
    st.wait(V)
    sv.inc(V.tensor_copy(out=idxT[:, :], in_=ptf[0:64, 0:16]), 1)
    sq.inc(G.dma_start(out=scr_i.ap().rearrange("o (h j) -> (o h) j", h=16),
                       in_=idx_l[:, :]), 16)
    sq.wait(G)
    sq.inc(G.dma_start(out=idxflat[0:1, :], in_=scr_i.ap()), 16)
    sq.wait(T)
    sv.wait(T)
    for half in range(2):
        mm = T.matmul(ptf[:, 0:512], onesr_f32[0:1, :],
                      idxflat[0:1, half * 512: half * 512 + 512], start=True, stop=True)
        st.inc(mm, 1)
        st.wait(V)
        op = V.tensor_copy(out=idx_bc[:, half * 512: half * 512 + 512], in_=ptf[:, 0:512])
        sv.inc(op, 1)
        sv.wait(T)
    es2.close()

    es2 = ExitStack()
    pqt = es2.enter_context(nc.psum_tensor("pqt", [128, 1024], BF16))
    pqg0 = es2.enter_context(nc.psum_tensor("pqg0", [128, 512], F32))
    pqg1 = es2.enter_context(nc.psum_tensor("pqg1", [128, 512], F32))
    for h in range(NH):
        sv.wait(T)
        for j in range(8):
            mm = T.transpose(pqt[:, j * 128:(j + 1) * 128],
                             q3[:, h, j * 128:(j + 1) * 128], ident_bf[:])
        st.inc(mm, 1)
        st.wait(V)
        sv.inc(V.tensor_copy(out=q_tmh[:, :], in_=pqt[:, :]), 1)
        in0 = idx_bc.ap()[:, h * 64:(h + 1) * 64].unsqueeze(1).broadcast_to([128, 8, 64])
        in1 = t_pb.rearrange("p (b j) -> p b j", b=8)
        op = V.tensor_tensor(out=oh_g.rearrange("p (b j) -> p b j", b=8),
                             in0=in0, in1=in1, op=ALU.is_equal)
        ohcnt = sv.inc(op, 1)
        sv.wait(T, ohcnt)
        pq = pqg0 if h < 8 else pqg1
        off = (h % 8) * 64
        for j in range(8):
            mm = T.matmul(pq[:, off:off + 64], q_tmh[:, j * 128:(j + 1) * 128],
                          oh_g[:, j * 64:(j + 1) * 64],
                          start=(j == 0), stop=(j == 7))
        st.inc(mm, 1)
    st.wait(V)
    sv.inc(V.tensor_copy(out=qg_sb[:, 0:512], in_=pqg0[:, :]), 1)
    sv.inc(V.tensor_copy(out=qg_sb[:, 512:1024], in_=pqg1[:, :]), 1)
    es2.close()
    sv.wait(SY)
    sd.inc(SY.dma_start(out=cc_qg_in.ap()[:, :], in_=qg_sb[:, :]), 16)
    sd.wait(G)
    cc = G.collective_compute("AllReduce", ALU.add, replica_groups=RG,
                              ins=[cc_qg_in.ap().opt()], outs=[cc_qg_out.ap().opt()])
    scc.inc(cc, 1)
    scc.wait(G)
    sq.inc(G.dma_start(out=qg_sb[:, :], in_=cc_qg_out.ap()[:, :]), 16)

    es2 = ExitStack()
    psg = es2.enter_context(nc.psum_tensor("psg", [64, TW], F32))
    pet = es2.enter_context(nc.psum_tensor("pet", [128, 9 * 64], BF16))
    pop = es2.enter_context(nc.psum_tensor("pop", [64, 128], F32))
    qgv = qg_sb.rearrange("p (h j) -> p h j", h=NH)
    sq.wait(T)

    def g_scores(h):
        for (off, nlen) in ((0, 512), (512, 512), (1024, 128)):
            mm = T.matmul(psg[:, off:off + nlen], qgv[:, h, :], k3[:, h, off:off + nlen],
                          start=True, stop=True)
        return mm

    sv.wait(S)
    sv.wait(T)
    tr_g = None
    for h in range(NH):
        if h > 0:
            ss.wait(T)
        mm = g_scores(h)
        st.inc(mm, 1)
        st.wait(S)
        if tr_g is not None:
            st.wait(S, tr_g)
        act = S.activation(out=e_g[:, :], in_=psg[:, :], func=AF.Exp,
                           bias=0.0, scale=c_invsqd[0:64, 0:1])
        ecnt = ss.inc(act, 1)
        ss.wait(V, ecnt)
        sv.inc(V.memset(e_g[:, 0:HALO], 0.0), 1)
        sv.inc(V.memset(e_g[:, HALO + TPC:TW], 0.0), 1)
        sv.wait(V)
        op = V.reduce_sum(out=ol[:, NH * HD + h: NH * HD + h + 1],
                          in_=e_g[:, HALO:HALO + TPC], axis=AX.X)
        sv.inc(op, 1)
        sv.wait(T)
        for j in range(9):
            mm = T.transpose(pet[:, j * 64:(j + 1) * 64],
                             e_g[:, j * 128:(j + 1) * 128], ident_bf[0:64, 0:64])
        tr_g = st.inc(mm, 1)
        st.wait(V, tr_g)
        sv.inc(V.tensor_copy(out=eT_g[:, :], in_=pet[:, :]), 1)
        sv.wait(T)
        for j in range(9):
            mm = T.matmul(pop[:, 0:128], eT_g[:, j * 64:(j + 1) * 64],
                          v3[:, j, h * 128:(h + 1) * 128],
                          start=(j == 0), stop=(j == 8))
        st.inc(mm, 1)
        st.wait(V)
        sv.inc(V.tensor_copy(out=ol[:, h * HD:(h + 1) * HD], in_=pop[:, 0:128]), 1)
    es2.close()
    sv.wait(SY)
    sd.inc(SY.dma_start(out=cc_ol_in.ap()[:, :], in_=ol[:, :]), 16)
    sd.wait(G)
    cc = G.collective_compute("AllReduce", ALU.add, replica_groups=RG,
                              ins=[cc_ol_in.ap().opt()], outs=[cc_ol_out.ap().opt()])
    scc.inc(cc, 1)
    scc.wait(G)
    sq.inc(G.dma_start(out=ol[:, :], in_=cc_ol_out.ap()[:, :]), 16)
    sq.wait(V)
    for h in range(NH):
        sv.inc(V.reciprocal(out=rloc[:, h:h + 1], in_=ol[:, NH * HD + h: NH * HD + h + 1]), 1)
    sv.wait(V)
    op = V.tensor_tensor(out=gctx.rearrange("p (h d) -> p h d", h=NH),
                         in0=ol.ap()[:, 0:NH * HD].rearrange("p (h d) -> p h d", h=NH),
                         in1=rloc.ap().unsqueeze(2).broadcast_to([64, NH, HD]),
                         op=ALU.mult)
    sv.inc(op, 1)

    es2 = ExitStack()
    pscat = es2.enter_context(nc.psum_tensor("pscat", [128, 2048], F32))
    pcnt = es2.enter_context(nc.psum_tensor("pcnt", [128, 16], F32))
    for b in range(NB):
        sv.wait(T)
        for h in range(NH):
            op = V.tensor_scalar(out=oh_s[:, 0:128],
                                 in0=iota_t[:, b * 128:(b + 1) * 128],
                                 scalar1=idxT[:, h:h + 1], scalar2=None, op0=ALU.is_equal)
            ohcnt = sv.inc(op, 1)
            sv.wait(T, ohcnt)
            mm = T.matmul(pscat[:, h * 128:(h + 1) * 128], oh_s[:, 0:128],
                          gctx[:, h * HD:(h + 1) * HD],
                          start=True, stop=True)
            mm2 = T.matmul(pcnt[:, h:h + 1], oh_s[:, 0:128], ones_bf[0:64, 0:1],
                           start=True, stop=True)
            st.inc(mm2, 1)
            st.wait(V)
        op = V.tensor_copy(out=cnt_sb[:, :], in_=pcnt[:, :])
        sv.inc(op, 1)
        sv.wait(V)
        sv.inc(V.tensor_single_scalar(out=zq[:, :], in_=cnt_sb[:, :], scalar=0.0,
                                      op=ALU.is_equal), 1)
        sv.inc(V.tensor_scalar_max(out=rq[:, :], in0=cnt_sb[:, :], scalar1=1.0), 1)
        sv.wait(V)
        sv.inc(V.reciprocal(out=rq[:, :], in_=rq[:, :]), 1)
        sv.wait(V)
        op = V.tensor_tensor(out=t1_sb.rearrange("p (h d) -> p h d", h=NH),
                             in0=pscat.rearrange("p (h d) -> p h d", h=NH),
                             in1=rq.ap().unsqueeze(2).broadcast_to([128, NH, HD]),
                             op=ALU.mult)
        sv.inc(op, 1)
        ctxb = ctx.ap()[:, b * D:(b + 1) * D]
        op = V.tensor_tensor(out=ctxb.rearrange("p (h d) -> p h d", h=NH),
                             in0=ctxb.rearrange("p (h d) -> p h d", h=NH),
                             in1=zq.ap().unsqueeze(2).broadcast_to([128, NH, HD]),
                             op=ALU.mult)
        sv.inc(op, 1)
        sv.wait(V)
        op = V.tensor_tensor(out=ctxb, in0=ctxb, in1=t1_sb[:, :], op=ALU.add)
        sv.inc(op, 1)
    es2.close()
    esP.close()
    esR2.close()

    es = ExitStack()
    xst3 = es.enter_context(nc.sbuf_tensor("xst3", [128, 2 * D], F32, side="right"))
    for i in range(NB):
        slot = i % 2
        xs = xst3[:, slot * D:(slot + 1) * D]
        sv.wait(SY)
        so_sl[slot].wait(SY)
        dcnt = sd.inc(SY.dma_start(out=xs, in_=x_main[i * 128:(i + 1) * 128, :]), 16)
        sd.wait(V, dcnt)
        sv.wait(V)
        op = V.tensor_tensor(out=xs, in0=xs, in1=ctx[:, i * D:(i + 1) * D], op=ALU.add)
        acnt = sv.inc(op, 1)
        sv.wait(SY, acnt)
        so_sl[slot].inc(SY.dma_start(out=xres_dram[i * 128:(i + 1) * 128, :], in_=xs), 16)
    es.close()
    esR1.close()

    esL2 = ExitStack()
    lnx2 = esL2.enter_context(nc.sbuf_tensor("lnx2", [128, KC * TPC], BF16))
    lnx23 = lnx2.rearrange("p (c t) -> p c t", c=KC)
    es = ExitStack()
    xst4 = es.enter_context(nc.sbuf_tensor("xst4", [128, D], F32, side="right"))
    lntm2 = es.enter_context(nc.sbuf_tensor("lntm2", [128, D], BF16, side="right"))
    sqd2 = es.enter_context(nc.sbuf_tensor("sqd2", [128, D], F32, side="right"))
    pt_ln2 = es.enter_context(nc.psum_tensor("pt_ln2", [128, 2048], BF16))
    so_sl[0].wait(SY)
    so_sl[1].wait(SY)
    for i in range(NB):
        sv.wait(SY)
        dcnt = sd.inc(SY.dma_start(out=xst4[:, :], in_=xres_dram[i * 128:(i + 1) * 128, :]), 16)
        sd.wait(V, dcnt)
        xs = xst4[:, :]
        V.reduce_sum(out=ln_s[:, i:i + 1], in_=xs, axis=AX.X)
        op = V.scalar_tensor_tensor(out=sqd2[:, :], in0=xs, scalar=1.0, in1=xs,
                                    op0=ALU.mult, op1=ALU.mult, accum_out=ln_q[:, i:i + 1])
        sv.inc(op, 1)
        sv.wait(V)
        op = V.tensor_scalar(out=ln_m[:, i:i + 1], in0=ln_s[:, i:i + 1],
                             scalar1=c_nmean[:, 0:1], scalar2=None, op0=ALU.mult)
        sv.inc(op, 1)
        sv.wait(V)
        op = V.tensor_mul(out=ln_sd[:, i:i + 1], in0=ln_m[:, i:i + 1], in1=ln_m[:, i:i + 1])
        sv.inc(op, 1)
        sv.wait(V)
        op = V.scalar_tensor_tensor(out=ln_v[:, i:i + 1], in0=ln_q[:, i:i + 1],
                                    scalar=1.0 / float(D), in1=ln_sd[:, i:i + 1],
                                    op0=ALU.mult, op1=ALU.subtract)
        sv.inc(op, 1)
        sv.wait(S)
        op = S.activation(out=ln_sd[:, i:i + 1], in_=ln_v[:, i:i + 1], func=AF.Sqrt,
                          bias=c_eps[:, 0:1], scale=1.0)
        ss.inc(op, 1)
        ss.wait(V)
        op = V.reciprocal(out=ln_r[:, i:i + 1], in_=ln_sd[:, i:i + 1])
        sv.inc(op, 1)
        sv.wait(V)
        op = V.tensor_scalar(out=lntm2[:, :], in0=xs, scalar1=ln_m[:, i:i + 1],
                             scalar2=ln_r[:, i:i + 1], op0=ALU.add, op1=ALU.mult)
        sv.inc(op, 1)
        sv.wait(T)
        for kc in range(KC):
            mm = T.transpose(pt_ln2[:, kc * 128:(kc + 1) * 128],
                             lntm2[:, kc * 128:(kc + 1) * 128], ident_bf[:])
        st.inc(mm, 1)
        st.wait(V)
        op = V.tensor_copy(out=lnx23[:, :, i * 128:(i + 1) * 128],
                           in_=pt_ln2.rearrange("p (c t) -> p c t", c=KC))
        sv.inc(op, 1)
    es.close()

    es = ExitStack()
    hbuf = es.enter_context(nc.sbuf_tensor("hbuf", [128, ICN * 512], BF16, side="right"))
    gu_u8s = es.enter_context(nc.sbuf_tensor("gu_u8s", [128, 4 * D], U8, side="right"))
    gu_bf = es.enter_context(nc.sbuf_tensor("gu_bf", [128, 4 * D], BF16, side="right"))
    wd_u8s = es.enter_context(nc.sbuf_tensor("wd_u8s", [128, 2 * 512], U8, side="right"))
    wd_bf = es.enter_context(nc.sbuf_tensor("wd_bf", [128, 2 * 512], BF16, side="right"))
    silu_s = es.enter_context(nc.sbuf_tensor("silu_s", [128, 2 * 512], F32, side="right"))
    gsc_s = es.enter_context(nc.sbuf_tensor("gsc_s", [128, ICN], F32, side="right"))
    usc_s = es.enter_context(nc.sbuf_tensor("usc_s", [128, ICN], F32, side="right"))
    dscb_s = es.enter_context(nc.sbuf_tensor("dscb_s", [128, D], F32, side="right"))
    c_nzp = es.enter_context(nc.sbuf_tensor("c_nzp", [128, 1], F32, side="right"))
    xr_st = es.enter_context(nc.sbuf_tensor("xr_st", [128, 2 * 512], F32, side="right"))
    out_st = es.enter_context(nc.sbuf_tensor("out_st", [128, 2 * 512], F32, side="right"))
    pgs = [es.enter_context(nc.psum_tensor(f"pg{i}", [128, 512], F32)) for i in range(2)]
    pus = [es.enter_context(nc.psum_tensor(f"pu{i}", [128, 512], F32)) for i in range(2)]
    pds = [es.enter_context(nc.psum_tensor(f"pd{i}", [128, 512], F32)) for i in range(4)]

    sv.wait(SY)
    st.wait(SY)
    sd.inc(SY.dma_start(out=gsc_s[:], in_=gsc[:, :]), 16)
    sd.inc(SY.dma_start(out=usc_s[:], in_=usc[:, :]), 16)
    dcnt = sd.inc(SY.dma_start(out=dscb_s[:], in_=dscb[:, :]), 16)
    sv.wait(V)
    sv.inc(V.memset(c_nzp[:], -128.0), 1)
    sd.wait(S, dcnt)
    sd.wait(V, dcnt)
    sv.wait(S)

    guw_gcast = {}
    guw_ucast = {}
    guw_mm = {}
    silu_cnt = {}
    hm_cnt = {}
    wd_cast = {}
    wd_mm = {}
    for ph in range(2):
        tok0 = ph * 512
        st.wait(V)
        st.wait(S)
        for ic in range(ICN):
            slot = ic % 2
            if slot in guw_gcast:
                ss.wait(SY, guw_gcast[slot])
            if slot in guw_ucast:
                sv.wait(SY, guw_ucast[slot])
            s_w[slot].inc(SY.dma_start(out=gu_u8s[:, (2 * slot) * D:(2 * slot + 1) * D],
                                       in_=g_u8[ic, :, :]), 16)
            d2 = SY.dma_start(out=gu_u8s[:, (2 * slot + 1) * D:(2 * slot + 2) * D],
                              in_=u_u8[ic, :, :])
            s_w[slot].inc(d2, 16)
            s_w[slot].wait(S)
            s_w[slot].wait(V)
            if slot in guw_mm:
                st.wait(S, guw_mm[slot])
                st.wait(V, guw_mm[slot])
            a1 = S.activation(out=gu_bf[:, (2 * slot) * D:(2 * slot + 1) * D],
                              in_=gu_u8s[:, (2 * slot) * D:(2 * slot + 1) * D],
                              func=AF.Copy, bias=-128.0, scale=1.0)
            guw_gcast[slot] = ss.inc(a1, 1)
            o1 = V.tensor_scalar(out=gu_bf[:, (2 * slot + 1) * D:(2 * slot + 2) * D],
                                 in0=gu_u8s[:, (2 * slot + 1) * D:(2 * slot + 2) * D],
                                 scalar1=-128.0, scalar2=None, op0=ALU.add)
            guw_ucast[slot] = sv.inc(o1, 1)
            ss.wait(T, guw_gcast[slot])
            sv.wait(T, guw_ucast[slot])
            if slot in silu_cnt:
                ss.wait(T, silu_cnt[slot])
            if slot in hm_cnt:
                sv.wait(T, hm_cnt[slot])
            gb = gu_bf.rearrange("p (s c f) -> p s c f", s=4, c=KC)
            for kc in range(KC):
                mm = T.matmul(pgs[slot][:, :], gb[:, 2 * slot, kc, :],
                              lnx23[:, kc, tok0:tok0 + 512],
                              start=(kc == 0), stop=(kc == KC - 1))
            for kc in range(KC):
                mm = T.matmul(pus[slot][:, :], gb[:, 2 * slot + 1, kc, :],
                              lnx23[:, kc, tok0:tok0 + 512],
                              start=(kc == 0), stop=(kc == KC - 1))
            guw_mm[slot] = st.inc(mm, 1)
            st.wait(S, guw_mm[slot])
            if slot in hm_cnt:
                sv.wait(S, hm_cnt[slot])
            act = S.activation(out=silu_s[:, slot * 512:(slot + 1) * 512],
                               in_=pgs[slot][:, :], func=AF.Silu,
                               bias=0.0, scale=gsc_s[:, ic:ic + 1])
            silu_cnt[slot] = ss.inc(act, 1)
            ss.wait(V, silu_cnt[slot])
            op = V.scalar_tensor_tensor(out=hbuf[:, ic * 512:(ic + 1) * 512],
                                        in0=pus[slot][:, :], scalar=usc_s[:, ic:ic + 1],
                                        in1=silu_s[:, slot * 512:(slot + 1) * 512],
                                        op0=ALU.mult, op1=ALU.mult)
            hm_cnt[slot] = sv.inc(op, 1)
        for dc in range(4):
            for ic in range(ICN):
                slot = ic % 2
                if slot in wd_cast:
                    ss.wait(SY, wd_cast[slot])
                d1 = SY.dma_start(out=wd_u8s[:, slot * 512:(slot + 1) * 512],
                                  in_=d_u8[ic, dc, :, :])
                s_d[slot].inc(d1, 16)
                s_d[slot].wait(S)
                if slot in wd_mm:
                    st.wait(S, wd_mm[slot])
                a1 = S.activation(out=wd_bf[:, slot * 512:(slot + 1) * 512],
                                  in_=wd_u8s[:, slot * 512:(slot + 1) * 512],
                                  func=AF.Copy, bias=-128.0, scale=1.0)
                wd_cast[slot] = ss.inc(a1, 1)
                ss.wait(T, wd_cast[slot])
                if ic == 0:
                    sv.wait(T)
                for tc in range(4):
                    mm = T.matmul(pds[tc][:, :],
                                  hbuf[:, ic * 512 + tc * 128: ic * 512 + tc * 128 + 128],
                                  wd_bf[:, slot * 512:(slot + 1) * 512],
                                  start=(ic == 0), stop=(ic == ICN - 1))
                wd_mm[slot] = st.inc(mm, 1)
            st.wait(V)
            for tc in range(4):
                gtc = ph * 4 + tc
                slot = tc % 2
                sv.wait(SY)
                dcnt = sd.inc(SY.dma_start(out=xr_st[:, slot * 512:(slot + 1) * 512],
                                           in_=xres_dram[gtc * 128:(gtc + 1) * 128,
                                                         dc * 512:(dc + 1) * 512]), 16)
                sd.wait(V, dcnt)
                sv.wait(V)
                so_sl[slot].wait(V)
                op = V.tensor_tensor(out=out_st[:, slot * 512:(slot + 1) * 512],
                                     in0=pds[tc][:, :], in1=dscb_s[:, dc * 512:(dc + 1) * 512],
                                     op=ALU.mult)
                sv.inc(op, 1)
                sv.wait(V)
                op = V.tensor_tensor(out=out_st[:, slot * 512:(slot + 1) * 512],
                                     in0=out_st[:, slot * 512:(slot + 1) * 512],
                                     in1=xr_st[:, slot * 512:(slot + 1) * 512],
                                     op=ALU.add)
                ocnt = sv.inc(op, 1)
                sv.wait(SY, ocnt)
                d = SY.dma_start(out=y[gtc * 128:(gtc + 1) * 128, dc * 512:(dc + 1) * 512],
                                 in_=out_st[:, slot * 512:(slot + 1) * 512])
                so_sl[slot].inc(d, 16)
    es.close()
    esL2.close()
    esK.close()

    sd.wait(SY)
    so_sl[0].wait(SY)
    so_sl[1].wait(SY)
    nc.all_engine_barrier()
    return nc


def _prep_inputs(inputs):
    x = np.asarray(inputs["L0_attn_hidden_states"]).reshape(SEQ, D)
    Wq = np.asarray(inputs["L0_attn_Wq"])
    Wk = np.asarray(inputs["L0_attn_Wk"])
    Wv = np.asarray(inputs["L0_attn_Wv"])
    bqv = np.asarray(inputs["L0_attn_bq"])
    bkv = np.asarray(inputs["L0_attn_bk"])
    bvv = np.asarray(inputs["L0_attn_bv"])
    gw = np.asarray(inputs["L0_ffn_gate_w"])
    uw = np.asarray(inputs["L0_ffn_up_w"])
    dw = np.asarray(inputs["L0_ffn_down_w"])
    gs = np.asarray(inputs["L0_ffn_gate_scale"])
    us = np.asarray(inputs["L0_ffn_up_scale"])
    ds = np.asarray(inputs["L0_ffn_down_scale"])

    def tile_w(W):
        t = W.reshape(KC, 128, KC, 128).transpose(2, 0, 1, 3)
        t = t.reshape(KC, 2, 8, 128, 128).transpose(0, 1, 3, 2, 4)
        return np.ascontiguousarray(t.reshape(KC, 2, 128, 1024))

    def u8_of(Wi32):
        return Wi32.view(np.uint8).reshape(*Wi32.shape, 4)[..., 0]

    g8 = u8_of(gw)
    u8a = u8_of(uw)
    d8 = u8_of(dw)
    g_t = g8.T.reshape(KC, 128, ICN, 128).transpose(2, 0, 1, 3)
    g_t = np.ascontiguousarray(g_t.transpose(0, 2, 1, 3).reshape(ICN, 128, D))
    u_t = u8a.T.reshape(KC, 128, ICN, 128).transpose(2, 0, 1, 3)
    u_t = np.ascontiguousarray(u_t.transpose(0, 2, 1, 3).reshape(ICN, 128, D))
    d_t = np.ascontiguousarray(d8.T.reshape(ICN, 128, 4, 512).transpose(0, 2, 1, 3))

    bq_s = np.ascontiguousarray(bqv.reshape(KC, 128).T)
    bk_s = np.ascontiguousarray(bkv.reshape(KC, 128).T)
    bv_s = np.ascontiguousarray(bvv.reshape(KC, 128).T)
    gsc_a = np.ascontiguousarray(gs.reshape(ICN, 128).T)
    usc_a = np.ascontiguousarray(us.reshape(ICN, 128).T)
    dscb_a = np.ascontiguousarray(np.broadcast_to(ds.reshape(1, D), (128, D)))
    wq_t, wk_t, wv_t = tile_w(Wq), tile_w(Wk), tile_w(Wv)

    in_maps = []
    for c in range(NCORES):
        t0 = c * TPC
        hl = x[t0 - HALO: t0] if t0 - HALO >= 0 else x[0:HALO]
        hr = x[t0 + TPC: t0 + TPC + HALO] if t0 + TPC + HALO <= SEQ else x[SEQ - HALO:SEQ]
        in_maps.append({
            "x_main": np.ascontiguousarray(x[t0:t0 + TPC]),
            "x_hl": np.ascontiguousarray(hl),
            "x_hr": np.ascontiguousarray(hr),
            "wq": wq_t, "wk": wk_t, "wv": wv_t,
            "bq": bq_s, "bk": bk_s, "bv": bv_s,
            "g_u8": g_t, "u_u8": u_t, "d_u8": d_t,
            "gsc": gsc_a, "usc": usc_a, "dscb": dscb_a,
            "mask_l": np.full((128, 1), 0.0 if c == 0 else 1.0, np.float32),
            "mask_r": np.full((128, 1), 0.0 if c == NCORES - 1 else 1.0, np.float32),
            "base16": np.full((16, 1), float(t0), np.float32),
        })
    return in_maps


def kernel(**inputs):
    in_maps = _prep_inputs(inputs)
    nc = build_nc()
    res = run_bass_kernel_spmd(nc, in_maps, core_ids=list(range(NCORES)))
    out = np.concatenate([res.results[c]["y"] for c in range(NCORES)], axis=0)
    return out.reshape(1, SEQ, D).astype(np.float32)


# version 11
# speedup vs baseline: 1.1885x; 1.0741x over previous
from contextlib import ExitStack

import numpy as np
import concourse.bass as bass
import concourse.mybir as mybir
from concourse.bass_utils import run_bass_kernel_spmd

F32 = mybir.dt.float32
BF16 = mybir.dt.bfloat16
U8 = mybir.dt.uint8
I32 = mybir.dt.int32
AF = mybir.ActivationFunctionType
ALU = mybir.AluOpType
AX = mybir.AxisListType

NCORES = 8
SEQ = 8192
D = 2048
NH = 16
HD = 128
TPC = 1024
HALO = 64
TW = TPC + 2 * HALO
NB = TPC // 128
KC = D // 128
II = 8192
ICN = II // 128
INV_SQRT_D = float(1.0 / np.sqrt(128.0))
BIG = 1.0e30


class Sem:
    def __init__(self, nc, name):
        self.h = nc.alloc_semaphore(name=name)
        self.n = 0

    def inc(self, instr, k=1):
        instr.then_inc(self.h, k)
        self.n += k
        return self.n

    def wait(self, eng, val=None):
        eng.wait_ge(self.h, self.n if val is None else val)


def build_nc():
    nc = bass.Bass("TRN2", target_bir_lowering=False)

    x_main = nc.dram_tensor("x_main", [TPC, D], F32, kind="ExternalInput").ap()
    x_hl = nc.dram_tensor("x_hl", [HALO, D], F32, kind="ExternalInput").ap()
    x_hr = nc.dram_tensor("x_hr", [HALO, D], F32, kind="ExternalInput").ap()
    wq = nc.dram_tensor("wq", [KC, 2, 128, 1024], F32, kind="ExternalInput").ap()
    wk = nc.dram_tensor("wk", [KC, 2, 128, 1024], F32, kind="ExternalInput").ap()
    wv = nc.dram_tensor("wv", [KC, 2, 128, 1024], F32, kind="ExternalInput").ap()
    bq = nc.dram_tensor("bq", [128, KC], F32, kind="ExternalInput").ap()
    bk = nc.dram_tensor("bk", [128, KC], F32, kind="ExternalInput").ap()
    bv = nc.dram_tensor("bv", [128, KC], F32, kind="ExternalInput").ap()
    g_u8 = nc.dram_tensor("g_u8", [ICN, 128, D], U8, kind="ExternalInput").ap()
    u_u8 = nc.dram_tensor("u_u8", [ICN, 128, D], U8, kind="ExternalInput").ap()
    d_u8 = nc.dram_tensor("d_u8", [ICN, 4, 128, 512], U8, kind="ExternalInput").ap()
    gsc = nc.dram_tensor("gsc", [128, ICN], F32, kind="ExternalInput").ap()
    usc = nc.dram_tensor("usc", [128, ICN], F32, kind="ExternalInput").ap()
    dscb = nc.dram_tensor("dscb", [128, D], F32, kind="ExternalInput").ap()
    mask_l = nc.dram_tensor("mask_l", [128, 1], F32, kind="ExternalInput").ap()
    mask_r = nc.dram_tensor("mask_r", [128, 1], F32, kind="ExternalInput").ap()
    base16 = nc.dram_tensor("base16", [16, 1], F32, kind="ExternalInput").ap()

    y = nc.dram_tensor("y", [TPC, D], F32, kind="ExternalOutput").ap()

    xres_dram = nc.dram_tensor("xres_dram", [TPC, D], F32).ap()
    scr_c = nc.dram_tensor("scr_c", [128, 64], F32)
    scr_i = nc.dram_tensor("scr_i", [1, NH * 64], F32)
    scr_n = nc.dram_tensor("scr_n", [16, 8 * TPC], F32)
    cc_nrm_in = nc.dram_tensor("cc_nrm_in", [16, TPC], F32)
    cc_nrm_out = nc.dram_tensor("cc_nrm_out", [128, TPC], F32)
    cc_qg_in = nc.dram_tensor("cc_qg_in", [128, NH * 64], BF16)
    cc_qg_out = nc.dram_tensor("cc_qg_out", [128, NH * 64], BF16)
    cc_m_in = nc.dram_tensor("cc_m_in", [64, NH], F32)
    cc_m_out = nc.dram_tensor("cc_m_out", [64, NH], F32)
    cc_ol_in = nc.dram_tensor("cc_ol_in", [64, NH * HD + NH], F32)
    cc_ol_out = nc.dram_tensor("cc_ol_out", [64, NH * HD + NH], F32)
    RG = [list(range(NCORES))]

    nc.gpsimd.dma_reset(range(155, 256))
    nc.gpsimd.sem_clear(range(155, 256))
    nc.all_engine_barrier()

    sv = Sem(nc, "sv")
    ss = Sem(nc, "ss")
    st = Sem(nc, "st")
    sd = Sem(nc, "sd")
    sq = Sem(nc, "sq")
    sg = Sem(nc, "sg")
    scc = Sem(nc, "scc")
    s_w = [Sem(nc, "sw0"), Sem(nc, "sw1")]
    s_d = [Sem(nc, "sd0"), Sem(nc, "sd1"),
           Sem(nc, "sd2"), Sem(nc, "sd3")]
    so_sl = [Sem(nc, "so0"), Sem(nc, "so1")]

    V, S, T, G, SY = nc.vector, nc.scalar, nc.tensor, nc.gpsimd, nc.sync

    esK = ExitStack()

    def sbK(name, shape, dt):
        return esK.enter_context(nc.sbuf_tensor(name, shape, dt))

    ident_bf = sbK("ident_bf", [128, 128], BF16)
    ident_f32 = sbK("ident_f32", [128, 128], F32)
    iota_pf = sbK("iota_pf", [128, 128], F32)
    c_invsqd = sbK("c_invsqd", [128, 1], F32)
    c_eps = sbK("c_eps", [128, 1], F32)
    c_nmean = sbK("c_nmean", [128, 1], F32)
    ones_f32 = sbK("ones_f32", [128, 1], F32)
    ones_bf = sbK("ones_bf", [128, 1], BF16)
    onesr_f32 = sbK("onesr_f32", [1, 128], F32)
    bq_s = sbK("bq_s", [128, KC], F32)
    bk_s = sbK("bk_s", [128, KC], F32)
    bv_s = sbK("bv_s", [128, KC], F32)
    maskl_s = sbK("maskl_s", [128, 1], F32)
    maskr_s = sbK("maskr_s", [128, 1], F32)
    base16_s = sbK("base16_s", [16, 1], F32)
    ln_s = sbK("ln_s", [128, 10], F32)
    ln_q = sbK("ln_q", [128, 10], F32)
    ln_v = sbK("ln_v", [128, 10], F32)
    ln_sd = sbK("ln_sd", [128, 10], F32)
    ln_r = sbK("ln_r", [128, 10], F32)
    ln_m = sbK("ln_m", [128, 10], F32)

    iop = G.iota(iota_pf[:, :], pattern=[[-1, 128]], base=0, channel_multiplier=1,
                 allow_small_or_imprecise_dtypes=True)
    sg.inc(iop, 1)
    s_d[0].inc(SY.dma_start(out=bq_s[:], in_=bq[:, :]), 16)
    s_d[0].inc(SY.dma_start(out=bk_s[:], in_=bk[:, :]), 16)
    s_d[0].inc(SY.dma_start(out=bv_s[:], in_=bv[:, :]), 16)
    s_d[0].inc(SY.dma_start(out=maskl_s[:], in_=mask_l[:, :]), 16)
    s_d[0].inc(SY.dma_start(out=maskr_s[:], in_=mask_r[:, :]), 16)
    s_d[0].inc(SY.dma_start(out=base16_s[:], in_=base16[:, :]), 16)
    V.memset(c_invsqd[:], INV_SQRT_D)
    V.memset(c_eps[:], 1.0e-5)
    V.memset(c_nmean[:], -1.0 / float(D))
    V.memset(ones_f32[:], 1.0)
    V.memset(ones_bf[:], 1.0)
    sv.inc(V.memset(onesr_f32[:], 1.0), 1)
    sg.wait(V)
    V.tensor_single_scalar(out=ident_bf[:], in_=iota_pf[:], scalar=0.0, op=ALU.is_equal)
    sv.inc(V.tensor_single_scalar(out=ident_f32[:], in_=iota_pf[:], scalar=0.0, op=ALU.is_equal), 1)
    s_d[0].wait(V)

    esR1 = ExitStack()
    ctx = esR1.enter_context(nc.sbuf_tensor("ctx", [128, NB * D], BF16, side="right"))
    esR2 = ExitStack()
    q_fm = esR2.enter_context(nc.sbuf_tensor("q_fm", [128, NH * TPC], BF16, side="right"))
    k_fm = esR2.enter_context(nc.sbuf_tensor("k_fm", [128, NH * TW], BF16, side="right"))
    v_tm = esR2.enter_context(nc.sbuf_tensor("v_tm", [128, 9 * D], BF16, side="right"))
    q3 = q_fm.rearrange("p (h t) -> p h t", h=NH)
    k3 = k_fm.rearrange("p (h t) -> p h t", h=NH)
    v3 = v_tm.rearrange("p (j d) -> p j d", j=9)

    esLN = ExitStack()
    lnx = esLN.enter_context(nc.sbuf_tensor("lnx", [128, KC * TW], BF16, side="right"))
    lnx3 = lnx.rearrange("p (c t) -> p c t", c=KC)

    es = ExitStack()
    xstage = es.enter_context(nc.sbuf_tensor("xstage", [128, D], F32, side="right"))
    lntm = es.enter_context(nc.sbuf_tensor("lntm", [128, D], BF16, side="right"))
    sqd = es.enter_context(nc.sbuf_tensor("sqd", [128, D], F32, side="right"))
    pt_ln = es.enter_context(nc.psum_tensor("pt_ln", [128, 2048], BF16))

    def ln_tile(i, nrow, src_ap, out3, dst_col0, xs, sq_t, lntm_t, pt):
        sv.wait(SY)
        dcnt = sd.inc(SY.dma_start(out=xs, in_=src_ap), 16)
        sd.wait(V, dcnt)
        V.reduce_sum(out=ln_s[0:nrow, i:i + 1], in_=xs, axis=AX.X)
        op = V.scalar_tensor_tensor(out=sq_t, in0=xs, scalar=1.0, in1=xs,
                                    op0=ALU.mult, op1=ALU.mult,
                                    accum_out=ln_q[0:nrow, i:i + 1])
        sv.inc(op, 1)
        sv.wait(V)
        op = V.tensor_scalar(out=ln_m[0:nrow, i:i + 1], in0=ln_s[0:nrow, i:i + 1],
                             scalar1=c_nmean[0:nrow, :], scalar2=None, op0=ALU.mult)
        sv.inc(op, 1)
        sv.wait(V)
        op = V.tensor_mul(out=ln_sd[0:nrow, i:i + 1], in0=ln_m[0:nrow, i:i + 1],
                          in1=ln_m[0:nrow, i:i + 1])
        sv.inc(op, 1)
        sv.wait(V)
        op = V.scalar_tensor_tensor(out=ln_v[0:nrow, i:i + 1], in0=ln_q[0:nrow, i:i + 1],
                                    scalar=1.0 / float(D), in1=ln_sd[0:nrow, i:i + 1],
                                    op0=ALU.mult, op1=ALU.subtract)
        sv.inc(op, 1)
        sv.wait(S)
        op = S.activation(out=ln_sd[0:nrow, i:i + 1], in_=ln_v[0:nrow, i:i + 1],
                          func=AF.Sqrt, bias=c_eps[0:nrow, :], scale=1.0)
        ss.inc(op, 1)
        ss.wait(V)
        op = V.reciprocal(out=ln_r[0:nrow, i:i + 1], in_=ln_sd[0:nrow, i:i + 1])
        sv.inc(op, 1)
        sv.wait(V)
        op = V.tensor_scalar(out=lntm_t, in0=xs, scalar1=ln_m[0:nrow, i:i + 1],
                             scalar2=ln_r[0:nrow, i:i + 1], op0=ALU.add, op1=ALU.mult)
        sv.inc(op, 1)
        sv.wait(T)
        for kc in range(KC):
            mm = T.transpose(pt[:, kc * 128: kc * 128 + nrow],
                             lntm_t[0:nrow, kc * 128:(kc + 1) * 128] if False
                             else lntm_t.tensor[0:nrow, kc * 128:(kc + 1) * 128]
                             if hasattr(lntm_t, "tensor") else lntm_t,
                             ident_bf[:])
        st.inc(mm, 1)
        st.wait(V)
        out_ap = out3[:, :, dst_col0:dst_col0 + nrow]
        in_ap = pt.rearrange("p (c t) -> p c t", c=KC)[:, :, 0:nrow]
        sv.inc(V.tensor_copy(out=out_ap, in_=in_ap), 1)

    def ln_tile2(i, nrow, src_ap, out3, dst_col0):
        xs = xstage[0:nrow, 0:D]
        sv.wait(SY)
        dcnt = sd.inc(SY.dma_start(out=xs, in_=src_ap), 16)
        sd.wait(V, dcnt)
        V.reduce_sum(out=ln_s[0:nrow, i:i + 1], in_=xs, axis=AX.X)
        op = V.scalar_tensor_tensor(out=sqd[0:nrow, :], in0=xs, scalar=1.0, in1=xs,
                                    op0=ALU.mult, op1=ALU.mult,
                                    accum_out=ln_q[0:nrow, i:i + 1])
        sv.inc(op, 1)
        sv.wait(V)
        op = V.tensor_scalar(out=ln_m[0:nrow, i:i + 1], in0=ln_s[0:nrow, i:i + 1],
                             scalar1=c_nmean[0:nrow, :], scalar2=None, op0=ALU.mult)
        sv.inc(op, 1)
        sv.wait(V)
        op = V.tensor_mul(out=ln_sd[0:nrow, i:i + 1], in0=ln_m[0:nrow, i:i + 1],
                          in1=ln_m[0:nrow, i:i + 1])
        sv.inc(op, 1)
        sv.wait(V)
        op = V.scalar_tensor_tensor(out=ln_v[0:nrow, i:i + 1], in0=ln_q[0:nrow, i:i + 1],
                                    scalar=1.0 / float(D), in1=ln_sd[0:nrow, i:i + 1],
                                    op0=ALU.mult, op1=ALU.subtract)
        sv.inc(op, 1)
        sv.wait(S)
        op = S.activation(out=ln_sd[0:nrow, i:i + 1], in_=ln_v[0:nrow, i:i + 1],
                          func=AF.Sqrt, bias=c_eps[0:nrow, :], scale=1.0)
        ss.inc(op, 1)
        ss.wait(V)
        op = V.reciprocal(out=ln_r[0:nrow, i:i + 1], in_=ln_sd[0:nrow, i:i + 1])
        sv.inc(op, 1)
        sv.wait(V)
        op = V.tensor_scalar(out=lntm[0:nrow, 0:D], in0=xs, scalar1=ln_m[0:nrow, i:i + 1],
                             scalar2=ln_r[0:nrow, i:i + 1], op0=ALU.add, op1=ALU.mult)
        sv.inc(op, 1)
        sv.wait(T)
        for kc in range(KC):
            mm = T.transpose(pt_ln[:, kc * 128: kc * 128 + nrow],
                             lntm[0:nrow, kc * 128:(kc + 1) * 128],
                             ident_bf[0:nrow, 0:nrow])
        st.inc(mm, 1)
        st.wait(V)
        out_ap = out3[:, :, dst_col0:dst_col0 + nrow]
        in_ap = pt_ln.rearrange("p (c t) -> p c t", c=KC)[:, :, 0:nrow]
        sv.inc(V.tensor_copy(out=out_ap, in_=in_ap), 1)

    ln_tile2(8, HALO, x_hl[:, :], lnx3, 0)
    ln_tile2(9, HALO, x_hr[:, :], lnx3, HALO + TPC)
    for i in range(NB):
        ln_tile2(i, 128, x_main[i * 128:(i + 1) * 128, :], lnx3, HALO + i * 128)
    es.close()

    es = ExitStack()
    wstage = es.enter_context(nc.sbuf_tensor("wstage", [128, 2 * 1024], F32, side="right"))
    wbf = es.enter_context(nc.sbuf_tensor("wbf", [128, 2 * 1024], BF16, side="right"))
    vfm_s = es.enter_context(nc.sbuf_tensor("vfm_s", [128, TW], BF16, side="right"))
    sqb = es.enter_context(nc.sbuf_tensor("sqb", [128, 512], F32, side="right"))
    norms16 = es.enter_context(nc.sbuf_tensor("norms16", [16, TPC], F32, side="right"))
    pq0 = es.enter_context(nc.psum_tensor("pq0", [128, 512], F32))
    pq1 = es.enter_context(nc.psum_tensor("pq1", [128, 512], F32))
    pk2 = es.enter_context(nc.psum_tensor("pk2", [128, 128], F32))
    pt_v = es.enter_context(nc.psum_tensor("pt_v", [128, 9 * 128], BF16))

    wbf3 = wbf.rearrange("p (s j f) -> p s j f", s=2, j=8)
    slot_cast = {}
    slot_mm = {}
    gslot = [0]

    def stage_half(src, mc, wh, cast_eng, cast_sem):
        slot = gslot[0] % 2
        gslot[0] += 1
        if slot in slot_cast:
            sem_prev, ccnt_prev = slot_cast[slot]
            sem_prev.wait(SY, ccnt_prev)
        s_w[slot].inc(SY.dma_start(out=wstage[:, slot * 1024:(slot + 1) * 1024],
                                   in_=src[mc, wh, :, :]), 16)
        s_w[slot].wait(cast_eng)
        if slot in slot_mm:
            st.wait(cast_eng, slot_mm[slot])
        if cast_eng is S:
            op = cast_eng.copy(out=wbf[:, slot * 1024:(slot + 1) * 1024],
                               in_=wstage[:, slot * 1024:(slot + 1) * 1024])
        else:
            op = cast_eng.tensor_copy(out=wbf[:, slot * 1024:(slot + 1) * 1024],
                                      in_=wstage[:, slot * 1024:(slot + 1) * 1024])
        ccnt = cast_sem.inc(op, 1)
        slot_cast[slot] = (cast_sem, ccnt)
        return slot, ccnt

    sv.wait(SY)
    for mc in range(KC):
        sv.wait(T)
        for wh in range(2):
            slot, ccnt = stage_half(wq, mc, wh, V, sv)
            sv.wait(T, ccnt)
            for half in range(2):
                ps = pq0 if half == 0 else pq1
                for j in range(8):
                    kc = wh * 8 + j
                    mm = T.matmul(ps[:, 0:512], wbf3[:, slot, j, :],
                                  lnx3[:, kc, HALO + half * 512: HALO + half * 512 + 512],
                                  start=(kc == 0), stop=(kc == KC - 1))
            st.inc(mm, 1)
            slot_mm[slot] = st.n
        st.wait(V)
        for half in range(2):
            ps = pq0 if half == 0 else pq1
            op = V.tensor_scalar(out=q3[:, mc, half * 512: half * 512 + 512],
                                 in0=ps[:, 0:512], scalar1=bq_s[:, mc:mc + 1],
                                 scalar2=None, op0=ALU.add)
            sv.inc(op, 1)

    for mc in range(KC):
        sv.wait(T)
        for wh in range(2):
            slot, ccnt = stage_half(wk, mc, wh, S, ss)
            ss.wait(T, ccnt)
            for (ps, off, nlen) in ((pq0, 0, 512), (pq1, 512, 512), (pk2, 1024, 128)):
                for j in range(8):
                    kc = wh * 8 + j
                    mm = T.matmul(ps[:, 0:nlen], wbf3[:, slot, j, :],
                                  lnx3[:, kc, off:off + nlen],
                                  start=(kc == 0), stop=(kc == KC - 1))
            st.inc(mm, 1)
            slot_mm[slot] = st.n
        st.wait(V)
        for (ps, off, nlen) in ((pq0, 0, 512), (pq1, 512, 512), (pk2, 1024, 128)):
            op = V.tensor_scalar(out=k3[:, mc, off:off + nlen],
                                 in0=ps[:, 0:nlen], scalar1=bk_s[:, mc:mc + 1],
                                 scalar2=None, op0=ALU.add)
            sv.inc(op, 1)

    for mc in range(KC):
        sv.wait(T)
        for wh in range(2):
            slot, ccnt = stage_half(wv, mc, wh, S, ss)
            ss.wait(T, ccnt)
            for (ps, off, nlen) in ((pq0, 0, 512), (pq1, 512, 512), (pk2, 1024, 128)):
                for j in range(8):
                    kc = wh * 8 + j
                    mm = T.matmul(ps[:, 0:nlen], wbf3[:, slot, j, :],
                                  lnx3[:, kc, off:off + nlen],
                                  start=(kc == 0), stop=(kc == KC - 1))
            st.inc(mm, 1)
            slot_mm[slot] = st.n
        st.wait(V)
        for (ps, off, nlen) in ((pq0, 0, 512), (pq1, 512, 512), (pk2, 1024, 128)):
            op = V.tensor_scalar(out=vfm_s[:, off:off + nlen],
                                 in0=ps[:, 0:nlen], scalar1=bv_s[:, mc:mc + 1],
                                 scalar2=None, op0=ALU.add)
            sv.inc(op, 1)
        sv.wait(T)
        for j in range(9):
            mm = T.transpose(pt_v[:, j * 128:(j + 1) * 128],
                             vfm_s[:, j * 128:(j + 1) * 128], ident_bf[:])
        st.inc(mm, 1)
        st.wait(V)
        sv.inc(V.tensor_copy(out=v3[:, :, mc * 128:(mc + 1) * 128],
                             in_=pt_v.rearrange("p (j d) -> p j d", j=9)), 1)

    es2n = ExitStack()
    ohall_i = es2n.enter_context(nc.sbuf_tensor("ohall_i", [128, 256], F32, side="right"))
    ohall = es2n.enter_context(nc.sbuf_tensor("ohall", [128, 256], F32, side="right"))
    pnrm = es2n.enter_context(nc.psum_tensor("pnrm", [16, 512], F32))
    iop = G.iota(ohall_i[:, :], pattern=[[-1, 16], [1, 16]], base=0,
                 channel_multiplier=0, allow_small_or_imprecise_dtypes=True)
    sg.inc(iop, 1)
    sg.wait(V)
    op = V.tensor_single_scalar(out=ohall[:, :], in_=ohall_i[:, :], scalar=0.0,
                                op=ALU.is_equal)
    sv.inc(op, 1)
    for half in range(2):
        for h in range(NH):
            sv.wait(V)
            st.wait(V)
            op = V.tensor_mul(out=sqb[:, 0:512],
                              in0=q3[:, h, half * 512: half * 512 + 512],
                              in1=q3[:, h, half * 512: half * 512 + 512])
            sv.inc(op, 1)
            sv.wait(T)
            mm = T.matmul(pnrm[0:16, 0:512], ohall[:, h * 16:(h + 1) * 16],
                          sqb[:, 0:512], start=(h == 0), stop=(h == NH - 1))
            st.inc(mm, 1)
        st.wait(V)
        op = V.tensor_copy(out=norms16[:, half * 512: half * 512 + 512],
                           in_=pnrm[0:16, 0:512])
        sv.inc(op, 1)
        sv.wait(T)
    es2n.close()
    sv.wait(SY)
    sd.inc(SY.dma_start(out=cc_nrm_in.ap()[:, :], in_=norms16[:, :]), 16)
    sd.wait(G)
    cc = G.collective_compute("AllGather", ALU.bypass, replica_groups=RG,
                              ins=[cc_nrm_in.ap().opt()], outs=[cc_nrm_out.ap().opt()])
    scc.inc(cc, 1)
    es.close()
    esLN.close()

    es = ExitStack()
    e_loc = es.enter_context(nc.sbuf_tensor("e_loc", [128, 2 * 1024], BF16, side="right"))
    pr_loc = es.enter_context(nc.sbuf_tensor("pr_loc", [128, 2 * 1024], BF16, side="right"))
    pT_loc = es.enter_context(nc.sbuf_tensor("pT_loc", [128, 2 * 1024], BF16, side="right"))
    m4 = es.enter_context(nc.sbuf_tensor("m4", [128, 8], F32, side="right"))
    nm4 = es.enter_context(nc.sbuf_tensor("nm4", [128, 8], F32, side="right"))
    l4 = es.enter_context(nc.sbuf_tensor("l4", [128, 8], F32, side="right"))
    r4 = es.enter_context(nc.sbuf_tensor("r4", [128, 8], F32, side="right"))
    psc0 = es.enter_context(nc.psum_tensor("psc0", [128, 1024], F32))
    psc1 = es.enter_context(nc.psum_tensor("psc1", [128, 1024], F32))
    ppt0 = es.enter_context(nc.psum_tensor("ppt0", [128, 1024], BF16))
    ppt1 = es.enter_context(nc.psum_tensor("ppt1", [128, 1024], BF16))
    ppv0 = es.enter_context(nc.psum_tensor("ppv0", [128, 512], F32))
    ppv1 = es.enter_context(nc.psum_tensor("ppv1", [128, 512], F32))

    evac_cnt = {}
    pv_cnt = {}
    exp_cnt = {}
    tr_cnt = {}
    ptb_cnt = {}
    recip_cnt = {}
    for b in range(NB):
        for g in range(4):
            par = (b * 4 + g) % 2
            psc = psc0 if par == 0 else psc1
            ppt = ppt0 if par == 0 else ppt1
            ppv = ppv0 if par == 0 else ppv1
            eb = e_loc[:, par * 1024:(par + 1) * 1024]
            ptb = pT_loc[:, par * 1024:(par + 1) * 1024]
            l4b = l4[:, par * 4:(par + 1) * 4]
            r4b = r4[:, par * 4:(par + 1) * 4]
            heads = [4 * g + i for i in range(4)]
            if par in exp_cnt:
                ss.wait(T, exp_cnt[par])
            for i, h in enumerate(heads):
                mm = T.matmul(psc[:, i * 256:(i + 1) * 256],
                              q3[:, h, b * 128:(b + 1) * 128],
                              k3[:, h, b * 128: b * 128 + 256],
                              start=True, stop=True)
            qk_cnt = st.inc(mm, 1)
            st.wait(S, qk_cnt)
            edge = (b == 0) or (b == NB - 1)
            if par in tr_cnt:
                st.wait(S, tr_cnt[par])
            if par in recip_cnt:
                sv.wait(S, recip_cnt[par])
            for i, h in enumerate(heads):
                act = S.activation(out=eb[:, i * 256:(i + 1) * 256],
                                   in_=psc[:, i * 256:(i + 1) * 256],
                                   func=AF.Exp, bias=0.0,
                                   scale=c_invsqd[:, 0:1],
                                   accum_out=(None if edge else l4b[:, i:i + 1]))
            exp_cnt[par] = ss.inc(act, 1)
            ss.wait(V, exp_cnt[par])
            if b == 0:
                eb4 = eb.rearrange("p (i k) -> p i k", i=4)
                op = V.tensor_scalar(out=eb4[:, :, 0:64], in0=eb4[:, :, 0:64],
                                     scalar1=maskl_s[:, 0:1], scalar2=None, op0=ALU.mult)
                sv.inc(op, 1)
                sv.wait(V)
            if b == NB - 1:
                eb4 = eb.rearrange("p (i k) -> p i k", i=4)
                op = V.tensor_scalar(out=eb4[:, :, 192:256], in0=eb4[:, :, 192:256],
                                     scalar1=maskr_s[:, 0:1], scalar2=None, op0=ALU.mult)
                sv.inc(op, 1)
                sv.wait(V)
            if edge:
                op = V.reduce_sum(out=l4b, in_=eb.rearrange("p (i k) -> p i k", i=4),
                                  axis=AX.X)
                sv.inc(op, 1)
                sv.wait(V)
            op = V.reciprocal(out=r4b, in_=l4b)
            recip_cnt[par] = sv.inc(op, 1)
            ss.wait(T, exp_cnt[par])
            if edge:
                sv.wait(T, recip_cnt[par])
            if par in ptb_cnt:
                ss.wait(T, ptb_cnt[par])
            for i in range(4):
                for c in range(2):
                    mm = T.transpose(ppt[:, (i * 2 + c) * 128:(i * 2 + c + 1) * 128],
                                     eb[:, i * 256 + c * 128: i * 256 + c * 128 + 128],
                                     ident_bf[:])
            tr_cnt[par] = st.inc(mm, 1)
            st.wait(S, tr_cnt[par])
            if par in pv_cnt:
                st.wait(S, pv_cnt[par])
            cop = S.copy(out=ptb, in_=ppt[:, :])
            ptb_cnt[par] = ss.inc(cop, 1)
            ss.wait(T, ptb_cnt[par])
            if par in evac_cnt:
                sv.wait(T, evac_cnt[par])
            for i, h in enumerate(heads):
                for c in range(2):
                    mm = T.matmul(ppv[:, i * 128:(i + 1) * 128],
                                  ptb[:, (i * 2 + c) * 128:(i * 2 + c + 1) * 128],
                                  v3[:, b + c, h * 128:(h + 1) * 128],
                                  start=(c == 0), stop=(c == 1))
            pv_cnt[par] = st.inc(mm, 1)
            st.wait(V, pv_cnt[par])
            ctxg = ctx.ap()[:, b * D + g * 512: b * D + (g + 1) * 512]
            op = V.tensor_tensor(out=ctxg.rearrange("p (i d) -> p i d", i=4),
                                 in0=ppv.rearrange("p (i d) -> p i d", i=4),
                                 in1=r4b.unsqueeze(2).broadcast_to([128, 4, 128]),
                                 op=ALU.mult)
            evac_cnt[par] = sv.inc(op, 1)
    es.close()

    esP = ExitStack()

    def sbP(name, shape, dt):
        return esP.enter_context(nc.sbuf_tensor(name, shape, dt))

    gidxn = sbP("gidxn", [128, TPC], F32)
    t_pb = sbP("t_pb", [128, 512], F32)
    iota_t = sbP("iota_t", [64, TPC], F32)
    bc16i = sbP("bc16i", [16, 128], F32)
    bc16 = sbP("bc16", [16, 128], F32)
    iota16 = sbP("iota16", [16, 1], F32)
    pm16 = esP.enter_context(nc.psum_tensor("pm16", [128, 1], F32))
    normsG = sbP("normsG", [128, TPC], F32)
    normsW = sbP("normsW", [128, TPC], F32)
    maskT = sbP("maskT", [128, TPC], U8)
    candA = sbP("candA", [128, 64], F32)
    cand2 = sbP("cand2", [16, 512], F32)
    sort2 = sbP("sort2", [16, 64], F32)
    thr = sbP("thr", [128, 1], F32)
    jrow = sbP("jrow", [128, 5], F32)
    i32a = sbP("i32a", [128, 1], I32)
    idx_g = sbP("idx_g", [16, 64], F32)
    idx_l = sbP("idx_l", [16, 64], F32)
    idxT = sbP("idxT", [64, 16], F32)
    idxflat = sbP("idxflat", [1, NH * 64], F32)
    idx_bc = sbP("idx_bc", [128, NH * 64], F32)
    qg_sb = sbP("qg_sb", [128, NH * 64], BF16)
    q_tmh = sbP("q_tmh", [128, TPC], BF16)
    oh_g = sbP("oh_g", [128, 512], BF16)
    m_all = sbP("m_all", [64, NH], F32)
    nmG = sbP("nmG", [64, NH], F32)
    rloc = sbP("rloc", [64, NH], F32)
    e_g = sbP("e_g", [64, TW], BF16)
    eT_g = sbP("eT_g", [128, 9 * 64], BF16)
    ol = sbP("ol", [64, NH * HD + NH], F32)
    gctx = sbP("gctx", [64, NH * HD], BF16)
    oh_s = sbP("oh_s", [64, 128], BF16)
    cnt_sb = sbP("cnt_sb", [128, NH], F32)
    zq = sbP("zq", [128, NH], F32)
    rq = sbP("rq", [128, NH], F32)
    t1_sb = sbP("t1_sb", [128, D], BF16)

    G.iota(gidxn[:, :], pattern=[[-1, TPC]], base=0, channel_multiplier=0,
           allow_small_or_imprecise_dtypes=True)
    G.iota(t_pb[:, :], pattern=[[128, 8], [0, 64]], base=0, channel_multiplier=1,
           allow_small_or_imprecise_dtypes=True)
    G.iota(iota_t[:, :], pattern=[[1, TPC]], base=0, channel_multiplier=0,
           allow_small_or_imprecise_dtypes=True)
    G.iota(bc16i[:, :], pattern=[[-1, 16], [0, 8]], base=0, channel_multiplier=1,
           allow_small_or_imprecise_dtypes=True)
    G.iota(iota16[:, :], pattern=[[0, 1]], base=0, channel_multiplier=1,
           allow_small_or_imprecise_dtypes=True)
    iop = G.iota(jrow[:, 0:1], pattern=[[0, 1]], base=0, channel_multiplier=1,
                 allow_small_or_imprecise_dtypes=True)
    sg.inc(iop, 1)
    sg.wait(V)
    sg.wait(T)
    sv.inc(V.tensor_single_scalar(out=bc16[:, :], in_=bc16i[:, :], scalar=0.0,
                                  op=ALU.is_equal), 1)
    sv.wait(T)
    mm = T.matmul(pm16[:, 0:1], bc16[:, :], iota16[:, :], start=True, stop=True)
    st.inc(mm, 1)
    st.wait(V)
    sv.inc(V.tensor_copy(out=jrow[:, 2:3], in_=pm16[:, 0:1]), 1)
    sv.wait(V)
    op = V.tensor_scalar(out=jrow[:, 3:4], in0=jrow[:, 2:3], scalar1=8192.0,
                         scalar2=None, op0=ALU.mult)
    sv.inc(op, 1)
    sv.wait(V)
    op = V.scalar_tensor_tensor(out=jrow[:, 4:5], in0=jrow[:, 0:1], scalar=-1024.0,
                                in1=jrow[:, 3:4], op0=ALU.mult, op1=ALU.add)
    sv.inc(op, 1)
    sv.wait(V)
    op = V.tensor_scalar(out=gidxn[:, :], in0=gidxn[:, :], scalar1=jrow[:, 4:5],
                         scalar2=None, op0=ALU.add)
    sv.inc(op, 1)

    scc.wait(G)
    sq.inc(G.dma_start(out=scr_n.ap().rearrange("h (j t) -> j h t", j=8),
                       in_=cc_nrm_out.ap().rearrange("(j h) t -> j h t", h=16)), 16)
    sq.wait(G)
    sq.inc(G.dma_start(out=normsG[:, :],
                       in_=scr_n.ap().rearrange("h (j t) -> (h j) t", j=8)), 16)
    sq.wait(V)
    sv.inc(V.tensor_copy(out=normsW[:, :], in_=normsG[:, :]), 1)
    sv.wait(V)

    def extract64(data, out_tile):
        for r in range(8):
            op = V.max(out=out_tile[:, r * 8:(r + 1) * 8], in_=data)
            sv.inc(op, 1)
            sv.wait(V)
            if r < 7:
                op = V.match_replace(out=data, in_to_replace=out_tile[:, r * 8:(r + 1) * 8],
                                     in_values=data, imm_value=-BIG)
                sv.inc(op, 1)
                sv.wait(V)

    def cand_regroup():
        sv.wait(G)
        sq.inc(G.dma_start(out=scr_c.ap()[:, :], in_=candA[:, :]), 16)
        sq.wait(G)
        sq.inc(G.dma_start(out=cand2[:, :],
                           in_=scr_c.ap().rearrange("(h j) c -> h (j c)", h=16)), 16)
        sq.wait(V)

    extract64(normsW[:, :], candA)
    cand_regroup()
    extract64(cand2[:, :], sort2)
    sv.wait(T)
    mm = T.matmul(pm16[:, 0:1], bc16[:, :], sort2[:, 61:62], start=True, stop=True)
    st.inc(mm, 1)
    st.wait(V)
    sv.inc(V.tensor_copy(out=thr[:, :], in_=pm16[:, 0:1]), 1)
    sv.wait(V)
    op = V.tensor_scalar(out=maskT[:, :], in0=normsG[:, :], scalar1=thr[:, 0:1],
                         scalar2=None, op0=ALU.is_ge)
    sv.inc(op, 1)
    sv.inc(V.memset(normsW[:, :], -BIG), 1)
    sv.wait(V)
    sv.inc(V.copy_predicated(out=normsW[:, :], mask=maskT[:, :], data=gidxn[:, :]), 1)
    sv.wait(V)
    extract64(normsW[:, :], candA)
    cand_regroup()
    extract64(cand2[:, :], sort2)
    sv.wait(V)
    sv.inc(V.memset(idx_g[:, 0:1], 0.0), 1)
    sv.inc(V.memset(idx_g[:, 63:64], 8191.0), 1)
    op = V.tensor_scalar(out=idx_g[:, 1:63], in0=sort2[:, 0:62], scalar1=-1.0,
                         scalar2=None, op0=ALU.mult)
    sv.inc(op, 1)
    sv.wait(V)
    op = V.tensor_scalar(out=idx_l[:, :], in0=idx_g[:, :], scalar1=base16_s[:, 0:1],
                         scalar2=None, op0=ALU.subtract)
    sv.inc(op, 1)
    sv.wait(T)
    sv.wait(G)

    es2 = ExitStack()
    ptf = es2.enter_context(nc.psum_tensor("ptf", [128, 512], F32))
    mm = T.transpose(ptf[0:64, 0:16], idx_l[:, :], ident_f32[0:16, 0:16])
    st.inc(mm, 1)
    st.wait(V)
    sv.inc(V.tensor_copy(out=idxT[:, :], in_=ptf[0:64, 0:16]), 1)
    sq.inc(G.dma_start(out=scr_i.ap().rearrange("o (h j) -> (o h) j", h=16),
                       in_=idx_l[:, :]), 16)
    sq.wait(G)
    sq.inc(G.dma_start(out=idxflat[0:1, :], in_=scr_i.ap()), 16)
    sq.wait(T)
    sv.wait(T)
    for half in range(2):
        mm = T.matmul(ptf[:, 0:512], onesr_f32[0:1, :],
                      idxflat[0:1, half * 512: half * 512 + 512], start=True, stop=True)
        st.inc(mm, 1)
        st.wait(V)
        op = V.tensor_copy(out=idx_bc[:, half * 512: half * 512 + 512], in_=ptf[:, 0:512])
        sv.inc(op, 1)
        sv.wait(T)
    es2.close()

    es2 = ExitStack()
    pqt = es2.enter_context(nc.psum_tensor("pqt", [128, 1024], BF16))
    pqg0 = es2.enter_context(nc.psum_tensor("pqg0", [128, 512], F32))
    pqg1 = es2.enter_context(nc.psum_tensor("pqg1", [128, 512], F32))
    for h in range(NH):
        sv.wait(T)
        for j in range(8):
            mm = T.transpose(pqt[:, j * 128:(j + 1) * 128],
                             q3[:, h, j * 128:(j + 1) * 128], ident_bf[:])
        st.inc(mm, 1)
        st.wait(V)
        sv.inc(V.tensor_copy(out=q_tmh[:, :], in_=pqt[:, :]), 1)
        in0 = idx_bc.ap()[:, h * 64:(h + 1) * 64].unsqueeze(1).broadcast_to([128, 8, 64])
        in1 = t_pb.rearrange("p (b j) -> p b j", b=8)
        op = V.tensor_tensor(out=oh_g.rearrange("p (b j) -> p b j", b=8),
                             in0=in0, in1=in1, op=ALU.is_equal)
        ohcnt = sv.inc(op, 1)
        sv.wait(T, ohcnt)
        pq = pqg0 if h < 8 else pqg1
        off = (h % 8) * 64
        for j in range(8):
            mm = T.matmul(pq[:, off:off + 64], q_tmh[:, j * 128:(j + 1) * 128],
                          oh_g[:, j * 64:(j + 1) * 64],
                          start=(j == 0), stop=(j == 7))
        st.inc(mm, 1)
    st.wait(V)
    sv.inc(V.tensor_copy(out=qg_sb[:, 0:512], in_=pqg0[:, :]), 1)
    sv.inc(V.tensor_copy(out=qg_sb[:, 512:1024], in_=pqg1[:, :]), 1)
    es2.close()
    sv.wait(SY)
    sd.inc(SY.dma_start(out=cc_qg_in.ap()[:, :], in_=qg_sb[:, :]), 16)
    sd.wait(G)
    cc = G.collective_compute("AllReduce", ALU.add, replica_groups=RG,
                              ins=[cc_qg_in.ap().opt()], outs=[cc_qg_out.ap().opt()])
    scc.inc(cc, 1)
    scc.wait(G)
    sq.inc(G.dma_start(out=qg_sb[:, :], in_=cc_qg_out.ap()[:, :]), 16)

    es2 = ExitStack()
    psg = es2.enter_context(nc.psum_tensor("psg", [64, TW], F32))
    pet = es2.enter_context(nc.psum_tensor("pet", [128, 9 * 64], BF16))
    pop = es2.enter_context(nc.psum_tensor("pop", [64, 128], F32))
    qgv = qg_sb.rearrange("p (h j) -> p h j", h=NH)
    sq.wait(T)

    def g_scores(h):
        for (off, nlen) in ((0, 512), (512, 512), (1024, 128)):
            mm = T.matmul(psg[:, off:off + nlen], qgv[:, h, :], k3[:, h, off:off + nlen],
                          start=True, stop=True)
        return mm

    sv.wait(S)
    sv.wait(T)
    tr_g = None
    for h in range(NH):
        if h > 0:
            ss.wait(T)
        mm = g_scores(h)
        st.inc(mm, 1)
        st.wait(S)
        if tr_g is not None:
            st.wait(S, tr_g)
        act = S.activation(out=e_g[:, :], in_=psg[:, :], func=AF.Exp,
                           bias=0.0, scale=c_invsqd[0:64, 0:1])
        ecnt = ss.inc(act, 1)
        ss.wait(V, ecnt)
        sv.inc(V.memset(e_g[:, 0:HALO], 0.0), 1)
        sv.inc(V.memset(e_g[:, HALO + TPC:TW], 0.0), 1)
        sv.wait(V)
        op = V.reduce_sum(out=ol[:, NH * HD + h: NH * HD + h + 1],
                          in_=e_g[:, HALO:HALO + TPC], axis=AX.X)
        sv.inc(op, 1)
        sv.wait(T)
        for j in range(9):
            mm = T.transpose(pet[:, j * 64:(j + 1) * 64],
                             e_g[:, j * 128:(j + 1) * 128], ident_bf[0:64, 0:64])
        tr_g = st.inc(mm, 1)
        st.wait(V, tr_g)
        sv.inc(V.tensor_copy(out=eT_g[:, :], in_=pet[:, :]), 1)
        sv.wait(T)
        for j in range(9):
            mm = T.matmul(pop[:, 0:128], eT_g[:, j * 64:(j + 1) * 64],
                          v3[:, j, h * 128:(h + 1) * 128],
                          start=(j == 0), stop=(j == 8))
        st.inc(mm, 1)
        st.wait(V)
        sv.inc(V.tensor_copy(out=ol[:, h * HD:(h + 1) * HD], in_=pop[:, 0:128]), 1)
    es2.close()
    sv.wait(SY)
    sd.inc(SY.dma_start(out=cc_ol_in.ap()[:, :], in_=ol[:, :]), 16)
    sd.wait(G)
    cc = G.collective_compute("AllReduce", ALU.add, replica_groups=RG,
                              ins=[cc_ol_in.ap().opt()], outs=[cc_ol_out.ap().opt()])
    scc.inc(cc, 1)
    scc.wait(G)
    sq.inc(G.dma_start(out=ol[:, :], in_=cc_ol_out.ap()[:, :]), 16)
    sq.wait(V)
    for h in range(NH):
        sv.inc(V.reciprocal(out=rloc[:, h:h + 1], in_=ol[:, NH * HD + h: NH * HD + h + 1]), 1)
    sv.wait(V)
    op = V.tensor_tensor(out=gctx.rearrange("p (h d) -> p h d", h=NH),
                         in0=ol.ap()[:, 0:NH * HD].rearrange("p (h d) -> p h d", h=NH),
                         in1=rloc.ap().unsqueeze(2).broadcast_to([64, NH, HD]),
                         op=ALU.mult)
    sv.inc(op, 1)

    es2 = ExitStack()
    pscat = es2.enter_context(nc.psum_tensor("pscat", [128, 2048], F32))
    pcnt = es2.enter_context(nc.psum_tensor("pcnt", [128, 16], F32))
    for b in range(NB):
        sv.wait(T)
        for h in range(NH):
            op = V.tensor_scalar(out=oh_s[:, 0:128],
                                 in0=iota_t[:, b * 128:(b + 1) * 128],
                                 scalar1=idxT[:, h:h + 1], scalar2=None, op0=ALU.is_equal)
            ohcnt = sv.inc(op, 1)
            sv.wait(T, ohcnt)
            mm = T.matmul(pscat[:, h * 128:(h + 1) * 128], oh_s[:, 0:128],
                          gctx[:, h * HD:(h + 1) * HD],
                          start=True, stop=True)
            mm2 = T.matmul(pcnt[:, h:h + 1], oh_s[:, 0:128], ones_bf[0:64, 0:1],
                           start=True, stop=True)
            st.inc(mm2, 1)
            st.wait(V)
        op = V.tensor_copy(out=cnt_sb[:, :], in_=pcnt[:, :])
        sv.inc(op, 1)
        sv.wait(V)
        sv.inc(V.tensor_single_scalar(out=zq[:, :], in_=cnt_sb[:, :], scalar=0.0,
                                      op=ALU.is_equal), 1)
        sv.inc(V.tensor_scalar_max(out=rq[:, :], in0=cnt_sb[:, :], scalar1=1.0), 1)
        sv.wait(V)
        sv.inc(V.reciprocal(out=rq[:, :], in_=rq[:, :]), 1)
        sv.wait(V)
        op = V.tensor_tensor(out=t1_sb.rearrange("p (h d) -> p h d", h=NH),
                             in0=pscat.rearrange("p (h d) -> p h d", h=NH),
                             in1=rq.ap().unsqueeze(2).broadcast_to([128, NH, HD]),
                             op=ALU.mult)
        sv.inc(op, 1)
        ctxb = ctx.ap()[:, b * D:(b + 1) * D]
        op = V.tensor_tensor(out=ctxb.rearrange("p (h d) -> p h d", h=NH),
                             in0=ctxb.rearrange("p (h d) -> p h d", h=NH),
                             in1=zq.ap().unsqueeze(2).broadcast_to([128, NH, HD]),
                             op=ALU.mult)
        sv.inc(op, 1)
        sv.wait(V)
        op = V.tensor_tensor(out=ctxb, in0=ctxb, in1=t1_sb[:, :], op=ALU.add)
        sv.inc(op, 1)
    es2.close()
    esP.close()
    esR2.close()

    esL2 = ExitStack()
    lnx2 = esL2.enter_context(nc.sbuf_tensor("lnx2", [128, KC * TPC], BF16))
    lnx23 = lnx2.rearrange("p (c t) -> p c t", c=KC)
    es = ExitStack()
    xst3 = es.enter_context(nc.sbuf_tensor("xst3", [128, 2 * D], F32, side="right"))
    lntm2 = es.enter_context(nc.sbuf_tensor("lntm2", [128, D], BF16, side="right"))
    sqd2 = es.enter_context(nc.sbuf_tensor("sqd2", [128, D], F32, side="right"))
    pt_ln2 = es.enter_context(nc.psum_tensor("pt_ln2", [128, 2048], BF16))
    for i in range(NB):
        slot = i % 2
        xs = xst3[:, slot * D:(slot + 1) * D]
        sv.wait(SY)
        so_sl[slot].wait(SY)
        dcnt = sd.inc(SY.dma_start(out=xs, in_=x_main[i * 128:(i + 1) * 128, :]), 16)
        sd.wait(V, dcnt)
        sv.wait(V)
        op = V.tensor_tensor(out=xs, in0=xs, in1=ctx[:, i * D:(i + 1) * D], op=ALU.add)
        acnt = sv.inc(op, 1)
        sv.wait(SY, acnt)
        so_sl[slot].inc(SY.dma_start(out=xres_dram[i * 128:(i + 1) * 128, :], in_=xs), 16)
        V.reduce_sum(out=ln_s[:, i:i + 1], in_=xs, axis=AX.X)
        op = V.scalar_tensor_tensor(out=sqd2[:, :], in0=xs, scalar=1.0, in1=xs,
                                    op0=ALU.mult, op1=ALU.mult, accum_out=ln_q[:, i:i + 1])
        sv.inc(op, 1)
        sv.wait(V)
        op = V.tensor_scalar(out=ln_m[:, i:i + 1], in0=ln_s[:, i:i + 1],
                             scalar1=c_nmean[:, 0:1], scalar2=None, op0=ALU.mult)
        sv.inc(op, 1)
        sv.wait(V)
        op = V.tensor_mul(out=ln_sd[:, i:i + 1], in0=ln_m[:, i:i + 1], in1=ln_m[:, i:i + 1])
        sv.inc(op, 1)
        sv.wait(V)
        op = V.scalar_tensor_tensor(out=ln_v[:, i:i + 1], in0=ln_q[:, i:i + 1],
                                    scalar=1.0 / float(D), in1=ln_sd[:, i:i + 1],
                                    op0=ALU.mult, op1=ALU.subtract)
        sv.inc(op, 1)
        sv.wait(S)
        op = S.activation(out=ln_sd[:, i:i + 1], in_=ln_v[:, i:i + 1], func=AF.Sqrt,
                          bias=c_eps[:, 0:1], scale=1.0)
        ss.inc(op, 1)
        ss.wait(V)
        op = V.reciprocal(out=ln_r[:, i:i + 1], in_=ln_sd[:, i:i + 1])
        sv.inc(op, 1)
        sv.wait(V)
        op = V.tensor_scalar(out=lntm2[:, :], in0=xs, scalar1=ln_m[:, i:i + 1],
                             scalar2=ln_r[:, i:i + 1], op0=ALU.add, op1=ALU.mult)
        sv.inc(op, 1)
        sv.wait(T)
        for kc in range(KC):
            mm = T.transpose(pt_ln2[:, kc * 128:(kc + 1) * 128],
                             lntm2[:, kc * 128:(kc + 1) * 128], ident_bf[:])
        st.inc(mm, 1)
        st.wait(V)
        op = V.tensor_copy(out=lnx23[:, :, i * 128:(i + 1) * 128],
                           in_=pt_ln2.rearrange("p (c t) -> p c t", c=KC))
        sv.inc(op, 1)
    es.close()
    esR1.close()

    es = ExitStack()
    hbuf = es.enter_context(nc.sbuf_tensor("hbuf", [128, ICN * 512], BF16, side="right"))
    gu_u8s = es.enter_context(nc.sbuf_tensor("gu_u8s", [128, 4 * D], U8, side="right"))
    gu_bf = es.enter_context(nc.sbuf_tensor("gu_bf", [128, 4 * D], BF16, side="right"))
    wd_u8s = es.enter_context(nc.sbuf_tensor("wd_u8s", [128, 4 * 512], U8, side="right"))
    wd_bf = es.enter_context(nc.sbuf_tensor("wd_bf", [128, 4 * 512], BF16, side="right"))
    silu_s = es.enter_context(nc.sbuf_tensor("silu_s", [128, 2 * 512], F32, side="right"))
    gsc_s = es.enter_context(nc.sbuf_tensor("gsc_s", [128, ICN], F32, side="right"))
    usc_s = es.enter_context(nc.sbuf_tensor("usc_s", [128, ICN], F32, side="right"))
    dscb_s = es.enter_context(nc.sbuf_tensor("dscb_s", [128, D], F32, side="right"))
    c_nzp = es.enter_context(nc.sbuf_tensor("c_nzp", [128, 1], F32, side="right"))
    xr_st = es.enter_context(nc.sbuf_tensor("xr_st", [128, 2 * 512], F32, side="right"))
    out_st = es.enter_context(nc.sbuf_tensor("out_st", [128, 2 * 512], F32, side="right"))
    pgs = [es.enter_context(nc.psum_tensor(f"pg{i}", [128, 512], F32)) for i in range(2)]
    pus = [es.enter_context(nc.psum_tensor(f"pu{i}", [128, 512], F32)) for i in range(2)]
    pds = [es.enter_context(nc.psum_tensor(f"pd{i}", [128, 512], F32)) for i in range(4)]

    sv.wait(SY)
    st.wait(SY)
    so_sl[0].wait(SY)
    so_sl[1].wait(SY)
    sd.inc(SY.dma_start(out=gsc_s[:], in_=gsc[:, :]), 16)
    sd.inc(SY.dma_start(out=usc_s[:], in_=usc[:, :]), 16)
    dcnt = sd.inc(SY.dma_start(out=dscb_s[:], in_=dscb[:, :]), 16)
    sv.wait(V)
    sv.inc(V.memset(c_nzp[:], -128.0), 1)
    sd.wait(S, dcnt)
    sd.wait(V, dcnt)
    sv.wait(S)

    guw_gcast = {}
    guw_ucast = {}
    guw_mm = {}
    silu_cnt = {}
    hm_cnt = {}
    wd_cast = {}
    wd_mm = {}
    for ph in range(2):
        tok0 = ph * 512
        st.wait(V)
        st.wait(S)
        for ic in range(ICN):
            slot = ic % 2
            if slot in guw_gcast:
                ss.wait(SY, guw_gcast[slot])
            if slot in guw_ucast:
                sv.wait(SY, guw_ucast[slot])
            s_w[slot].inc(SY.dma_start(out=gu_u8s[:, (2 * slot) * D:(2 * slot + 1) * D],
                                       in_=g_u8[ic, :, :]), 16)
            d2 = SY.dma_start(out=gu_u8s[:, (2 * slot + 1) * D:(2 * slot + 2) * D],
                              in_=u_u8[ic, :, :])
            s_w[slot].inc(d2, 16)
            s_w[slot].wait(S)
            s_w[slot].wait(V)
            if slot in guw_mm:
                st.wait(S, guw_mm[slot])
                st.wait(V, guw_mm[slot])
            a1 = S.activation(out=gu_bf[:, (2 * slot) * D:(2 * slot + 1) * D],
                              in_=gu_u8s[:, (2 * slot) * D:(2 * slot + 1) * D],
                              func=AF.Copy, bias=-128.0, scale=1.0)
            guw_gcast[slot] = ss.inc(a1, 1)
            o1 = V.tensor_scalar(out=gu_bf[:, (2 * slot + 1) * D:(2 * slot + 2) * D],
                                 in0=gu_u8s[:, (2 * slot + 1) * D:(2 * slot + 2) * D],
                                 scalar1=-128.0, scalar2=None, op0=ALU.add)
            guw_ucast[slot] = sv.inc(o1, 1)
            ss.wait(T, guw_gcast[slot])
            sv.wait(T, guw_ucast[slot])
            if slot in silu_cnt:
                ss.wait(T, silu_cnt[slot])
            if slot in hm_cnt:
                sv.wait(T, hm_cnt[slot])
            gb = gu_bf.rearrange("p (s c f) -> p s c f", s=4, c=KC)
            for kc in range(KC):
                mm = T.matmul(pgs[slot][:, :], gb[:, 2 * slot, kc, :],
                              lnx23[:, kc, tok0:tok0 + 512],
                              start=(kc == 0), stop=(kc == KC - 1))
            for kc in range(KC):
                mm = T.matmul(pus[slot][:, :], gb[:, 2 * slot + 1, kc, :],
                              lnx23[:, kc, tok0:tok0 + 512],
                              start=(kc == 0), stop=(kc == KC - 1))
            guw_mm[slot] = st.inc(mm, 1)
            st.wait(S, guw_mm[slot])
            if slot in hm_cnt:
                sv.wait(S, hm_cnt[slot])
            act = S.activation(out=silu_s[:, slot * 512:(slot + 1) * 512],
                               in_=pgs[slot][:, :], func=AF.Silu,
                               bias=0.0, scale=gsc_s[:, ic:ic + 1])
            silu_cnt[slot] = ss.inc(act, 1)
            ss.wait(V, silu_cnt[slot])
            op = V.scalar_tensor_tensor(out=hbuf[:, ic * 512:(ic + 1) * 512],
                                        in0=pus[slot][:, :], scalar=usc_s[:, ic:ic + 1],
                                        in1=silu_s[:, slot * 512:(slot + 1) * 512],
                                        op0=ALU.mult, op1=ALU.mult)
            hm_cnt[slot] = sv.inc(op, 1)
        for dc in range(4):
            for ic in range(ICN):
                slot = ic % 4
                if slot in wd_cast:
                    ss.wait(SY, wd_cast[slot])
                d1 = SY.dma_start(out=wd_u8s[:, slot * 512:(slot + 1) * 512],
                                  in_=d_u8[ic, dc, :, :])
                s_d[slot].inc(d1, 16)
                s_d[slot].wait(S)
                if slot in wd_mm:
                    st.wait(S, wd_mm[slot])
                a1 = S.activation(out=wd_bf[:, slot * 512:(slot + 1) * 512],
                                  in_=wd_u8s[:, slot * 512:(slot + 1) * 512],
                                  func=AF.Copy, bias=-128.0, scale=1.0)
                wd_cast[slot] = ss.inc(a1, 1)
                ss.wait(T, wd_cast[slot])
                if ic == 0:
                    sv.wait(T)
                for tc in range(4):
                    mm = T.matmul(pds[tc][:, :],
                                  hbuf[:, ic * 512 + tc * 128: ic * 512 + tc * 128 + 128],
                                  wd_bf[:, slot * 512:(slot + 1) * 512],
                                  start=(ic == 0), stop=(ic == ICN - 1))
                wd_mm[slot] = st.inc(mm, 1)
            st.wait(V)
            for tc in range(4):
                gtc = ph * 4 + tc
                slot = tc % 2
                sv.wait(SY)
                dcnt = sd.inc(SY.dma_start(out=xr_st[:, slot * 512:(slot + 1) * 512],
                                           in_=xres_dram[gtc * 128:(gtc + 1) * 128,
                                                         dc * 512:(dc + 1) * 512]), 16)
                sd.wait(V, dcnt)
                sv.wait(V)
                so_sl[slot].wait(V)
                op = V.tensor_tensor(out=out_st[:, slot * 512:(slot + 1) * 512],
                                     in0=pds[tc][:, :], in1=dscb_s[:, dc * 512:(dc + 1) * 512],
                                     op=ALU.mult)
                sv.inc(op, 1)
                sv.wait(V)
                op = V.tensor_tensor(out=out_st[:, slot * 512:(slot + 1) * 512],
                                     in0=out_st[:, slot * 512:(slot + 1) * 512],
                                     in1=xr_st[:, slot * 512:(slot + 1) * 512],
                                     op=ALU.add)
                ocnt = sv.inc(op, 1)
                sv.wait(SY, ocnt)
                d = SY.dma_start(out=y[gtc * 128:(gtc + 1) * 128, dc * 512:(dc + 1) * 512],
                                 in_=out_st[:, slot * 512:(slot + 1) * 512])
                so_sl[slot].inc(d, 16)
    es.close()
    esL2.close()
    esK.close()

    sd.wait(SY)
    so_sl[0].wait(SY)
    so_sl[1].wait(SY)
    nc.all_engine_barrier()
    return nc


def _prep_inputs(inputs):
    x = np.asarray(inputs["L0_attn_hidden_states"]).reshape(SEQ, D)
    Wq = np.asarray(inputs["L0_attn_Wq"])
    Wk = np.asarray(inputs["L0_attn_Wk"])
    Wv = np.asarray(inputs["L0_attn_Wv"])
    bqv = np.asarray(inputs["L0_attn_bq"])
    bkv = np.asarray(inputs["L0_attn_bk"])
    bvv = np.asarray(inputs["L0_attn_bv"])
    gw = np.asarray(inputs["L0_ffn_gate_w"])
    uw = np.asarray(inputs["L0_ffn_up_w"])
    dw = np.asarray(inputs["L0_ffn_down_w"])
    gs = np.asarray(inputs["L0_ffn_gate_scale"])
    us = np.asarray(inputs["L0_ffn_up_scale"])
    ds = np.asarray(inputs["L0_ffn_down_scale"])

    def tile_w(W):
        t = W.reshape(KC, 128, KC, 128).transpose(2, 0, 1, 3)
        t = t.reshape(KC, 2, 8, 128, 128).transpose(0, 1, 3, 2, 4)
        return np.ascontiguousarray(t.reshape(KC, 2, 128, 1024))

    def u8_of(Wi32):
        return Wi32.view(np.uint8).reshape(*Wi32.shape, 4)[..., 0]

    g8 = u8_of(gw)
    u8a = u8_of(uw)
    d8 = u8_of(dw)
    g_t = g8.T.reshape(KC, 128, ICN, 128).transpose(2, 0, 1, 3)
    g_t = np.ascontiguousarray(g_t.transpose(0, 2, 1, 3).reshape(ICN, 128, D))
    u_t = u8a.T.reshape(KC, 128, ICN, 128).transpose(2, 0, 1, 3)
    u_t = np.ascontiguousarray(u_t.transpose(0, 2, 1, 3).reshape(ICN, 128, D))
    d_t = np.ascontiguousarray(d8.T.reshape(ICN, 128, 4, 512).transpose(0, 2, 1, 3))

    bq_s = np.ascontiguousarray(bqv.reshape(KC, 128).T)
    bk_s = np.ascontiguousarray(bkv.reshape(KC, 128).T)
    bv_s = np.ascontiguousarray(bvv.reshape(KC, 128).T)
    gsc_a = np.ascontiguousarray(gs.reshape(ICN, 128).T)
    usc_a = np.ascontiguousarray(us.reshape(ICN, 128).T)
    dscb_a = np.ascontiguousarray(np.broadcast_to(ds.reshape(1, D), (128, D)))
    wq_t, wk_t, wv_t = tile_w(Wq), tile_w(Wk), tile_w(Wv)

    in_maps = []
    for c in range(NCORES):
        t0 = c * TPC
        hl = x[t0 - HALO: t0] if t0 - HALO >= 0 else x[0:HALO]
        hr = x[t0 + TPC: t0 + TPC + HALO] if t0 + TPC + HALO <= SEQ else x[SEQ - HALO:SEQ]
        in_maps.append({
            "x_main": np.ascontiguousarray(x[t0:t0 + TPC]),
            "x_hl": np.ascontiguousarray(hl),
            "x_hr": np.ascontiguousarray(hr),
            "wq": wq_t, "wk": wk_t, "wv": wv_t,
            "bq": bq_s, "bk": bk_s, "bv": bv_s,
            "g_u8": g_t, "u_u8": u_t, "d_u8": d_t,
            "gsc": gsc_a, "usc": usc_a, "dscb": dscb_a,
            "mask_l": np.full((128, 1), 0.0 if c == 0 else 1.0, np.float32),
            "mask_r": np.full((128, 1), 0.0 if c == NCORES - 1 else 1.0, np.float32),
            "base16": np.full((16, 1), float(t0), np.float32),
        })
    return in_maps


def kernel(**inputs):
    in_maps = _prep_inputs(inputs)
    nc = build_nc()
    res = run_bass_kernel_spmd(nc, in_maps, core_ids=list(range(NCORES)))
    out = np.concatenate([res.results[c]["y"] for c in range(NCORES)], axis=0)
    return out.reshape(1, SEQ, D).astype(np.float32)
